# version 60
# baseline (speedup 1.0000x reference)
import numpy as np
import ml_dtypes
import concourse.bass as bass
import concourse.mybir as mybir
from concourse.bass_utils import run_bass_kernel_spmd
from contextlib import ExitStack

F32 = mybir.dt.float32
BF16 = mybir.dt.bfloat16
F8 = mybir.dt.float8e4
I32 = mybir.dt.int32
AF = mybir.ActivationFunctionType
ALU = mybir.AluOpType

MAGIC = 12582912.0
TWO_PI = 6.283185307179586
EPS = 1e-5
D = 1024
TL = 128
NCHUNK = 8

ENGSEM = {"T": "semT", "V": "semV", "S": "semS", "P": "semP"}


class Sched:

    def __init__(self):
        self.q = {k: [] for k in ["T", "V", "S", "P", "D"]}
        self.counts = {}
        self.sem_names = set()
        self.seq = 0

    def add(self, eng, fn, deps=(), sem=None, inc=None, tag=""):
        if sem is None:
            sem = ENGSEM[eng]
        if inc is None:
            inc = 16 if eng == "D" else 1
        self.sem_names.add(sem)
        self.counts[sem] = self.counts.get(sem, 0) + inc
        ev = (sem, self.counts[sem])
        deps = [d for d in deps if d is not None]
        self.seq += 1
        self.q[eng].append((deps, fn, sem, inc, self.seq, tag))
        return ev

    def dma_counts_upto(self, cut):
        out = {}
        engsems = set(ENGSEM.values())
        for qk in self.q:
            for (deps, fn, sem, inc, seq, tag) in self.q[qk]:
                if sem in engsems:
                    continue
                if cut is None or seq <= cut:
                    out[sem] = out.get(sem, 0) + inc
        return out

    def emit(self, eng_key, engine, sems, cut=None, selfsync=""):
        floors = {}
        own = ENGSEM.get(eng_key)
        own_count = 0
        for deps, fn, sem, inc, seq, tag in self.q[eng_key]:
            if cut is not None and seq > cut:
                continue
            need = {}
            for (s, v) in deps:
                if v > need.get(s, 0):
                    need[s] = v
            ssync = (eng_key in selfsync) or (tag and tag in selfsync)
            if ssync and own is not None \
                    and own_count > 0 and own_count > need.get(own, 0):
                need[own] = own_count
            for s, v in sorted(need.items()):
                if v > floors.get(s, 0):
                    engine.wait_ge(sems[s], v)
                    floors[s] = v
            instr = fn(engine)
            instr.then_inc(sems[sem], inc)
            if sem == own:
                own_count += inc


def build(nt=32, cut=None, racecheck=False, selfsync="G"):
    nc = bass.Bass("TRN2", target_bir_lowering=False, debug=False,
                   detect_race_conditions=racecheck)
    T = nt * TL

    xtok_d = nc.dram_tensor("xtok", [T, D], F32, kind="ExternalInput")
    xft_d = nc.dram_tensor("xft", [nt * TL, D], F32, kind="ExternalInput")
    wvt_d = nc.dram_tensor("wvt", [NCHUNK * TL, D], F32, kind="ExternalInput")
    wot_d = nc.dram_tensor("wot", [NCHUNK * TL, D], F32, kind="ExternalInput")
    wff_d = nc.dram_tensor("wff", [NCHUNK * TL, D], BF16, kind="ExternalInput")
    freqs_d = nc.dram_tensor("freqs", [1, D], F32, kind="ExternalInput")
    tcols_d = nc.dram_tensor("tcols", [TL, nt], F32, kind="ExternalInput")
    us_d = nc.dram_tensor("usmat", [TL, TL], BF16, kind="ExternalInput")
    id_d = nc.dram_tensor("idmat", [TL, TL], BF16, kind="ExternalInput")
    sel_d = nc.dram_tensor("selmat", [TL, TL], BF16, kind="ExternalInput")
    sin_d = nc.dram_tensor("s_in", [1, 1], F32, kind="ExternalInput")
    sk_d = nc.dram_tensor("s_k", [1, 1], F32, kind="ExternalInput")
    bias_d = nc.dram_tensor("biasv", [1, D], BF16, kind="ExternalInput")
    out_d = nc.dram_tensor("out", [T, D], F32, kind="ExternalOutput")

    sch = Sched()
    with ExitStack() as st:
        def sb(name, shape, dt):
            return st.enter_context(nc.sbuf_tensor(name, shape, dt))

        sems = {}
        for s in ["semT", "semV", "semS", "semP", "DW", "DWS", "DWT",
                  "DWU", "DWV", "DW4", "DW5", "DW6", "DW7",
                  "DX0", "DX1", "DX2", "DX3", "DF0", "DF1", "DO0", "DO1"]:
            sems[s] = st.enter_context(nc.semaphore(s))

        wstage = sb("wstage", [TL, 1024], F32)
        tcall_s = sb("tcall", [TL, nt], F32)
        wvt_s = sb("wvt_s", [TL, NCHUNK * D], BF16)
        wot_s = sb("wot_s", [TL, NCHUNK * D], BF16)
        wffq_s = sb("wffq_s", [TL, NCHUNK * D], F8)
        f2pi = sb("f2pi", [TL, D], F32)
        frow = sb("frow", [1, D], F32)
        tcol_s = [sb(f"tc{i}", [TL, 1], F32) for i in range(nt)]
        usmat = sb("usmat_s", [TL, TL], BF16)
        idm = sb("idm", [TL, TL], BF16)
        selm = sb("selm", [TL, TL], BF16)
        onesf = sb("onesf", [1, TL], F32)
        onesb_row = sb("onesb_row", [1, TL], BF16)
        onescol_b = sb("onescol_b", [TL, 1], BF16)
        w1n_rep = sb("w1n_rep", [TL, D], BF16)
        w1row = sb("w1row", [1, D], BF16)
        bdiv = sb("bdiv", [1, D], BF16)
        rr = sb("rr", [1, 1], F32)
        brow = sb("brow", [1, D], BF16)
        sincol = sb("sincol", [TL, 1], F32)
        skcol = sb("skcol", [TL, 1], F32)
        rsin = sb("rsin", [TL, 1], F32)
        rsk = sb("rsk", [TL, 1], F32)
        sisk = sb("sisk", [TL, 1], F32)
        piby2 = sb("piby2", [TL, 1], F32)
        mhalf = sb("mhalf", [TL, 1], F32)
        srow = sb("srow", [1, 2], F32)

        xtok_s = [sb(f"xtok{i}", [TL, D], F32) for i in range(4)]
        xftf_s = [sb(f"xftf{i}", [TL, D], F32) for i in range(2)]
        xftb_s = [sb(f"xftb{i}", [TL, D], BF16) for i in range(2)]
        u_s = sb("u_s", [TL, D], F32)
        k_s = sb("k_s", [TL, D], F32)
        w_s = [sb(f"w{i}", [TL, D], F32) for i in range(2)]
        wc_s = [sb(f"wc{i}", [TL, D], F32) for i in range(2)]
        sina = [sb(f"sina{i}", [TL, D], BF16) for i in range(2)]
        cosa = [sb(f"cosa{i}", [TL, D], BF16) for i in range(2)]
        usr_s = [sb(f"usr{i}", [TL, TL], BF16) for i in range(2)]
        v_s = [sb(f"v{i}", [TL, D], BF16) for i in range(2)]
        vcvs = [sb(f"vcvs{i}", [TL, 2 * D], BF16) for i in range(2)]
        ta_s = [sb(f"ta{i}", [TL, D], BF16) for i in range(2)]
        tb_s = [sb(f"tb{i}", [TL, D], BF16) for i in range(2)]
        ret_s = [sb(f"ret{i}", [TL, D], BF16) for i in range(2)]
        retT_s = [sb(f"retT{i}", [TL, D], BF16) for i in range(2)]
        cums = [sb(f"cums{i}", [64, 2 * D], BF16) for i in range(2)]
        t2s_s = [sb("t2s0", [TL, D], F32)]
        xp_s = [sb(f"xp{i}", [TL, D], F32) for i in range(3)]
        h2_s = [sb(f"h2_{i}", [TL, D], BF16) for i in range(2)]
        h2q_s = [sb(f"h2q{i}", [TL, D], F8) for i in range(2)]
        ybf_s = [sb(f"ybf{i}", [TL, D], BF16) for i in range(2)]
        ln4_s = [sb("ln4_0", [TL, D], F32)]
        wstage2 = ln4_s[0]
        out_s = [sb(f"outs{i}", [TL, D], F32) for i in range(2)]
        bnst = [[sb(f"bnst{ln}_{i}", [TL, 12], F32) for i in range(2)]
                for ln in range(4)]
        mv = [[sb(f"mv{ln}_{i}", [TL, 2], F32) for i in range(2)]
              for ln in range(4)]
        vep = [[sb(f"vep{ln}_{i}", [TL, 1], F32) for i in range(2)]
               for ln in range(4)]
        rcol = [[sb(f"r{ln}_{i}", [TL, 1], F32) for i in range(2)]
                for ln in range(4)]
        nmr = [[sb(f"nmr{ln}_{i}", [TL, 1], F32) for i in range(2)]
               for ln in range(4)]

        pS = [st.enter_context(nc.psum_tensor("pA", [TL, D], F32)),
              st.enter_context(nc.psum_tensor("pB", [TL, D], F32)),
              st.enter_context(nc.psum_tensor("pC", [TL, D], F32))]
        pCum = st.enter_context(nc.psum_tensor("pCum", [TL, D], F32))
        pS_b = [p.ap().bitcast(BF16) for p in pS]

        A = sch.add
        pre_loads = []
        for dst, src in [(usmat, us_d), (idm, id_d),
                         (selm, sel_d), (frow, freqs_d), (brow, bias_d)]:
            pre_loads.append(A("D", lambda e, d=dst, s=src: e.dma_start(d[:, :], s[:, :]),
                               deps=(), sem="DW"))
        e_tcall = A("D", lambda e: e.dma_start(tcall_s[:, :], tcols_d[:, :]),
                    deps=(), sem="DW")
        e_sin_l = A("D", lambda e: e.dma_start(srow[0:1, 0:1], sin_d[:, :]), sem="DW")
        e_sk_l = A("D", lambda e: e.dma_start(srow[0:1, 1:2], sk_d[:, :]), sem="DW")
        pre_loads += [e_tcall, e_sin_l, e_sk_l]
        e_dw_all = [e_sk_l]

        e_ms = [
            A("V", lambda v: v.memset(onesf[:, :], 1.0)),
            A("V", lambda v: v.memset(onescol_b[:, :], 1.0)),
            A("V", lambda v: v.memset(piby2[:, :], TWO_PI / 4.0)),
            A("P", lambda g: g.memset(mhalf[:, :], -0.5)),
            A("V", lambda v: v.memset(onesb_row[:, :], 1.0)),
        ]

        E = {}
        E[("xl", 0)] = A("D", lambda e: e.dma_start(
            xtok_s[0][:, :], xtok_d[0:TL, :]), deps=(), sem="DX0")
        E[("fl", 0)] = A("D", lambda e: e.dma_start(
            xftf_s[0][:, :], xft_d[0:TL, :]), deps=(), sem="DF0")
        E[("xl", 1)] = A("D", lambda e: e.dma_start(
            xtok_s[1][:, :], xtok_d[TL:2 * TL, :]), deps=(), sem="DX1")
        E[("fl", 1)] = A("D", lambda e: e.dma_start(
            xftf_s[1][:, :], xft_d[TL:2 * TL, :]), deps=(), sem="DF1")
        E[("fcast", 0)] = A("P", lambda g: g.tensor_scalar(
            xftb_s[0][:, :], xftf_s[0][:, :], 1.0, 1.0, ALU.mult,
            ALU.mult), deps=[E[("fl", 0)]])
        E[("fcast", 1)] = A("P", lambda g: g.tensor_scalar(
            xftb_s[1][:, :], xftf_s[1][:, :], 1.0, 1.0, ALU.mult,
            ALU.mult), deps=[E[("fl", 1)]])

        e_warm = A("S", lambda s_: s_.activation(
            ybf_s[0][0:1, 0:1], piby2[0:1, 0:1], AF.Sin, scale=0.0),
            deps=[e_ms[2]])

        e_sbcast = A("T", lambda t: t.matmul(
            pS[0][:, 0:2], onesf[:, :], srow[0:1, 0:2], start=True, stop=True),
            deps=e_dw_all + [e_ms[0]])
        e_scols = A("V", lambda v: v.tensor_copy(sincol[:, :], pS[0][:, 0:1]),
                    deps=[e_sbcast])
        e_scols2 = A("V", lambda v: v.tensor_copy(skcol[:, :], pS[0][:, 1:2]),
                     deps=[e_sbcast])
        e_rsin = A("V", lambda v: v.reciprocal(rsin[:, :], sincol[:, :]),
                   deps=[e_scols])
        e_rsk = A("V", lambda v: v.reciprocal(rsk[:, :], skcol[:, :]),
                  deps=[e_scols2])
        e_sisk = A("V", lambda v: v.tensor_tensor(sisk[:, :], sincol[:, :],
                                                  skcol[:, :], ALU.mult),
                   deps=[e_scols, e_scols2])

        e_fb = [A("T", lambda t, j=j: t.matmul(
            pS[1][:, j * 512:(j + 1) * 512], onesf[:, :],
            frow[0:1, j * 512:(j + 1) * 512], start=True, stop=True),
            deps=e_dw_all + [e_ms[0], e_scols, e_scols2]) for j in range(2)]
        e_f2pi = A("V", lambda v: v.tensor_scalar(
            f2pi[:, :], pS[1][:, :], 1.0 / TWO_PI, None, ALU.mult), deps=e_fb)

        e_rr = A("V", lambda v: v.tensor_tensor(
            rr[0:1, :], rsin[0:1, 0:1], rsk[0:1, 0:1], ALU.mult),
            deps=[e_rsin, e_rsk])
        e_bdiv = A("V", lambda v: v.tensor_scalar(
            bdiv[0:1, :], brow[0:1, :], rr[0:1, 0:1], None, ALU.mult),
            deps=[e_rr] + e_dw_all)

        wstage_bf = wstage.ap().bitcast(BF16)
        wstage2_bf = wstage2.ap().bitcast(BF16)
        stg = [wstage, wstage2]
        stg_b = [wstage_bf, wstage2_bf]
        e_wv_cast = []
        e_wv0 = None
        for c in range(8):
            dep = [e_wv_cast[c - 2]] if c >= 2 else []
            qsems = (["DWS", "DWT", "DWU", "DWV"] if c % 2 == 0
                     else ["DW4", "DW5", "DW6", "DW7"])
            lds = []
            for h in range(4):
                lds.append(A("D", lambda e, c=c, h=h: e.dma_start(
                    stg[c % 2][:, h * 256:(h + 1) * 256],
                    wvt_d[c * TL:(c + 1) * TL, h * 256:(h + 1) * 256]),
                    deps=dep, sem=qsems[h]))
            e_wv_cast.append(A("V", lambda v, c=c: v.tensor_scalar(
                wvt_s[:, c * D:(c + 1) * D], stg[c % 2][:, :],
                1.0, None, ALU.mult), deps=lds))

            def w1c_mms(t, c=c):
                r = None
                for j in range(2):
                    r = t.matmul(pCum[0:1, j * 512:(j + 1) * 512],
                                 onescol_b[:, :],
                                 wvt_s[:, c * D + j * 512:c * D + (j + 1) * 512],
                                 start=(c == 0), stop=(c == NCHUNK - 1))
                return r
            e_w1mm = A("T", w1c_mms,
                       deps=[e_wv_cast[c]] + ([e_ms[1]] if c == 0 else []))

            def wv0c_mms(t, c=c):
                r = None
                for j in range(2):
                    r = t.matmul(pS[0][:, j * 512:(j + 1) * 512],
                                 xftb_s[0][:, c * TL:(c + 1) * TL],
                                 wvt_s[:, c * D + j * 512:c * D + (j + 1) * 512],
                                 start=(c == 0), stop=(c == NCHUNK - 1))
                return r
            e_wv0 = A("T", wv0c_mms,
                      deps=[e_wv_cast[c]]
                      + ([E[("fcast", 0)], e_scols, e_scols2] if c == 0 else []))
        e_wvt = e_wv_cast[7]
        E[("wv", 0)] = e_wv0

        e_w1row = A("V", lambda v: v.tensor_scalar(
            w1row[0:1, :], pCum[0:1, 0:D], -1.0, None, ALU.mult), deps=[e_w1mm])
        e_w1b = [A("T", lambda t, j=j: t.matmul(
            pCum[:, j * 512:(j + 1) * 512], onesb_row[0:1, :],
            w1row[0:1, j * 512:(j + 1) * 512], start=True, stop=True),
            deps=[e_w1row, e_ms[4]]) for j in range(2)]
        e_w1rep = A("V", lambda v: v.tensor_scalar(
            w1n_rep[:, :], pCum[:, :], 1.0, None, ALU.mult), deps=e_w1b)

        for i in range(min(4, nt)):
            E[("tc", i)] = A("V", lambda v, i=i: v.tensor_copy(
                tcol_s[i][:, :], tcall_s[:, i:i + 1]), deps=[e_tcall])
        e_wot = None
        e_wffq = None


        def sv(q):
            return q % 3

        def sa(q):
            return (q + 1) % 3

        def sy(q):
            return (q + 2) % 3

        for p in range(nt + 5):
            q0, q1, q2, q3, q4 = p, p - 1, p - 2, p - 3, p - 4

            if 2 <= p + 1 < nt:
                qq = p + 1
                E[("xl", qq)] = A("D", lambda e, qq=qq: e.dma_start(
                    xtok_s[qq % 4][:, :], xtok_d[qq * TL:(qq + 1) * TL, :]),
                    deps=[E.get(("xp", qq - 4))],
                    sem=f"DX{qq % 4}")
                E[("fl", qq)] = A("D", lambda e, qq=qq: e.dma_start(
                    xftf_s[qq % 2][:, :], xft_d[qq * TL:(qq + 1) * TL, :]),
                    deps=[E.get(("fcast", qq - 2))],
                    sem=f"DF{qq % 2}")
                E[("fcast", qq)] = A("P", lambda g, qq=qq: g.tensor_scalar(
                    xftb_s[qq % 2][:, :], xftf_s[qq % 2][:, :], 1.0, 1.0,
                    ALU.mult, ALU.mult),
                    deps=[E[("fl", qq)], E.get(("wv", qq - 2))])

            if 0 <= q4 < nt:
                b = q4 % 2
                E[("deq", q4)] = A("S", lambda s, b=b, q4=q4: s.activation(
                    ybf_s[b][:, :], pS[sy(q4)][:, :], AF.Identity,
                    scale=sisk[:, 0:1]),
                    deps=[E[("ffn", q4)], E.get(("ln4", q4 - 2)),
                          E.get(("mv4", q4 - 2)), e_sisk]
                    + ([e_wf_cast[6 + q4]] if q4 <= 1 else []))

            for qR in ([0, 1] if p == 0 else []):
                if qR >= nt:
                    continue
                b = qR % 2
                E[("bn1", qR)] = A("V", lambda v, b=b, qR=qR: v.bn_stats(
                    bnst[0][b][:, 0:6], xtok_s[qR % 4][:, 0:512]),
                    deps=[E[("xl", qR)]])
                A("V", lambda v, b=b, qR=qR: v.bn_stats(
                    bnst[0][b][:, 6:12], xtok_s[qR % 4][:, 512:1024]),
                    deps=[E[("xl", qR)]])
                E[("mv1", qR)] = A("V", lambda v, b=b: v.bn_aggr(
                    mv[0][b][:, :], bnst[0][b][:, :]),
                    deps=[E.get(("v", qR - 2)), E.get(("vep1", qR - 2))],
                    tag="G")
                E[("vep1", qR)] = A("P", lambda g, b=b: g.tensor_scalar(
                    vep[0][b][:, :], mv[0][b][:, 1:2], 1.0, EPS, ALU.mult,
                    ALU.add), deps=[E[("mv1", qR)], E.get(("pow1", qR - 2))])
                E[("pow1", qR)] = A("P", lambda g, b=b: g.tensor_tensor(
                    rcol[0][b][:, :], vep[0][b][:, :], mhalf[:, :], ALU.pow),
                    deps=[E[("vep1", qR)], E.get(("usr", qR - 2))])
            for qV in ([0, 1] if p == 0 else []):
                if qV >= nt:
                    continue
                E[("u", qV)] = A("V", lambda v, qV=qV: v.tensor_scalar(
                    u_s[:, :], f2pi[:, :], tcol_s[qV][:, 0:1], None, ALU.mult),
                    deps=([e_f2pi] if qV == 0 else []) + [E.get(("k1", qV - 1)),
                          E[("tc", qV)]])
                E[("k1", qV)] = A("P", lambda g: g.tensor_scalar(
                    k_s[:, :], u_s[:, :], 1.0, MAGIC, ALU.mult, ALU.add),
                    deps=[E[("u", qV)], E.get(("w", qV - 1))])
                E[("k", qV)] = A("P", lambda g: g.tensor_scalar(
                    k_s[:, :], k_s[:, :], 1.0, -MAGIC, ALU.mult, ALU.add),
                    deps=[E[("k1", qV)]])
                b = qV % 2
                E[("w", qV)] = A("V", lambda v, b=b: v.scalar_tensor_tensor(
                    w_s[b][:, :], k_s[:, :], -1.0, u_s[:, :], ALU.mult, ALU.add),
                    deps=[E[("k", qV)], E.get(("sin", qV - 2))])
                E[("wc", qV)] = A("V", lambda v, b=b: v.scalar_tensor_tensor(
                    wc_s[b][:, :], w_s[b][:, :], 0.25, w_s[b][:, :],
                    ALU.is_ge, ALU.subtract),
                    deps=[E.get(("cos", qV - 2))])

            for qS in ([0, 1] if p == 0 else []):
                if qS >= nt:
                    continue
                b = qS % 2
                E[("sin", qS)] = A("S", lambda s, b=b: s.activation(
                    sina[b][:, :], w_s[b][:, :], AF.Sin, scale=TWO_PI),
                    deps=[E[("w", qS)]] + [e_ms[2]])
                E[("cos", qS)] = A("S", lambda s, b=b: s.activation(
                    cosa[b][:, :], wc_s[b][:, :], AF.Sin, scale=-TWO_PI,
                    bias=piby2[:, :]),
                    deps=[E[("wc", qS)]] + [e_ms[2]])

            for qV in ([2] if p == 0 else []):
                if qV >= nt:
                    continue
                E[("u", qV)] = A("V", lambda v, qV=qV: v.tensor_scalar(
                    u_s[:, :], f2pi[:, :], tcol_s[qV][:, 0:1], None, ALU.mult),
                    deps=[E.get(("k1", qV - 1)), E[("tc", qV)]])
                E[("k1", qV)] = A("P", lambda g: g.tensor_scalar(
                    k_s[:, :], u_s[:, :], 1.0, MAGIC, ALU.mult, ALU.add),
                    deps=[E[("u", qV)], E.get(("w", qV - 1))])
                E[("k", qV)] = A("P", lambda g: g.tensor_scalar(
                    k_s[:, :], k_s[:, :], 1.0, -MAGIC, ALU.mult, ALU.add),
                    deps=[E[("k1", qV)]])
                b = qV % 2
                E[("w", qV)] = A("V", lambda v, b=b: v.scalar_tensor_tensor(
                    w_s[b][:, :], k_s[:, :], -1.0, u_s[:, :], ALU.mult, ALU.add),
                    deps=[E[("k", qV)], E.get(("sin", qV - 2))])
                E[("wc", qV)] = A("V", lambda v, b=b: v.scalar_tensor_tensor(
                    wc_s[b][:, :], w_s[b][:, :], 0.25, w_s[b][:, :],
                    ALU.is_ge, ALU.subtract),
                    deps=[E.get(("cos", qV - 2))])

            for qR in ([0, 1] if p == 0 else []):
                if qR >= nt:
                    continue
                b = qR % 2
                E[("usr", qR)] = A("P", lambda g, b=b: g.tensor_scalar(
                    usr_s[b][:, :], usmat[:, :], rcol[0][b][:, 0:1], 1.0,
                    ALU.mult, ALU.mult),
                    deps=[E[("pow1", qR)]])

            if q0 < nt:
                b = q0 % 2
                E[("v", q0)] = A("V", lambda v, b=b, q0=q0: v.scalar_tensor_tensor(
                    v_s[b][:, :], w1n_rep[:, :], mv[0][b][:, 0:1],
                    pS[sv(q0)][:, :], ALU.mult, ALU.add),
                    deps=[E[("wv", q0)], E[("mv1", q0)]]
                    + ([e_w1rep] if q0 == 0 else []))
                E[("vc", q0)] = A("V", lambda v, b=b: v.tensor_tensor(
                    vcvs[b][:, 0:D], v_s[b][:, :], cosa[b][:, :], ALU.mult),
                    deps=[E[("v", q0)], E[("cos", q0)],
                          E.get(("cumC", q0 - 2))])
                E[("vs", q0)] = A("V", lambda v, b=b: v.tensor_tensor(
                    vcvs[b][:, D:2 * D], v_s[b][:, :], sina[b][:, :], ALU.mult),
                    deps=[E[("v", q0)], E[("sin", q0)],
                          E.get(("cumS", q0 - 2))])

            if p == 0:
                e_wo_cast = []
                for c in range(8):
                    dep = ([e_wo_cast[c - 2]] if c >= 2
                           else [e_wv_cast[6 + c]])
                    ld = A("D", lambda e, c=c: e.dma_start(
                        stg[c % 2][:, :], wot_d[c * TL:(c + 1) * TL, :]),
                        deps=dep, sem=("DWS" if c % 2 == 0 else "DWT"))
                    e_wo_cast.append(A("P", lambda g, c=c: g.tensor_scalar(
                        wot_s[:, c * D:(c + 1) * D], stg[c % 2][:, :],
                        1.0, 1.0, ALU.mult, ALU.mult), deps=[ld]))
                e_wot = e_wo_cast[7]

            if p == 0:
                e_wf_cast = []
                for c in range(8):
                    dep = ([e_wf_cast[c - 2]] if c >= 2
                           else [e_wvt, e_warm])
                    ld = A("D", lambda e, c=c: e.dma_start(
                        ybf_s[c % 2][:, :], wff_d[c * TL:(c + 1) * TL, :]),
                        deps=dep, sem=("DW4" if c % 2 == 0 else "DW5"))
                    e_wf_cast.append(A("P", lambda g, c=c: g.tensor_scalar(
                        wffq_s[:, c * D:(c + 1) * D], ybf_s[c % 2][:, :],
                        rsk[:, 0:1], 1.0, ALU.mult, ALU.mult),
                        deps=[ld, e_rsk]))
                e_wffq = e_wf_cast[7]


            if 0 <= q1 < nt:
                b = q1 % 2
                svr = sa(q1)

                def retT_mms(t, b=b, svr=svr):
                    r = None
                    for j in range(NCHUNK):
                        r = t.transpose(pS_b[svr][:, j * TL:(j + 1) * TL],
                                        ret_s[b][:, j * TL:(j + 1) * TL],
                                        idm[:, :])
                    return r
                E[("retT", q1)] = A("T", retT_mms,
                                    deps=[E[("retadd", q1)],
                                          E.get(("v", q1 + 1))])
                E[("retTe", q1)] = A("S", lambda s, b=b, svr=svr: s.copy(
                    retT_s[b][:, :], pS_b[svr][:, 0:D]),
                    deps=[E[("retT", q1)], E.get(("wo", q1 - 2))])

            if 0 <= q3 < nt:
                b = q3 % 2
                svh = sv(q3)

                def h2T_mms(t, b=b, svh=svh):
                    r = None
                    for j in range(NCHUNK):
                        r = t.transpose(
                            pS_b[svh][:, j * TL:(j + 1) * TL],
                            h2_s[b][:, j * TL:(j + 1) * TL], idm[:, :])
                    return r
                E[("h2T", q3)] = A("T", h2T_mms,
                                   deps=[E[("h2", q3)], E.get(("v", q3 + 3)),
                                         E.get(("retTe", q3 + 2)),
                                         E.get(("quant", q3 - 2))])
                E[("quant", q3)] = A("S", lambda s, b=b, svh=svh: s.activation(
                    h2q_s[b][:, :], pS_b[svh][:, 0:D], AF.Identity,
                    scale=rsin[:, 0:1]),
                    deps=[E[("h2T", q3)], E.get(("ffn", q3 - 2)), e_rsin])

            if q0 < nt:
                b = q0 % 2

                def cumc_mms(t, b=b, q0=q0):
                    r = None
                    for j in range(2):
                        r = t.matmul(pCum[:, j * 512:(j + 1) * 512],
                                     usr_s[b][:, :],
                                     vcvs[b][:, j * 512:(j + 1) * 512],
                                     start=True, stop=(q0 == 0))
                    if q0 > 0:
                        for j in range(2):
                            r = t.matmul(pCum[:, j * 512:(j + 1) * 512],
                                         selm[0:64, :],
                                         cums[(q0 - 1) % 2][:, j * 512:(j + 1) * 512],
                                         start=False, stop=True)
                    return r
                E[("cumC", q0)] = A("T", cumc_mms,
                                    deps=[E[("vc", q0)], E[("usr", q0)],
                                          E.get(("t2", q0 - 1)),
                                          E.get(("cveS", q0 - 1)),
                                          E.get(("cveC", q0 - 1))]
                                    + ([e_w1rep] if q0 == 0 else []))

            if 0 <= q1 < nt:

                def wo_mms(t, b=q1 % 2, q1=q1):
                    r = None
                    for c in range(NCHUNK):
                        for j in range(2):
                            r = t.matmul(
                                pS[sa(q1)][:, j * 512:(j + 1) * 512],
                                retT_s[b][:, c * TL:(c + 1) * TL],
                                wot_s[:, c * D + j * 512:c * D + (j + 1) * 512],
                                start=(c == 0), stop=(c == NCHUNK - 1))
                    return r
                E[("wo", q1)] = A("T", wo_mms,
                                  deps=[E[("retTe", q1)],
                                        E.get(("quant", q1 - 2)),
                                        E.get(("t2act", q1 - 3)),
                                        E.get(("mv2", q1 - 3))]
                                  + ([e_wot] if q1 == 0 else []))

            if 0 <= q2 < nt:
                b = q2 % 2
                pslot = pS[sa(q2)]
                E[("bn2", q2)] = A("V", lambda v, b=b, pslot=pslot: v.bn_stats(
                    bnst[1][b][:, 0:6], pslot[:, 0:512]), deps=[E[("wo", q2)]])
                A("V", lambda v, b=b, pslot=pslot: v.bn_stats(
                    bnst[1][b][:, 6:12], pslot[:, 512:1024]),
                    deps=[E[("wo", q2)]])
                E[("mv2", q2)] = A("V", lambda v, b=b: v.bn_aggr(
                    mv[1][b][:, :], bnst[1][b][:, :]),
                    deps=[E.get(("vep2", q2 - 2))], tag="G")
                E[("vep2", q2)] = A("P", lambda g, b=b: g.tensor_scalar(
                    vep[1][b][:, :], mv[1][b][:, 1:2], 1.0, EPS, ALU.mult,
                    ALU.add), deps=[E[("mv2", q2)], E.get(("pow2", q2 - 2))])
                E[("pow2", q2)] = A("P", lambda g, b=b: g.tensor_tensor(
                    rcol[1][b][:, :], vep[1][b][:, :], mhalf[:, :], ALU.pow),
                    deps=[E[("vep2", q2)], E.get(("t2act", q2 - 2))])
                E[("nmr2", q2)] = A("P", lambda g, b=b: g.tensor_scalar(
                    nmr[1][b][:, :], rcol[1][b][:, :], mv[1][b][:, 0:1], -1.0,
                    ALU.mult, ALU.mult),
                    deps=[E[("pow2", q2)], E.get(("t2act", q2 - 2))])
                E[("t2act", q2)] = A("S", lambda s, b=b, pslot=pslot: s.activation(
                    t2s_s[0][:, :], pslot[:, :], AF.Identity,
                    scale=rcol[1][b][:, 0:1], bias=nmr[1][b][:, 0:1]),
                    deps=[E[("nmr2", q2)], E.get(("xp", q2 - 1))])

            if 0 <= q4 < nt:
                b = q4 % 2
                E[("bn4", q4)] = A("V", lambda v, b=b: v.bn_stats(
                    bnst[3][b][:, 0:6], ybf_s[b][:, 0:512]),
                    deps=[E[("deq", q4)]])
                A("V", lambda v, b=b: v.bn_stats(
                    bnst[3][b][:, 6:12], ybf_s[b][:, 512:1024]),
                    deps=[E[("deq", q4)]])
                E[("mv4", q4)] = A("V", lambda v, b=b: v.bn_aggr(
                    mv[3][b][:, :], bnst[3][b][:, :]),
                    deps=[E.get(("vep4", q4 - 2))], tag="G")

            if q0 < nt:
                b = q0 % 2
                E[("cveC", q0)] = A("S", lambda s, b=b: s.copy(
                    cums[b][:, 0:D], pCum[64:128, :]),
                    deps=[E[("cumC", q0)], E.get(("cumC", q0 - 1))])

            if 0 <= q4 < nt:
                b = q4 % 2
                E[("vep4", q4)] = A("P", lambda g, b=b: g.tensor_scalar(
                    vep[3][b][:, :], mv[3][b][:, 1:2], 1.0, EPS, ALU.mult,
                    ALU.add), deps=[E[("mv4", q4)], E.get(("pow4", q4 - 2))])
                E[("pow4", q4)] = A("P", lambda g, b=b: g.tensor_tensor(
                    rcol[3][b][:, :], vep[3][b][:, :], mhalf[:, :], ALU.pow),
                    deps=[E[("vep4", q4)], E.get(("ln4", q4 - 2))])
                E[("nmr4", q4)] = A("P", lambda g, b=b: g.tensor_scalar(
                    nmr[3][b][:, :], rcol[3][b][:, :], mv[3][b][:, 0:1], -1.0,
                    ALU.mult, ALU.mult),
                    deps=[E[("pow4", q4)], E.get(("ln4", q4 - 2))])
                E[("ln4", q4)] = A("S", lambda s, b=b: s.activation(
                    ln4_s[0][:, :], ybf_s[b][:, :], AF.Identity,
                    scale=rcol[3][b][:, 0:1], bias=nmr[3][b][:, 0:1]),
                    deps=[E[("nmr4", q4)], E.get(("outadd", q4 - 1))]
                    + ([e_wot] if q4 == 0 else []))

            for qR in ([] if p == 0 else [q0 + 1]):
                if qR >= nt:
                    continue
                b = qR % 2
                E[("bn1", qR)] = A("V", lambda v, b=b, qR=qR: v.bn_stats(
                    bnst[0][b][:, 0:6], xtok_s[qR % 4][:, 0:512]),
                    deps=[E[("xl", qR)]])
                A("V", lambda v, b=b, qR=qR: v.bn_stats(
                    bnst[0][b][:, 6:12], xtok_s[qR % 4][:, 512:1024]),
                    deps=[E[("xl", qR)]])
                E[("mv1", qR)] = A("V", lambda v, b=b: v.bn_aggr(
                    mv[0][b][:, :], bnst[0][b][:, :]),
                    deps=[E.get(("v", qR - 2)), E.get(("vep1", qR - 2))],
                    tag="G")
                E[("vep1", qR)] = A("P", lambda g, b=b: g.tensor_scalar(
                    vep[0][b][:, :], mv[0][b][:, 1:2], 1.0, EPS, ALU.mult,
                    ALU.add), deps=[E[("mv1", qR)], E.get(("pow1", qR - 2))])
                E[("pow1", qR)] = A("P", lambda g, b=b: g.tensor_tensor(
                    rcol[0][b][:, :], vep[0][b][:, :], mhalf[:, :], ALU.pow),
                    deps=[E[("vep1", qR)], E.get(("usr", qR - 2))])

            for qV in ([] if p == 0 else [q0 + 2]):
                if qV >= nt:
                    continue
                E[("u", qV)] = A("V", lambda v, qV=qV: v.tensor_scalar(
                    u_s[:, :], f2pi[:, :], tcol_s[qV][:, 0:1], None, ALU.mult),
                    deps=[E.get(("k1", qV - 1)), E[("tc", qV)]])
                E[("k1", qV)] = A("P", lambda g: g.tensor_scalar(
                    k_s[:, :], u_s[:, :], 1.0, MAGIC, ALU.mult, ALU.add),
                    deps=[E[("u", qV)], E.get(("w", qV - 1))])
                E[("k", qV)] = A("P", lambda g: g.tensor_scalar(
                    k_s[:, :], k_s[:, :], 1.0, -MAGIC, ALU.mult, ALU.add),
                    deps=[E[("k1", qV)]])

            for qT in [q0 + 4]:
                if qT >= nt:
                    continue
                E[("tc", qT)] = A("V", lambda v, qT=qT: v.tensor_copy(
                    tcol_s[qT][:, :], tcall_s[:, qT:qT + 1]), deps=[e_tcall])

            if 0 <= q3 < nt:
                b = q3 % 2

                def ffn_mms(t, b=b, q3=q3):
                    r = None
                    for m in range(4):
                        for j in range(2):
                            lhsT = bass.AP(h2q_s[b], m * 256,
                                           [[NCHUNK * D // 8, TL], [TL, 2], [1, TL]])
                            rhs = bass.AP(wffq_s, m * 2 * D + j * 512,
                                          [[NCHUNK * D, TL], [D, 2], [1, 512]])
                            r = t.matmul(
                                pS[sy(q3)][:, j * 512:(j + 1) * 512],
                                lhsT, rhs, start=(m == 0), stop=False,
                                perf_mode=mybir.MatmulPerfMode.DoubleRow)
                    for j in range(2):
                        r = t.matmul(
                            pS[sy(q3)][:, j * 512:(j + 1) * 512],
                            onesb_row[0:1, :],
                            bdiv[0:1, j * 512:(j + 1) * 512],
                            start=False, stop=True)
                    return r
                E[("ffn", q3)] = A("T", ffn_mms,
                                   deps=[E[("quant", q3)], e_wffq,
                                         E.get(("t2act", q3 + 1)),
                                         E.get(("mv2", q3 + 1))]
                                   + ([e_bdiv, e_ms[4]] if q3 == 0 else []))

            if 0 <= q2 < nt:
                b = q2 % 2
                E[("xp", q2)] = A("V", lambda v, b=b, q2=q2: v.tensor_tensor(
                    xp_s[q2 % 3][:, :], xtok_s[q2 % 4][:, :], t2s_s[0][:, :],
                    ALU.add), deps=[E[("t2act", q2)],
                                    E.get(("outadd", q2 - 3))])
                E[("bn3", q2)] = A("V", lambda v, b=b, q2=q2: v.bn_stats(
                    bnst[2][b][:, 0:6], xp_s[q2 % 3][:, 0:512]),
                    deps=[E[("xp", q2)]])
                A("V", lambda v, b=b, q2=q2: v.bn_stats(
                    bnst[2][b][:, 6:12], xp_s[q2 % 3][:, 512:1024]),
                    deps=[E[("xp", q2)]])
                E[("mv3", q2)] = A("V", lambda v, b=b: v.bn_aggr(
                    mv[2][b][:, :], bnst[2][b][:, :]),
                    deps=[E.get(("vep3", q2 - 2))], tag="G")
                E[("vep3", q2)] = A("P", lambda g, b=b: g.tensor_scalar(
                    vep[2][b][:, :], mv[2][b][:, 1:2], 1.0, EPS, ALU.mult,
                    ALU.add), deps=[E[("mv3", q2)], E.get(("pow3", q2 - 2))])
                E[("pow3", q2)] = A("P", lambda g, b=b: g.tensor_tensor(
                    rcol[2][b][:, :], vep[2][b][:, :], mhalf[:, :], ALU.pow),
                    deps=[E[("vep3", q2)], E.get(("h2", q2 - 2))])
                E[("nmr3", q2)] = A("P", lambda g, b=b: g.tensor_scalar(
                    nmr[2][b][:, :], rcol[2][b][:, :], mv[2][b][:, 0:1], -1.0,
                    ALU.mult, ALU.mult),
                    deps=[E[("pow3", q2)], E.get(("h2", q2 - 2))])
                E[("h2", q2)] = A("S", lambda s, b=b, q2=q2: s.activation(
                    h2_s[b][:, :], xp_s[q2 % 3][:, :], AF.Identity,
                    scale=rcol[2][b][:, 0:1], bias=nmr[2][b][:, 0:1]),
                    deps=[E[("nmr3", q2)], E.get(("h2T", q2 - 2))])

            if q0 < nt:
                b = q0 % 2
                E[("t1", q0)] = A("V", lambda v, b=b: v.tensor_tensor(
                    ta_s[b][:, :], pCum[:, :], cosa[b][:, :], ALU.mult),
                    deps=[E[("cumC", q0)], E[("cveC", q0)], E[("cos", q0)],
                          E.get(("retadd", q0 - 2))])

            for qR in ([] if p == 0 else [q0 + 1]):
                if qR >= nt:
                    continue
                b = qR % 2
                E[("usr", qR)] = A("P", lambda g, b=b: g.tensor_scalar(
                    usr_s[b][:, :], usmat[:, :], rcol[0][b][:, 0:1], 1.0,
                    ALU.mult, ALU.mult),
                    deps=[E[("pow1", qR)], E.get(("cumS", qR - 2))])

            for qV in ([] if p == 0 else [q0 + 2]):
                if qV >= nt:
                    continue
                b = qV % 2
                E[("w", qV)] = A("V", lambda v, b=b: v.scalar_tensor_tensor(
                    w_s[b][:, :], k_s[:, :], -1.0, u_s[:, :], ALU.mult, ALU.add),
                    deps=[E[("k", qV)], E.get(("sin", qV - 2))])
                E[("wc", qV)] = A("V", lambda v, b=b: v.scalar_tensor_tensor(
                    wc_s[b][:, :], w_s[b][:, :], 0.25, w_s[b][:, :],
                    ALU.is_ge, ALU.subtract),
                    deps=[E.get(("cos", qV - 2))])

            for qS in ([] if p == 0 else [q0 + 1]):
                if qS >= nt:
                    continue
                b = qS % 2
                E[("sin", qS)] = A("S", lambda s, b=b: s.activation(
                    sina[b][:, :], w_s[b][:, :], AF.Sin, scale=TWO_PI),
                    deps=[E[("w", qS)], E.get(("t2", qS - 2)),
                          E.get(("vs", qS - 2))]
                    + ([e_ms[2]] if qS == 0 else []))
                E[("cos", qS)] = A("S", lambda s, b=b: s.activation(
                    cosa[b][:, :], wc_s[b][:, :], AF.Sin, scale=-TWO_PI,
                    bias=piby2[:, :]),
                    deps=[E[("wc", qS)], E.get(("t1", qS - 2)),
                          E.get(("vc", qS - 2))]
                    + ([e_ms[2]] if qS == 0 else []))

            if q0 < nt:
                b = q0 % 2

                def cums_mms(t, b=b, q0=q0):
                    r = None
                    for j in range(2):
                        r = t.matmul(pCum[:, j * 512:(j + 1) * 512],
                                     usr_s[b][:, :],
                                     vcvs[b][:, D + j * 512:D + (j + 1) * 512],
                                     start=True, stop=(q0 == 0))
                    if q0 > 0:
                        for j in range(2):
                            r = t.matmul(pCum[:, j * 512:(j + 1) * 512],
                                         selm[0:64, :],
                                         cums[(q0 - 1) % 2][:, D + j * 512:D + (j + 1) * 512],
                                         start=False, stop=True)
                    return r
                E[("cumS", q0)] = A("T", cums_mms,
                                    deps=[E[("vs", q0)], E[("t1", q0)],
                                          E[("cveC", q0)]])
                E[("t2", q0)] = A("V", lambda v, b=b: v.tensor_tensor(
                    tb_s[b][:, :], pCum[:, :], sina[b][:, :], ALU.mult),
                    deps=[E[("cumS", q0)], E[("sin", q0)],
                          E.get(("retadd", q0 - 2))])
                E[("retadd", q0)] = A("V", lambda v, b=b: v.tensor_tensor(
                    ret_s[b][:, :], ta_s[b][:, :], tb_s[b][:, :], ALU.add),
                    deps=[E[("t1", q0)], E[("t2", q0)],
                          E.get(("retT", q0 - 2))])
                E[("cveS", q0)] = A("S", lambda s, b=b: s.copy(
                    cums[b][:, D:2 * D], pCum[64:128, :]),
                    deps=[E[("cumS", q0)], E[("t2", q0)],
                          E.get(("cumS", q0 - 1))])

            for qW in ([1] if p == 0 else [q0 + 1]):
                if qW >= nt:
                    continue

                def wv_mms(t, qW=qW):
                    r = None
                    for c in range(NCHUNK):
                        for j in range(2):
                            r = t.matmul(
                                pS[sv(qW)][:, j * 512:(j + 1) * 512],
                                xftb_s[qW % 2][:, c * TL:(c + 1) * TL],
                                wvt_s[:, c * D + j * 512:c * D + (j + 1) * 512],
                                start=(c == 0), stop=(c == NCHUNK - 1))
                    return r
                E[("wv", qW)] = A("T", wv_mms,
                                  deps=[E[("fcast", qW)], e_wvt,
                                        E.get(("deq", qW - 5)),
                                        E.get(("t2act", qW - 4)),
                                        E.get(("mv2", qW - 4))]
                                  + ([e_f2pi] if qW <= 1 else []))
            if 0 <= q4 < nt:
                b = q4 % 2
                E[("outadd", q4)] = A("P", lambda g, b=b, q4=q4: g.tensor_tensor(
                    out_s[b][:, :], xp_s[q4 % 3][:, :], ln4_s[0][:, :], ALU.add),
                    deps=[E[("ln4", q4)], E[("xp", q4)],
                          E.get(("ost", q4 - 2))])
                E[("ost", q4)] = A("P", lambda g, b=b, q4=q4: g.dma_start(
                    out_d[q4 * TL:(q4 + 1) * TL, :], out_s[b][:, :]),
                    deps=[E[("outadd", q4)]], sem=f"DO{q4 % 2}", inc=16)

        with nc.Block() as block:

            @block.sync
            def _(e):
                sch.emit("D", e, sems, cut)
                for sname, cval in sorted(sch.dma_counts_upto(cut).items()):
                    e.wait_ge(sems[sname], cval)

            @block.vector
            def _(v):
                sch.emit("V", v, sems, cut, selfsync=("VSP" if racecheck else selfsync))

            @block.scalar
            def _(s):
                sch.emit("S", s, sems, cut, selfsync=("VSP" if racecheck else selfsync))

            @block.gpsimd
            def _(g):
                sch.emit("P", g, sems, cut, selfsync=("VSP" if racecheck else selfsync))

            @block.tensor
            def _(t):
                sch.emit("T", t, sems, cut, selfsync=("VSP" if racecheck else selfsync))

    return nc


def _host_layout(x_b, Wv, Wo, Wff, freqs, s_in, s_k, bias, nt):
    T = nt * TL
    xtok = np.ascontiguousarray(x_b)
    xft = np.ascontiguousarray(
        x_b.T.reshape(NCHUNK, TL, nt, TL).transpose(2, 1, 0, 3)
        .reshape(nt * TL, D))
    wvt = np.ascontiguousarray(Wv.T)
    wot = np.ascontiguousarray(Wo.T)
    wff = np.ascontiguousarray(Wff)
    tcols = (np.arange(TL, dtype=np.float32)[:, None]
             + TL * np.arange(nt, dtype=np.float32)[None, :])
    us = np.triu(np.ones((TL, TL), dtype=np.float32)).astype(ml_dtypes.bfloat16)
    idm = np.eye(TL, dtype=np.float32).astype(ml_dtypes.bfloat16)
    sel = np.zeros((TL, TL), dtype=np.float32)
    sel[63, :] = 1.0
    sel = sel.astype(ml_dtypes.bfloat16)
    return {
        "xtok": xtok, "xft": xft, "wvt": wvt, "wot": wot, "wff": wff,
        "freqs": np.ascontiguousarray(freqs.reshape(1, D)),
        "tcols": np.ascontiguousarray(tcols),
        "usmat": us, "idmat": idm, "selmat": sel,
        "s_in": s_in.reshape(1, 1), "s_k": s_k.reshape(1, 1),
        "biasv": np.ascontiguousarray(bias.reshape(1, D)),
    }


def kernel(L0_attn_x, L0_attn_Wk, L0_attn_Wv, L0_attn_Wo, L0_attn_freqs,
           L0_ffn_kernel, L0_ffn_input_scale, L0_ffn_kernel_scale, L0_ffn_bias):
    x = np.asarray(L0_attn_x)
    assert np.asarray(L0_attn_Wk).shape == np.asarray(L0_attn_Wv).shape
    B, T, Dd = x.shape
    nt = T // TL
    nc = build(nt)
    in_maps = []
    for b in range(B):
        in_maps.append(_host_layout(
            x[b], np.asarray(L0_attn_Wv), np.asarray(L0_attn_Wo),
            np.asarray(L0_ffn_kernel), np.asarray(L0_attn_freqs),
            np.asarray(L0_ffn_input_scale), np.asarray(L0_ffn_kernel_scale),
            np.asarray(L0_ffn_bias), nt))
    res = run_bass_kernel_spmd(nc, in_maps, core_ids=list(range(B)))
    out = np.stack([np.asarray(r["out"]) for r in res.results], axis=0)
    return out.astype(np.float32)


# version 61
# speedup vs baseline: 1.1983x; 1.1983x over previous
import numpy as np
import ml_dtypes
import concourse.bass as bass
import concourse.mybir as mybir
from concourse.bass_utils import run_bass_kernel_spmd
from contextlib import ExitStack

F32 = mybir.dt.float32
BF16 = mybir.dt.bfloat16
F8 = mybir.dt.float8e4
I32 = mybir.dt.int32
AF = mybir.ActivationFunctionType
ALU = mybir.AluOpType

MAGIC = 12582912.0
TWO_PI = 6.283185307179586
EPS = 1e-5
D = 1024
TL = 128
NCHUNK = 8

ENGSEM = {"T": "semT", "V": "semV", "S": "semS", "P": "semP"}


class Sched:

    def __init__(self):
        self.q = {k: [] for k in ["T", "V", "S", "P", "D"]}
        self.counts = {}
        self.sem_names = set()
        self.seq = 0

    def add(self, eng, fn, deps=(), sem=None, inc=None, tag=""):
        if sem is None:
            sem = ENGSEM[eng]
        if inc is None:
            inc = 16 if eng == "D" else 1
        self.sem_names.add(sem)
        self.counts[sem] = self.counts.get(sem, 0) + inc
        ev = (sem, self.counts[sem])
        deps = [d for d in deps if d is not None]
        self.seq += 1
        self.q[eng].append((deps, fn, sem, inc, self.seq, tag))
        return ev

    def dma_counts_upto(self, cut):
        out = {}
        engsems = set(ENGSEM.values())
        for qk in self.q:
            for (deps, fn, sem, inc, seq, tag) in self.q[qk]:
                if sem in engsems:
                    continue
                if cut is None or seq <= cut:
                    out[sem] = out.get(sem, 0) + inc
        return out

    def emit(self, eng_key, engine, sems, cut=None, selfsync=""):
        floors = {}
        own = ENGSEM.get(eng_key)
        own_count = 0
        for deps, fn, sem, inc, seq, tag in self.q[eng_key]:
            if cut is not None and seq > cut:
                continue
            need = {}
            for (s, v) in deps:
                if v > need.get(s, 0):
                    need[s] = v
            ssync = (eng_key in selfsync) or (tag and tag in selfsync)
            if ssync and own is not None \
                    and own_count > 0 and own_count > need.get(own, 0):
                need[own] = own_count
            for s, v in sorted(need.items()):
                if v > floors.get(s, 0):
                    engine.wait_ge(sems[s], v)
                    floors[s] = v
            instr = fn(engine)
            instr.then_inc(sems[sem], inc)
            if sem == own:
                own_count += inc


def build(nt=32, cut=None, racecheck=False, selfsync="G"):
    nc = bass.Bass("TRN2", target_bir_lowering=False, debug=False,
                   detect_race_conditions=racecheck)
    T = nt * TL

    xtok_d = nc.dram_tensor("xtok", [T, D], F32, kind="ExternalInput")
    xft_d = nc.dram_tensor("xft", [nt * TL, D], F32, kind="ExternalInput")
    wvt_d = nc.dram_tensor("wvt", [NCHUNK * TL, D], F32, kind="ExternalInput")
    wot_d = nc.dram_tensor("wot", [NCHUNK * TL, D], F32, kind="ExternalInput")
    wff_d = nc.dram_tensor("wff", [NCHUNK * TL, D], BF16, kind="ExternalInput")
    freqs_d = nc.dram_tensor("freqs", [1, D], F32, kind="ExternalInput")
    tcols_d = nc.dram_tensor("tcols", [TL, nt], F32, kind="ExternalInput")
    us_d = nc.dram_tensor("usmat", [TL, TL], BF16, kind="ExternalInput")
    id_d = nc.dram_tensor("idmat", [TL, TL], BF16, kind="ExternalInput")
    sel_d = nc.dram_tensor("selmat", [TL, TL], BF16, kind="ExternalInput")
    sin_d = nc.dram_tensor("s_in", [1, 1], F32, kind="ExternalInput")
    sk_d = nc.dram_tensor("s_k", [1, 1], F32, kind="ExternalInput")
    bias_d = nc.dram_tensor("biasv", [1, D], BF16, kind="ExternalInput")
    out_d = nc.dram_tensor("out", [T, D], F32, kind="ExternalOutput")

    sch = Sched()
    with ExitStack() as st:
        def sb(name, shape, dt):
            return st.enter_context(nc.sbuf_tensor(name, shape, dt))

        sems = {}
        for s in ["semT", "semV", "semS", "semP", "DW", "DWS", "DWT",
                  "DWU", "DWV", "DW4", "DW5", "DW6", "DW7",
                  "DWA", "DWB", "DWC", "DWD", "DWE", "DWF", "DWG", "DWH",
                  "DX0", "DX1", "DX2", "DX3", "DF0", "DF1", "DO0", "DO1"]:
            sems[s] = st.enter_context(nc.semaphore(s))

        wstage = sb("wstage", [TL, 1024], F32)
        tcall_s = sb("tcall", [TL, nt], F32)
        wvt_s = sb("wvt_s", [TL, NCHUNK * D], BF16)
        wot_s = sb("wot_s", [TL, NCHUNK * D], BF16)
        wffq_s = sb("wffq_s", [TL, NCHUNK * D], F8)
        f2pi = sb("f2pi", [TL, D], F32)
        frow = sb("frow", [1, D], F32)
        tcol_s = [sb(f"tc{i}", [TL, 1], F32) for i in range(nt)]
        usmat = sb("usmat_s", [TL, TL], BF16)
        idm = sb("idm", [TL, TL], BF16)
        selm = sb("selm", [TL, TL], BF16)
        onesf = sb("onesf", [1, TL], F32)
        onesb_row = sb("onesb_row", [1, TL], BF16)
        onescol_b = sb("onescol_b", [TL, 1], BF16)
        w1n_rep = sb("w1n_rep", [TL, D], BF16)
        w1row = sb("w1row", [1, D], BF16)
        bdiv = sb("bdiv", [1, D], BF16)
        rr = sb("rr", [1, 1], F32)
        brow = sb("brow", [1, D], BF16)
        sincol = sb("sincol", [TL, 1], F32)
        skcol = sb("skcol", [TL, 1], F32)
        rsin = sb("rsin", [TL, 1], F32)
        rsk = sb("rsk", [TL, 1], F32)
        sisk = sb("sisk", [TL, 1], F32)
        piby2 = sb("piby2", [TL, 1], F32)
        mhalf = sb("mhalf", [TL, 1], F32)
        srow = sb("srow", [1, 2], F32)

        xtok_s = [sb(f"xtok{i}", [TL, D], F32) for i in range(4)]
        xftf_s = [sb(f"xftf{i}", [TL, D], F32) for i in range(2)]
        xftb_s = [sb(f"xftb{i}", [TL, D], BF16) for i in range(2)]
        u_s = sb("u_s", [TL, D], F32)
        k_s = sb("k_s", [TL, D], F32)
        w_s = [sb(f"w{i}", [TL, D], F32) for i in range(2)]
        wc_s = [sb(f"wc{i}", [TL, D], F32) for i in range(2)]
        sina = [sb(f"sina{i}", [TL, D], BF16) for i in range(2)]
        cosa = [sb(f"cosa{i}", [TL, D], BF16) for i in range(2)]
        usr_s = [sb(f"usr{i}", [TL, TL], BF16) for i in range(2)]
        v_s = [sb(f"v{i}", [TL, D], BF16) for i in range(2)]
        vcvs = [sb(f"vcvs{i}", [TL, 2 * D], BF16) for i in range(2)]
        ta_s = [sb(f"ta{i}", [TL, D], BF16) for i in range(2)]
        tb_s = [sb(f"tb{i}", [TL, D], BF16) for i in range(2)]
        ret_s = [sb(f"ret{i}", [TL, D], BF16) for i in range(2)]
        retT_s = [sb(f"retT{i}", [TL, D], BF16) for i in range(2)]
        cums = [sb(f"cums{i}", [64, 2 * D], BF16) for i in range(2)]
        t2s_s = [sb("t2s0", [TL, D], F32)]
        xp_s = [sb(f"xp{i}", [TL, D], F32) for i in range(3)]
        h2_s = [sb(f"h2_{i}", [TL, D], BF16) for i in range(2)]
        h2q_s = [sb(f"h2q{i}", [TL, D], F8) for i in range(2)]
        ybf_s = [sb(f"ybf{i}", [TL, D], BF16) for i in range(2)]
        ln4_s = [sb("ln4_0", [TL, D], F32)]
        wstage2 = ln4_s[0]
        out_s = [sb(f"outs{i}", [TL, D], F32) for i in range(2)]
        bnst = [[sb(f"bnst{ln}_{i}", [TL, 12], F32) for i in range(2)]
                for ln in range(4)]
        mv = [[sb(f"mv{ln}_{i}", [TL, 2], F32) for i in range(2)]
              for ln in range(4)]
        vep = [[sb(f"vep{ln}_{i}", [TL, 1], F32) for i in range(2)]
               for ln in range(4)]
        rcol = [[sb(f"r{ln}_{i}", [TL, 1], F32) for i in range(2)]
                for ln in range(4)]
        nmr = [[sb(f"nmr{ln}_{i}", [TL, 1], F32) for i in range(2)]
               for ln in range(4)]

        pS = [st.enter_context(nc.psum_tensor("pA", [TL, D], F32)),
              st.enter_context(nc.psum_tensor("pB", [TL, D], F32)),
              st.enter_context(nc.psum_tensor("pC", [TL, D], F32))]
        pCum = st.enter_context(nc.psum_tensor("pCum", [TL, D], F32))
        pS_b = [p.ap().bitcast(BF16) for p in pS]

        A = sch.add
        pre_loads = []
        for dst, src in [(usmat, us_d), (idm, id_d),
                         (selm, sel_d), (frow, freqs_d), (brow, bias_d)]:
            pre_loads.append(A("D", lambda e, d=dst, s=src: e.dma_start(d[:, :], s[:, :]),
                               deps=(), sem="DW"))
        e_tcall = A("D", lambda e: e.dma_start(tcall_s[:, :], tcols_d[:, :]),
                    deps=(), sem="DW")
        e_sin_l = A("D", lambda e: e.dma_start(srow[0:1, 0:1], sin_d[:, :]), sem="DW")
        e_sk_l = A("D", lambda e: e.dma_start(srow[0:1, 1:2], sk_d[:, :]), sem="DW")
        pre_loads += [e_tcall, e_sin_l, e_sk_l]
        e_dw_all = [e_sk_l]

        e_ms = [
            A("V", lambda v: v.memset(onesf[:, :], 1.0)),
            A("V", lambda v: v.memset(onescol_b[:, :], 1.0)),
            A("V", lambda v: v.memset(piby2[:, :], TWO_PI / 4.0)),
            A("P", lambda g: g.memset(mhalf[:, :], -0.5)),
            A("V", lambda v: v.memset(onesb_row[:, :], 1.0)),
        ]

        E = {}
        E[("xl", 0)] = A("D", lambda e: e.dma_start(
            xtok_s[0][:, :], xtok_d[0:TL, :]), deps=(), sem="DX0")
        E[("fl", 0)] = A("D", lambda e: e.dma_start(
            xftf_s[0][:, :], xft_d[0:TL, :]), deps=(), sem="DF0")
        E[("xl", 1)] = A("D", lambda e: e.dma_start(
            xtok_s[1][:, :], xtok_d[TL:2 * TL, :]), deps=(), sem="DX1")
        E[("fl", 1)] = A("D", lambda e: e.dma_start(
            xftf_s[1][:, :], xft_d[TL:2 * TL, :]), deps=(), sem="DF1")
        E[("fcast", 0)] = A("P", lambda g: g.tensor_scalar(
            xftb_s[0][:, :], xftf_s[0][:, :], 1.0, 1.0, ALU.mult,
            ALU.mult), deps=[E[("fl", 0)]])
        E[("fcast", 1)] = A("P", lambda g: g.tensor_scalar(
            xftb_s[1][:, :], xftf_s[1][:, :], 1.0, 1.0, ALU.mult,
            ALU.mult), deps=[E[("fl", 1)]])

        e_warm = A("S", lambda s_: s_.activation(
            ybf_s[0][0:1, 0:1], piby2[0:1, 0:1], AF.Sin, scale=0.0),
            deps=[e_ms[2]])

        e_sbcast = A("T", lambda t: t.matmul(
            pS[0][:, 0:2], onesf[:, :], srow[0:1, 0:2], start=True, stop=True),
            deps=e_dw_all + [e_ms[0]])
        e_scols = A("V", lambda v: v.tensor_copy(sincol[:, :], pS[0][:, 0:1]),
                    deps=[e_sbcast])
        e_scols2 = A("V", lambda v: v.tensor_copy(skcol[:, :], pS[0][:, 1:2]),
                     deps=[e_sbcast])
        e_rsin = A("V", lambda v: v.reciprocal(rsin[:, :], sincol[:, :]),
                   deps=[e_scols])
        e_rsk = A("V", lambda v: v.reciprocal(rsk[:, :], skcol[:, :]),
                  deps=[e_scols2])
        e_sisk = A("V", lambda v: v.tensor_tensor(sisk[:, :], sincol[:, :],
                                                  skcol[:, :], ALU.mult),
                   deps=[e_scols, e_scols2])

        e_fb = [A("T", lambda t, j=j: t.matmul(
            pS[1][:, j * 512:(j + 1) * 512], onesf[:, :],
            frow[0:1, j * 512:(j + 1) * 512], start=True, stop=True),
            deps=e_dw_all + [e_ms[0], e_scols, e_scols2]) for j in range(2)]
        e_f2pi = A("V", lambda v: v.tensor_scalar(
            f2pi[:, :], pS[1][:, :], 1.0 / TWO_PI, None, ALU.mult), deps=e_fb)

        e_rr = A("V", lambda v: v.tensor_tensor(
            rr[0:1, :], rsin[0:1, 0:1], rsk[0:1, 0:1], ALU.mult),
            deps=[e_rsin, e_rsk])
        e_bdiv = A("V", lambda v: v.tensor_scalar(
            bdiv[0:1, :], brow[0:1, :], rr[0:1, 0:1], None, ALU.mult),
            deps=[e_rr] + e_dw_all)

        wstage_bf = wstage.ap().bitcast(BF16)
        wstage2_bf = wstage2.ap().bitcast(BF16)
        stg = [wstage, wstage2]
        stg_b = [wstage_bf, wstage2_bf]
        e_wv_cast = []
        e_wv0 = None
        for c in range(8):
            dep = [e_wv_cast[c - 2]] if c >= 2 else []
            qsems = (["DWS", "DWT", "DWU", "DWV"] if c % 2 == 0
                     else ["DW4", "DW5", "DW6", "DW7"])
            lds = []
            for h in range(4):
                lds.append(A("D", lambda e, c=c, h=h: e.dma_start(
                    stg[c % 2][:, h * 256:(h + 1) * 256],
                    wvt_d[c * TL:(c + 1) * TL, h * 256:(h + 1) * 256]),
                    deps=dep, sem=qsems[h]))
            e_wv_cast.append(A("V", lambda v, c=c: v.tensor_scalar(
                wvt_s[:, c * D:(c + 1) * D], stg[c % 2][:, :],
                1.0, None, ALU.mult), deps=lds))

            def w1c_mms(t, c=c):
                r = None
                for j in range(2):
                    r = t.matmul(pCum[0:1, j * 512:(j + 1) * 512],
                                 onescol_b[:, :],
                                 wvt_s[:, c * D + j * 512:c * D + (j + 1) * 512],
                                 start=(c == 0), stop=(c == NCHUNK - 1))
                return r
            e_w1mm = A("T", w1c_mms,
                       deps=[e_wv_cast[c]] + ([e_ms[1]] if c == 0 else []))

            def wv0c_mms(t, c=c):
                r = None
                for j in range(2):
                    r = t.matmul(pS[0][:, j * 512:(j + 1) * 512],
                                 xftb_s[0][:, c * TL:(c + 1) * TL],
                                 wvt_s[:, c * D + j * 512:c * D + (j + 1) * 512],
                                 start=(c == 0), stop=(c == NCHUNK - 1))
                return r
            e_wv0 = A("T", wv0c_mms,
                      deps=[e_wv_cast[c]]
                      + ([E[("fcast", 0)], e_scols, e_scols2] if c == 0 else []))
        e_wvt = e_wv_cast[7]
        E[("wv", 0)] = e_wv0

        e_w1row = A("V", lambda v: v.tensor_scalar(
            w1row[0:1, :], pCum[0:1, 0:D], -1.0, None, ALU.mult), deps=[e_w1mm])
        e_w1b = [A("T", lambda t, j=j: t.matmul(
            pCum[:, j * 512:(j + 1) * 512], onesb_row[0:1, :],
            w1row[0:1, j * 512:(j + 1) * 512], start=True, stop=True),
            deps=[e_w1row, e_ms[4]]) for j in range(2)]
        e_w1rep = A("V", lambda v: v.tensor_scalar(
            w1n_rep[:, :], pCum[:, :], 1.0, None, ALU.mult), deps=e_w1b)

        for i in range(min(4, nt)):
            E[("tc", i)] = A("V", lambda v, i=i: v.tensor_copy(
                tcol_s[i][:, :], tcall_s[:, i:i + 1]), deps=[e_tcall])
        e_wot = None
        e_wffq = None


        def sv(q):
            return q % 3

        def sa(q):
            return (q + 1) % 3

        def sy(q):
            return (q + 2) % 3

        for p in range(nt + 5):
            q0, q1, q2, q3, q4 = p, p - 1, p - 2, p - 3, p - 4

            if 2 <= p + 1 < nt:
                qq = p + 1
                E[("xl", qq)] = A("D", lambda e, qq=qq: e.dma_start(
                    xtok_s[qq % 4][:, :], xtok_d[qq * TL:(qq + 1) * TL, :]),
                    deps=[E.get(("xp", qq - 4))],
                    sem=f"DX{qq % 4}")
                E[("fl", qq)] = A("D", lambda e, qq=qq: e.dma_start(
                    xftf_s[qq % 2][:, :], xft_d[qq * TL:(qq + 1) * TL, :]),
                    deps=[E.get(("fcast", qq - 2))],
                    sem=f"DF{qq % 2}")
                E[("fcast", qq)] = A("P", lambda g, qq=qq: g.tensor_scalar(
                    xftb_s[qq % 2][:, :], xftf_s[qq % 2][:, :], 1.0, 1.0,
                    ALU.mult, ALU.mult),
                    deps=[E[("fl", qq)], E.get(("wv", qq - 2))])

            if 0 <= q4 < nt:
                b = q4 % 2
                E[("deq", q4)] = A("S", lambda s, b=b, q4=q4: s.activation(
                    ybf_s[b][:, :], pS[sy(q4)][:, :], AF.Identity,
                    scale=sisk[:, 0:1]),
                    deps=[E[("ffn", q4)], E.get(("ln4", q4 - 2)),
                          E.get(("mv4", q4 - 2)), e_sisk]
                    + ([e_wf_cast[4 + q4]] if q4 <= 1 else []))

            for qR in ([0, 1] if p == 0 else []):
                if qR >= nt:
                    continue
                b = qR % 2
                E[("bn1", qR)] = A("V", lambda v, b=b, qR=qR: v.bn_stats(
                    bnst[0][b][:, 0:6], xtok_s[qR % 4][:, 0:512]),
                    deps=[E[("xl", qR)]])
                A("V", lambda v, b=b, qR=qR: v.bn_stats(
                    bnst[0][b][:, 6:12], xtok_s[qR % 4][:, 512:1024]),
                    deps=[E[("xl", qR)]])
                E[("mv1", qR)] = A("V", lambda v, b=b: v.bn_aggr(
                    mv[0][b][:, :], bnst[0][b][:, :]),
                    deps=[E.get(("v", qR - 2)), E.get(("vep1", qR - 2))],
                    tag="G")
                E[("vep1", qR)] = A("P", lambda g, b=b: g.tensor_scalar(
                    vep[0][b][:, :], mv[0][b][:, 1:2], 1.0, EPS, ALU.mult,
                    ALU.add), deps=[E[("mv1", qR)], E.get(("pow1", qR - 2))])
                E[("pow1", qR)] = A("P", lambda g, b=b: g.tensor_tensor(
                    rcol[0][b][:, :], vep[0][b][:, :], mhalf[:, :], ALU.pow),
                    deps=[E[("vep1", qR)], E.get(("usr", qR - 2))])
            for qV in ([0, 1] if p == 0 else []):
                if qV >= nt:
                    continue
                E[("u", qV)] = A("V", lambda v, qV=qV: v.tensor_scalar(
                    u_s[:, :], f2pi[:, :], tcol_s[qV][:, 0:1], None, ALU.mult),
                    deps=([e_f2pi] if qV == 0 else []) + [E.get(("k1", qV - 1)),
                          E[("tc", qV)]])
                E[("k1", qV)] = A("P", lambda g: g.tensor_scalar(
                    k_s[:, :], u_s[:, :], 1.0, MAGIC, ALU.mult, ALU.add),
                    deps=[E[("u", qV)], E.get(("w", qV - 1))])
                E[("k", qV)] = A("P", lambda g: g.tensor_scalar(
                    k_s[:, :], k_s[:, :], 1.0, -MAGIC, ALU.mult, ALU.add),
                    deps=[E[("k1", qV)]])
                b = qV % 2
                E[("w", qV)] = A("V", lambda v, b=b: v.scalar_tensor_tensor(
                    w_s[b][:, :], k_s[:, :], -1.0, u_s[:, :], ALU.mult, ALU.add),
                    deps=[E[("k", qV)], E.get(("sin", qV - 2))])
                E[("wc", qV)] = A("V", lambda v, b=b: v.scalar_tensor_tensor(
                    wc_s[b][:, :], w_s[b][:, :], 0.25, w_s[b][:, :],
                    ALU.is_ge, ALU.subtract),
                    deps=[E.get(("cos", qV - 2))])

            for qS in ([0, 1] if p == 0 else []):
                if qS >= nt:
                    continue
                b = qS % 2
                E[("sin", qS)] = A("S", lambda s, b=b: s.activation(
                    sina[b][:, :], w_s[b][:, :], AF.Sin, scale=TWO_PI),
                    deps=[E[("w", qS)]] + [e_ms[2]])
                E[("cos", qS)] = A("S", lambda s, b=b: s.activation(
                    cosa[b][:, :], wc_s[b][:, :], AF.Sin, scale=-TWO_PI,
                    bias=piby2[:, :]),
                    deps=[E[("wc", qS)]] + [e_ms[2]])

            for qV in ([2] if p == 0 else []):
                if qV >= nt:
                    continue
                E[("u", qV)] = A("V", lambda v, qV=qV: v.tensor_scalar(
                    u_s[:, :], f2pi[:, :], tcol_s[qV][:, 0:1], None, ALU.mult),
                    deps=[E.get(("k1", qV - 1)), E[("tc", qV)]])
                E[("k1", qV)] = A("P", lambda g: g.tensor_scalar(
                    k_s[:, :], u_s[:, :], 1.0, MAGIC, ALU.mult, ALU.add),
                    deps=[E[("u", qV)], E.get(("w", qV - 1))])
                E[("k", qV)] = A("P", lambda g: g.tensor_scalar(
                    k_s[:, :], k_s[:, :], 1.0, -MAGIC, ALU.mult, ALU.add),
                    deps=[E[("k1", qV)]])
                b = qV % 2
                E[("w", qV)] = A("V", lambda v, b=b: v.scalar_tensor_tensor(
                    w_s[b][:, :], k_s[:, :], -1.0, u_s[:, :], ALU.mult, ALU.add),
                    deps=[E[("k", qV)], E.get(("sin", qV - 2))])
                E[("wc", qV)] = A("V", lambda v, b=b: v.scalar_tensor_tensor(
                    wc_s[b][:, :], w_s[b][:, :], 0.25, w_s[b][:, :],
                    ALU.is_ge, ALU.subtract),
                    deps=[E.get(("cos", qV - 2))])

            for qR in ([0, 1] if p == 0 else []):
                if qR >= nt:
                    continue
                b = qR % 2
                E[("usr", qR)] = A("P", lambda g, b=b: g.tensor_scalar(
                    usr_s[b][:, :], usmat[:, :], rcol[0][b][:, 0:1], 1.0,
                    ALU.mult, ALU.mult),
                    deps=[E[("pow1", qR)]])

            if q0 < nt:
                b = q0 % 2
                E[("v", q0)] = A("V", lambda v, b=b, q0=q0: v.scalar_tensor_tensor(
                    v_s[b][:, :], w1n_rep[:, :], mv[0][b][:, 0:1],
                    pS[sv(q0)][:, :], ALU.mult, ALU.add),
                    deps=[E[("wv", q0)], E[("mv1", q0)]]
                    + ([e_w1rep] if q0 == 0 else []))
                E[("vc", q0)] = A("V", lambda v, b=b: v.tensor_tensor(
                    vcvs[b][:, 0:D], v_s[b][:, :], cosa[b][:, :], ALU.mult),
                    deps=[E[("v", q0)], E[("cos", q0)],
                          E.get(("cumC", q0 - 2))])
                E[("vs", q0)] = A("V", lambda v, b=b: v.tensor_tensor(
                    vcvs[b][:, D:2 * D], v_s[b][:, :], sina[b][:, :], ALU.mult),
                    deps=[E[("v", q0)], E[("sin", q0)],
                          E.get(("cumS", q0 - 2))])

            if p == 0:
                stgo = [wstage, wstage2, out_s[0], out_s[1]]
                wo_sems = ["DWA", "DWB", "DWC", "DWD"]
                e_wo_cast = []
                for c in range(8):
                    if c >= 4:
                        dep = [e_wo_cast[c - 4]]
                    elif c <= 1:
                        dep = [e_wv_cast[6 + c]]
                    else:
                        dep = []
                    ld = A("D", lambda e, c=c: e.dma_start(
                        stgo[c % 4][:, :], wot_d[c * TL:(c + 1) * TL, :]),
                        deps=dep, sem=wo_sems[c % 4])
                    e_wo_cast.append(A("P", lambda g, c=c: g.tensor_scalar(
                        wot_s[:, c * D:(c + 1) * D], stgo[c % 4][:, :],
                        1.0, 1.0, ALU.mult, ALU.mult), deps=[ld]))
                e_wot = e_wo_cast[7]

            if p == 0:
                stgf = [ybf_s[0], ybf_s[1], ret_s[0], ret_s[1]]
                wf_sems = ["DWE", "DWF", "DWG", "DWH"]
                e_wf_cast = []
                for c in range(8):
                    if c >= 4:
                        dep = [e_wf_cast[c - 4]]
                    elif c == 0:
                        dep = [e_warm]
                    else:
                        dep = []
                    ld = A("D", lambda e, c=c: e.dma_start(
                        stgf[c % 4][:, :], wff_d[c * TL:(c + 1) * TL, :]),
                        deps=dep, sem=wf_sems[c % 4])
                    e_wf_cast.append(A("P", lambda g, c=c: g.tensor_scalar(
                        wffq_s[:, c * D:(c + 1) * D], stgf[c % 4][:, :],
                        rsk[:, 0:1], 1.0, ALU.mult, ALU.mult),
                        deps=[ld, e_rsk]))
                e_wffq = e_wf_cast[7]


            if 0 <= q1 < nt:
                b = q1 % 2
                svr = sa(q1)

                def retT_mms(t, b=b, svr=svr):
                    r = None
                    for j in range(NCHUNK):
                        r = t.transpose(pS_b[svr][:, j * TL:(j + 1) * TL],
                                        ret_s[b][:, j * TL:(j + 1) * TL],
                                        idm[:, :])
                    return r
                E[("retT", q1)] = A("T", retT_mms,
                                    deps=[E[("retadd", q1)],
                                          E.get(("v", q1 + 1))])
                E[("retTe", q1)] = A("S", lambda s, b=b, svr=svr: s.copy(
                    retT_s[b][:, :], pS_b[svr][:, 0:D]),
                    deps=[E[("retT", q1)], E.get(("wo", q1 - 2))])

            if 0 <= q3 < nt:
                b = q3 % 2
                svh = sv(q3)

                def h2T_mms(t, b=b, svh=svh):
                    r = None
                    for j in range(NCHUNK):
                        r = t.transpose(
                            pS_b[svh][:, j * TL:(j + 1) * TL],
                            h2_s[b][:, j * TL:(j + 1) * TL], idm[:, :])
                    return r
                E[("h2T", q3)] = A("T", h2T_mms,
                                   deps=[E[("h2", q3)], E.get(("v", q3 + 3)),
                                         E.get(("retTe", q3 + 2)),
                                         E.get(("quant", q3 - 2))])
                E[("quant", q3)] = A("S", lambda s, b=b, svh=svh: s.activation(
                    h2q_s[b][:, :], pS_b[svh][:, 0:D], AF.Identity,
                    scale=rsin[:, 0:1]),
                    deps=[E[("h2T", q3)], E.get(("ffn", q3 - 2)), e_rsin])

            if q0 < nt:
                b = q0 % 2

                def cumc_mms(t, b=b, q0=q0):
                    r = None
                    for j in range(2):
                        r = t.matmul(pCum[:, j * 512:(j + 1) * 512],
                                     usr_s[b][:, :],
                                     vcvs[b][:, j * 512:(j + 1) * 512],
                                     start=True, stop=(q0 == 0))
                    if q0 > 0:
                        for j in range(2):
                            r = t.matmul(pCum[:, j * 512:(j + 1) * 512],
                                         selm[0:64, :],
                                         cums[(q0 - 1) % 2][:, j * 512:(j + 1) * 512],
                                         start=False, stop=True)
                    return r
                E[("cumC", q0)] = A("T", cumc_mms,
                                    deps=[E[("vc", q0)], E[("usr", q0)],
                                          E.get(("t2", q0 - 1)),
                                          E.get(("cveS", q0 - 1)),
                                          E.get(("cveC", q0 - 1))]
                                    + ([e_w1rep] if q0 == 0 else []))

            if 0 <= q1 < nt:

                def wo_mms(t, b=q1 % 2, q1=q1):
                    r = None
                    for c in range(NCHUNK):
                        for j in range(2):
                            r = t.matmul(
                                pS[sa(q1)][:, j * 512:(j + 1) * 512],
                                retT_s[b][:, c * TL:(c + 1) * TL],
                                wot_s[:, c * D + j * 512:c * D + (j + 1) * 512],
                                start=(c == 0), stop=(c == NCHUNK - 1))
                    return r
                E[("wo", q1)] = A("T", wo_mms,
                                  deps=[E[("retTe", q1)],
                                        E.get(("quant", q1 - 2)),
                                        E.get(("t2act", q1 - 3)),
                                        E.get(("mv2", q1 - 3))]
                                  + ([e_wot] if q1 == 0 else []))

            if 0 <= q2 < nt:
                b = q2 % 2
                pslot = pS[sa(q2)]
                E[("bn2", q2)] = A("V", lambda v, b=b, pslot=pslot: v.bn_stats(
                    bnst[1][b][:, 0:6], pslot[:, 0:512]), deps=[E[("wo", q2)]])
                A("V", lambda v, b=b, pslot=pslot: v.bn_stats(
                    bnst[1][b][:, 6:12], pslot[:, 512:1024]),
                    deps=[E[("wo", q2)]])
                E[("mv2", q2)] = A("V", lambda v, b=b: v.bn_aggr(
                    mv[1][b][:, :], bnst[1][b][:, :]),
                    deps=[E.get(("vep2", q2 - 2))], tag="G")
                E[("vep2", q2)] = A("P", lambda g, b=b: g.tensor_scalar(
                    vep[1][b][:, :], mv[1][b][:, 1:2], 1.0, EPS, ALU.mult,
                    ALU.add), deps=[E[("mv2", q2)], E.get(("pow2", q2 - 2))])
                E[("pow2", q2)] = A("P", lambda g, b=b: g.tensor_tensor(
                    rcol[1][b][:, :], vep[1][b][:, :], mhalf[:, :], ALU.pow),
                    deps=[E[("vep2", q2)], E.get(("t2act", q2 - 2))])
                E[("nmr2", q2)] = A("P", lambda g, b=b: g.tensor_scalar(
                    nmr[1][b][:, :], rcol[1][b][:, :], mv[1][b][:, 0:1], -1.0,
                    ALU.mult, ALU.mult),
                    deps=[E[("pow2", q2)], E.get(("t2act", q2 - 2))])
                E[("t2act", q2)] = A("S", lambda s, b=b, pslot=pslot: s.activation(
                    t2s_s[0][:, :], pslot[:, :], AF.Identity,
                    scale=rcol[1][b][:, 0:1], bias=nmr[1][b][:, 0:1]),
                    deps=[E[("nmr2", q2)], E.get(("xp", q2 - 1))])

            if 0 <= q4 < nt:
                b = q4 % 2
                E[("bn4", q4)] = A("V", lambda v, b=b: v.bn_stats(
                    bnst[3][b][:, 0:6], ybf_s[b][:, 0:512]),
                    deps=[E[("deq", q4)]])
                A("V", lambda v, b=b: v.bn_stats(
                    bnst[3][b][:, 6:12], ybf_s[b][:, 512:1024]),
                    deps=[E[("deq", q4)]])
                E[("mv4", q4)] = A("V", lambda v, b=b: v.bn_aggr(
                    mv[3][b][:, :], bnst[3][b][:, :]),
                    deps=[E.get(("vep4", q4 - 2))], tag="G")

            if q0 < nt:
                b = q0 % 2
                E[("cveC", q0)] = A("S", lambda s, b=b: s.copy(
                    cums[b][:, 0:D], pCum[64:128, :]),
                    deps=[E[("cumC", q0)], E.get(("cumC", q0 - 1))])

            if 0 <= q4 < nt:
                b = q4 % 2
                E[("vep4", q4)] = A("P", lambda g, b=b: g.tensor_scalar(
                    vep[3][b][:, :], mv[3][b][:, 1:2], 1.0, EPS, ALU.mult,
                    ALU.add), deps=[E[("mv4", q4)], E.get(("pow4", q4 - 2))])
                E[("pow4", q4)] = A("P", lambda g, b=b: g.tensor_tensor(
                    rcol[3][b][:, :], vep[3][b][:, :], mhalf[:, :], ALU.pow),
                    deps=[E[("vep4", q4)], E.get(("ln4", q4 - 2))])
                E[("nmr4", q4)] = A("P", lambda g, b=b: g.tensor_scalar(
                    nmr[3][b][:, :], rcol[3][b][:, :], mv[3][b][:, 0:1], -1.0,
                    ALU.mult, ALU.mult),
                    deps=[E[("pow4", q4)], E.get(("ln4", q4 - 2))])
                E[("ln4", q4)] = A("S", lambda s, b=b: s.activation(
                    ln4_s[0][:, :], ybf_s[b][:, :], AF.Identity,
                    scale=rcol[3][b][:, 0:1], bias=nmr[3][b][:, 0:1]),
                    deps=[E[("nmr4", q4)], E.get(("outadd", q4 - 1))]
                    + ([e_wot] if q4 == 0 else []))

            for qR in ([] if p == 0 else [q0 + 1]):
                if qR >= nt:
                    continue
                b = qR % 2
                E[("bn1", qR)] = A("V", lambda v, b=b, qR=qR: v.bn_stats(
                    bnst[0][b][:, 0:6], xtok_s[qR % 4][:, 0:512]),
                    deps=[E[("xl", qR)]])
                A("V", lambda v, b=b, qR=qR: v.bn_stats(
                    bnst[0][b][:, 6:12], xtok_s[qR % 4][:, 512:1024]),
                    deps=[E[("xl", qR)]])
                E[("mv1", qR)] = A("V", lambda v, b=b: v.bn_aggr(
                    mv[0][b][:, :], bnst[0][b][:, :]),
                    deps=[E.get(("v", qR - 2)), E.get(("vep1", qR - 2))],
                    tag="G")
                E[("vep1", qR)] = A("P", lambda g, b=b: g.tensor_scalar(
                    vep[0][b][:, :], mv[0][b][:, 1:2], 1.0, EPS, ALU.mult,
                    ALU.add), deps=[E[("mv1", qR)], E.get(("pow1", qR - 2))])
                E[("pow1", qR)] = A("P", lambda g, b=b: g.tensor_tensor(
                    rcol[0][b][:, :], vep[0][b][:, :], mhalf[:, :], ALU.pow),
                    deps=[E[("vep1", qR)], E.get(("usr", qR - 2))])

            for qV in ([] if p == 0 else [q0 + 2]):
                if qV >= nt:
                    continue
                E[("u", qV)] = A("V", lambda v, qV=qV: v.tensor_scalar(
                    u_s[:, :], f2pi[:, :], tcol_s[qV][:, 0:1], None, ALU.mult),
                    deps=[E.get(("k1", qV - 1)), E[("tc", qV)]])
                E[("k1", qV)] = A("P", lambda g: g.tensor_scalar(
                    k_s[:, :], u_s[:, :], 1.0, MAGIC, ALU.mult, ALU.add),
                    deps=[E[("u", qV)], E.get(("w", qV - 1))])
                E[("k", qV)] = A("P", lambda g: g.tensor_scalar(
                    k_s[:, :], k_s[:, :], 1.0, -MAGIC, ALU.mult, ALU.add),
                    deps=[E[("k1", qV)]])

            for qT in [q0 + 4]:
                if qT >= nt:
                    continue
                E[("tc", qT)] = A("V", lambda v, qT=qT: v.tensor_copy(
                    tcol_s[qT][:, :], tcall_s[:, qT:qT + 1]), deps=[e_tcall])

            if 0 <= q3 < nt:
                b = q3 % 2

                def ffn_mms(t, b=b, q3=q3):
                    r = None
                    for m in range(4):
                        for j in range(2):
                            lhsT = bass.AP(h2q_s[b], m * 256,
                                           [[NCHUNK * D // 8, TL], [TL, 2], [1, TL]])
                            rhs = bass.AP(wffq_s, m * 2 * D + j * 512,
                                          [[NCHUNK * D, TL], [D, 2], [1, 512]])
                            r = t.matmul(
                                pS[sy(q3)][:, j * 512:(j + 1) * 512],
                                lhsT, rhs, start=(m == 0), stop=False,
                                perf_mode=mybir.MatmulPerfMode.DoubleRow)
                    for j in range(2):
                        r = t.matmul(
                            pS[sy(q3)][:, j * 512:(j + 1) * 512],
                            onesb_row[0:1, :],
                            bdiv[0:1, j * 512:(j + 1) * 512],
                            start=False, stop=True)
                    return r
                E[("ffn", q3)] = A("T", ffn_mms,
                                   deps=[E[("quant", q3)], e_wffq,
                                         E.get(("t2act", q3 + 1)),
                                         E.get(("mv2", q3 + 1))]
                                   + ([e_bdiv, e_ms[4]] if q3 == 0 else []))

            if 0 <= q2 < nt:
                b = q2 % 2
                E[("xp", q2)] = A("V", lambda v, b=b, q2=q2: v.tensor_tensor(
                    xp_s[q2 % 3][:, :], xtok_s[q2 % 4][:, :], t2s_s[0][:, :],
                    ALU.add), deps=[E[("t2act", q2)],
                                    E.get(("outadd", q2 - 3))])
                E[("bn3", q2)] = A("V", lambda v, b=b, q2=q2: v.bn_stats(
                    bnst[2][b][:, 0:6], xp_s[q2 % 3][:, 0:512]),
                    deps=[E[("xp", q2)]])
                A("V", lambda v, b=b, q2=q2: v.bn_stats(
                    bnst[2][b][:, 6:12], xp_s[q2 % 3][:, 512:1024]),
                    deps=[E[("xp", q2)]])
                E[("mv3", q2)] = A("V", lambda v, b=b: v.bn_aggr(
                    mv[2][b][:, :], bnst[2][b][:, :]),
                    deps=[E.get(("vep3", q2 - 2))], tag="G")
                E[("vep3", q2)] = A("P", lambda g, b=b: g.tensor_scalar(
                    vep[2][b][:, :], mv[2][b][:, 1:2], 1.0, EPS, ALU.mult,
                    ALU.add), deps=[E[("mv3", q2)], E.get(("pow3", q2 - 2))])
                E[("pow3", q2)] = A("P", lambda g, b=b: g.tensor_tensor(
                    rcol[2][b][:, :], vep[2][b][:, :], mhalf[:, :], ALU.pow),
                    deps=[E[("vep3", q2)], E.get(("h2", q2 - 2))])
                E[("nmr3", q2)] = A("P", lambda g, b=b: g.tensor_scalar(
                    nmr[2][b][:, :], rcol[2][b][:, :], mv[2][b][:, 0:1], -1.0,
                    ALU.mult, ALU.mult),
                    deps=[E[("pow3", q2)], E.get(("h2", q2 - 2))])
                E[("h2", q2)] = A("S", lambda s, b=b, q2=q2: s.activation(
                    h2_s[b][:, :], xp_s[q2 % 3][:, :], AF.Identity,
                    scale=rcol[2][b][:, 0:1], bias=nmr[2][b][:, 0:1]),
                    deps=[E[("nmr3", q2)], E.get(("h2T", q2 - 2))])

            if q0 < nt:
                b = q0 % 2
                E[("t1", q0)] = A("V", lambda v, b=b: v.tensor_tensor(
                    ta_s[b][:, :], pCum[:, :], cosa[b][:, :], ALU.mult),
                    deps=[E[("cumC", q0)], E[("cveC", q0)], E[("cos", q0)],
                          E.get(("retadd", q0 - 2))])

            for qR in ([] if p == 0 else [q0 + 1]):
                if qR >= nt:
                    continue
                b = qR % 2
                E[("usr", qR)] = A("P", lambda g, b=b: g.tensor_scalar(
                    usr_s[b][:, :], usmat[:, :], rcol[0][b][:, 0:1], 1.0,
                    ALU.mult, ALU.mult),
                    deps=[E[("pow1", qR)], E.get(("cumS", qR - 2))])

            for qV in ([] if p == 0 else [q0 + 2]):
                if qV >= nt:
                    continue
                b = qV % 2
                E[("w", qV)] = A("V", lambda v, b=b: v.scalar_tensor_tensor(
                    w_s[b][:, :], k_s[:, :], -1.0, u_s[:, :], ALU.mult, ALU.add),
                    deps=[E[("k", qV)], E.get(("sin", qV - 2))])
                E[("wc", qV)] = A("V", lambda v, b=b: v.scalar_tensor_tensor(
                    wc_s[b][:, :], w_s[b][:, :], 0.25, w_s[b][:, :],
                    ALU.is_ge, ALU.subtract),
                    deps=[E.get(("cos", qV - 2))])

            for qS in ([] if p == 0 else [q0 + 1]):
                if qS >= nt:
                    continue
                b = qS % 2
                E[("sin", qS)] = A("S", lambda s, b=b: s.activation(
                    sina[b][:, :], w_s[b][:, :], AF.Sin, scale=TWO_PI),
                    deps=[E[("w", qS)], E.get(("t2", qS - 2)),
                          E.get(("vs", qS - 2))]
                    + ([e_ms[2]] if qS == 0 else []))
                E[("cos", qS)] = A("S", lambda s, b=b: s.activation(
                    cosa[b][:, :], wc_s[b][:, :], AF.Sin, scale=-TWO_PI,
                    bias=piby2[:, :]),
                    deps=[E[("wc", qS)], E.get(("t1", qS - 2)),
                          E.get(("vc", qS - 2))]
                    + ([e_ms[2]] if qS == 0 else []))

            if q0 < nt:
                b = q0 % 2

                def cums_mms(t, b=b, q0=q0):
                    r = None
                    for j in range(2):
                        r = t.matmul(pCum[:, j * 512:(j + 1) * 512],
                                     usr_s[b][:, :],
                                     vcvs[b][:, D + j * 512:D + (j + 1) * 512],
                                     start=True, stop=(q0 == 0))
                    if q0 > 0:
                        for j in range(2):
                            r = t.matmul(pCum[:, j * 512:(j + 1) * 512],
                                         selm[0:64, :],
                                         cums[(q0 - 1) % 2][:, D + j * 512:D + (j + 1) * 512],
                                         start=False, stop=True)
                    return r
                E[("cumS", q0)] = A("T", cums_mms,
                                    deps=[E[("vs", q0)], E[("t1", q0)],
                                          E[("cveC", q0)]])
                E[("t2", q0)] = A("V", lambda v, b=b: v.tensor_tensor(
                    tb_s[b][:, :], pCum[:, :], sina[b][:, :], ALU.mult),
                    deps=[E[("cumS", q0)], E[("sin", q0)],
                          E.get(("retadd", q0 - 2))])
                E[("retadd", q0)] = A("V", lambda v, b=b: v.tensor_tensor(
                    ret_s[b][:, :], ta_s[b][:, :], tb_s[b][:, :], ALU.add),
                    deps=[E[("t1", q0)], E[("t2", q0)],
                          E.get(("retT", q0 - 2))]
                    + ([e_wf_cast[6 + q0]] if q0 <= 1 else []))
                E[("cveS", q0)] = A("S", lambda s, b=b: s.copy(
                    cums[b][:, D:2 * D], pCum[64:128, :]),
                    deps=[E[("cumS", q0)], E[("t2", q0)],
                          E.get(("cumS", q0 - 1))])

            for qW in ([1] if p == 0 else [q0 + 1]):
                if qW >= nt:
                    continue

                def wv_mms(t, qW=qW):
                    r = None
                    for c in range(NCHUNK):
                        for j in range(2):
                            r = t.matmul(
                                pS[sv(qW)][:, j * 512:(j + 1) * 512],
                                xftb_s[qW % 2][:, c * TL:(c + 1) * TL],
                                wvt_s[:, c * D + j * 512:c * D + (j + 1) * 512],
                                start=(c == 0), stop=(c == NCHUNK - 1))
                    return r
                E[("wv", qW)] = A("T", wv_mms,
                                  deps=[E[("fcast", qW)], e_wvt,
                                        E.get(("deq", qW - 5)),
                                        E.get(("t2act", qW - 4)),
                                        E.get(("mv2", qW - 4))]
                                  + ([e_f2pi] if qW <= 1 else []))
            if 0 <= q4 < nt:
                b = q4 % 2
                E[("outadd", q4)] = A("P", lambda g, b=b, q4=q4: g.tensor_tensor(
                    out_s[b][:, :], xp_s[q4 % 3][:, :], ln4_s[0][:, :], ALU.add),
                    deps=[E[("ln4", q4)], E[("xp", q4)],
                          E.get(("ost", q4 - 2))]
                    + ([e_wo_cast[6 + q4]] if q4 <= 1 else []))
                E[("ost", q4)] = A("P", lambda g, b=b, q4=q4: g.dma_start(
                    out_d[q4 * TL:(q4 + 1) * TL, :], out_s[b][:, :]),
                    deps=[E[("outadd", q4)]], sem=f"DO{q4 % 2}", inc=16)

        with nc.Block() as block:

            @block.sync
            def _(e):
                sch.emit("D", e, sems, cut)
                for sname, cval in sorted(sch.dma_counts_upto(cut).items()):
                    e.wait_ge(sems[sname], cval)

            @block.vector
            def _(v):
                sch.emit("V", v, sems, cut, selfsync=("VSP" if racecheck else selfsync))

            @block.scalar
            def _(s):
                sch.emit("S", s, sems, cut, selfsync=("VSP" if racecheck else selfsync))

            @block.gpsimd
            def _(g):
                sch.emit("P", g, sems, cut, selfsync=("VSP" if racecheck else selfsync))

            @block.tensor
            def _(t):
                sch.emit("T", t, sems, cut, selfsync=("VSP" if racecheck else selfsync))

    return nc


def _host_layout(x_b, Wv, Wo, Wff, freqs, s_in, s_k, bias, nt):
    T = nt * TL
    xtok = np.ascontiguousarray(x_b)
    xft = np.ascontiguousarray(
        x_b.T.reshape(NCHUNK, TL, nt, TL).transpose(2, 1, 0, 3)
        .reshape(nt * TL, D))
    wvt = np.ascontiguousarray(Wv.T)
    wot = np.ascontiguousarray(Wo.T)
    wff = np.ascontiguousarray(Wff)
    tcols = (np.arange(TL, dtype=np.float32)[:, None]
             + TL * np.arange(nt, dtype=np.float32)[None, :])
    us = np.triu(np.ones((TL, TL), dtype=np.float32)).astype(ml_dtypes.bfloat16)
    idm = np.eye(TL, dtype=np.float32).astype(ml_dtypes.bfloat16)
    sel = np.zeros((TL, TL), dtype=np.float32)
    sel[63, :] = 1.0
    sel = sel.astype(ml_dtypes.bfloat16)
    return {
        "xtok": xtok, "xft": xft, "wvt": wvt, "wot": wot, "wff": wff,
        "freqs": np.ascontiguousarray(freqs.reshape(1, D)),
        "tcols": np.ascontiguousarray(tcols),
        "usmat": us, "idmat": idm, "selmat": sel,
        "s_in": s_in.reshape(1, 1), "s_k": s_k.reshape(1, 1),
        "biasv": np.ascontiguousarray(bias.reshape(1, D)),
    }


def kernel(L0_attn_x, L0_attn_Wk, L0_attn_Wv, L0_attn_Wo, L0_attn_freqs,
           L0_ffn_kernel, L0_ffn_input_scale, L0_ffn_kernel_scale, L0_ffn_bias):
    x = np.asarray(L0_attn_x)
    assert np.asarray(L0_attn_Wk).shape == np.asarray(L0_attn_Wv).shape
    B, T, Dd = x.shape
    nt = T // TL
    nc = build(nt)
    in_maps = []
    for b in range(B):
        in_maps.append(_host_layout(
            x[b], np.asarray(L0_attn_Wv), np.asarray(L0_attn_Wo),
            np.asarray(L0_ffn_kernel), np.asarray(L0_attn_freqs),
            np.asarray(L0_ffn_input_scale), np.asarray(L0_ffn_kernel_scale),
            np.asarray(L0_ffn_bias), nt))
    res = run_bass_kernel_spmd(nc, in_maps, core_ids=list(range(B)))
    out = np.stack([np.asarray(r["out"]) for r in res.results], axis=0)
    return out.astype(np.float32)


# version 62
# speedup vs baseline: 1.2089x; 1.0088x over previous
import numpy as np
import ml_dtypes
import concourse.bass as bass
import concourse.mybir as mybir
from concourse.bass_utils import run_bass_kernel_spmd
from contextlib import ExitStack

F32 = mybir.dt.float32
BF16 = mybir.dt.bfloat16
F8 = mybir.dt.float8e4
I32 = mybir.dt.int32
AF = mybir.ActivationFunctionType
ALU = mybir.AluOpType

MAGIC = 12582912.0
TWO_PI = 6.283185307179586
EPS = 1e-5
D = 1024
TL = 128
NCHUNK = 8

ENGSEM = {"T": "semT", "V": "semV", "S": "semS", "P": "semP"}


class Sched:

    def __init__(self):
        self.q = {k: [] for k in ["T", "V", "S", "P", "D"]}
        self.counts = {}
        self.sem_names = set()
        self.seq = 0

    def add(self, eng, fn, deps=(), sem=None, inc=None, tag=""):
        if sem is None:
            sem = ENGSEM[eng]
        if inc is None:
            inc = 16 if eng == "D" else 1
        self.sem_names.add(sem)
        self.counts[sem] = self.counts.get(sem, 0) + inc
        ev = (sem, self.counts[sem])
        deps = [d for d in deps if d is not None]
        self.seq += 1
        self.q[eng].append((deps, fn, sem, inc, self.seq, tag))
        return ev

    def dma_counts_upto(self, cut):
        out = {}
        engsems = set(ENGSEM.values())
        for qk in self.q:
            for (deps, fn, sem, inc, seq, tag) in self.q[qk]:
                if sem in engsems:
                    continue
                if cut is None or seq <= cut:
                    out[sem] = out.get(sem, 0) + inc
        return out

    def emit(self, eng_key, engine, sems, cut=None, selfsync=""):
        floors = {}
        own = ENGSEM.get(eng_key)
        own_count = 0
        for deps, fn, sem, inc, seq, tag in self.q[eng_key]:
            if cut is not None and seq > cut:
                continue
            need = {}
            for (s, v) in deps:
                if v > need.get(s, 0):
                    need[s] = v
            ssync = (eng_key in selfsync) or (tag and tag in selfsync)
            if ssync and own is not None \
                    and own_count > 0 and own_count > need.get(own, 0):
                need[own] = own_count
            for s, v in sorted(need.items()):
                if v > floors.get(s, 0):
                    engine.wait_ge(sems[s], v)
                    floors[s] = v
            instr = fn(engine)
            instr.then_inc(sems[sem], inc)
            if sem == own:
                own_count += inc


def build(nt=32, cut=None, racecheck=False, selfsync="G"):
    nc = bass.Bass("TRN2", target_bir_lowering=False, debug=False,
                   detect_race_conditions=racecheck)
    T = nt * TL

    xtok_d = nc.dram_tensor("xtok", [T, D], F32, kind="ExternalInput")
    xft_d = nc.dram_tensor("xft", [nt * TL, D], F32, kind="ExternalInput")
    wvt_d = nc.dram_tensor("wvt", [NCHUNK * TL, D], F32, kind="ExternalInput")
    wot_d = nc.dram_tensor("wot", [NCHUNK * TL, D], F32, kind="ExternalInput")
    wff_d = nc.dram_tensor("wff", [NCHUNK * TL, D], BF16, kind="ExternalInput")
    freqs_d = nc.dram_tensor("freqs", [1, D], F32, kind="ExternalInput")
    tcols_d = nc.dram_tensor("tcols", [TL, nt], F32, kind="ExternalInput")
    us_d = nc.dram_tensor("usmat", [TL, TL], BF16, kind="ExternalInput")
    id_d = nc.dram_tensor("idmat", [TL, TL], BF16, kind="ExternalInput")
    sel_d = nc.dram_tensor("selmat", [TL, TL], BF16, kind="ExternalInput")
    sin_d = nc.dram_tensor("s_in", [1, 1], F32, kind="ExternalInput")
    sk_d = nc.dram_tensor("s_k", [1, 1], F32, kind="ExternalInput")
    bias_d = nc.dram_tensor("biasv", [1, D], BF16, kind="ExternalInput")
    out_d = nc.dram_tensor("out", [T, D], F32, kind="ExternalOutput")

    sch = Sched()
    with ExitStack() as st:
        def sb(name, shape, dt):
            return st.enter_context(nc.sbuf_tensor(name, shape, dt))

        sems = {}
        for s in ["semT", "semV", "semS", "semP", "DW", "DWS", "DWT",
                  "DWU", "DWV", "DW4", "DW5", "DW6", "DW7",
                  "DWA", "DWB", "DWC", "DWD", "DWE", "DWF", "DWG", "DWH",
                  "DX0", "DX1", "DX2", "DX3", "DF0", "DF1", "DO0", "DO1"]:
            sems[s] = st.enter_context(nc.semaphore(s))

        wstage = sb("wstage", [TL, 1024], F32)
        tcall_s = sb("tcall", [TL, nt], F32)
        wvt_s = sb("wvt_s", [TL, NCHUNK * D], BF16)
        wot_s = sb("wot_s", [TL, NCHUNK * D], BF16)
        wffq_s = sb("wffq_s", [TL, NCHUNK * D], F8)
        f2pi = sb("f2pi", [TL, D], F32)
        frow = sb("frow", [1, D], F32)
        tcol_s = [sb(f"tc{i}", [TL, 1], F32) for i in range(nt)]
        usmat = sb("usmat_s", [TL, TL], BF16)
        idm = sb("idm", [TL, TL], BF16)
        selm = sb("selm", [TL, TL], BF16)
        onesf = sb("onesf", [1, TL], F32)
        onesb_row = sb("onesb_row", [1, TL], BF16)
        onescol_b = sb("onescol_b", [TL, 1], BF16)
        w1n_rep = sb("w1n_rep", [TL, D], BF16)
        w1row = sb("w1row", [1, D], BF16)
        bdiv = sb("bdiv", [1, D], BF16)
        rr = sb("rr", [1, 1], F32)
        brow = sb("brow", [1, D], BF16)
        sincol = sb("sincol", [TL, 1], F32)
        skcol = sb("skcol", [TL, 1], F32)
        rsin = sb("rsin", [TL, 1], F32)
        rsk = sb("rsk", [TL, 1], F32)
        sisk = sb("sisk", [TL, 1], F32)
        piby2 = sb("piby2", [TL, 1], F32)
        mhalf = sb("mhalf", [TL, 1], F32)
        srow = sb("srow", [1, 2], F32)

        xtok_s = [sb(f"xtok{i}", [TL, D], F32) for i in range(4)]
        xftf_s = [sb(f"xftf{i}", [TL, D], F32) for i in range(2)]
        xftb_s = [sb(f"xftb{i}", [TL, D], BF16) for i in range(2)]
        u_s = sb("u_s", [TL, D], F32)
        k_s = sb("k_s", [TL, D], F32)
        w_s = [sb(f"w{i}", [TL, D], F32) for i in range(2)]
        wc_s = [sb(f"wc{i}", [TL, D], F32) for i in range(2)]
        sina = [sb(f"sina{i}", [TL, D], BF16) for i in range(2)]
        cosa = [sb(f"cosa{i}", [TL, D], BF16) for i in range(2)]
        usr_s = [sb(f"usr{i}", [TL, TL], BF16) for i in range(2)]
        v_s = [sb(f"v{i}", [TL, D], BF16) for i in range(2)]
        vcvs = [sb(f"vcvs{i}", [TL, 2 * D], BF16) for i in range(2)]
        ta_s = [sb(f"ta{i}", [TL, D], BF16) for i in range(2)]
        tb_s = [sb(f"tb{i}", [TL, D], BF16) for i in range(2)]
        ret_s = [sb(f"ret{i}", [TL, D], BF16) for i in range(2)]
        retT_s = [sb(f"retT{i}", [TL, D], BF16) for i in range(2)]
        cums = [sb(f"cums{i}", [64, 2 * D], BF16) for i in range(2)]
        t2s_s = [sb("t2s0", [TL, D], F32)]
        xp_s = [sb(f"xp{i}", [TL, D], F32) for i in range(3)]
        h2_s = [sb(f"h2_{i}", [TL, D], BF16) for i in range(2)]
        h2q_s = [sb(f"h2q{i}", [TL, D], F8) for i in range(2)]
        ybf_s = [sb(f"ybf{i}", [TL, D], BF16) for i in range(2)]
        ln4_s = [sb("ln4_0", [TL, D], F32)]
        wstage2 = ln4_s[0]
        out_s = [sb(f"outs{i}", [TL, D], F32) for i in range(2)]
        bnst = [[sb(f"bnst{ln}_{i}", [TL, 12], F32) for i in range(2)]
                for ln in range(4)]
        mv = [[sb(f"mv{ln}_{i}", [TL, 2], F32) for i in range(2)]
              for ln in range(4)]
        vep = [[sb(f"vep{ln}_{i}", [TL, 1], F32) for i in range(2)]
               for ln in range(4)]
        rcol = [[sb(f"r{ln}_{i}", [TL, 1], F32) for i in range(2)]
                for ln in range(4)]
        nmr = [[sb(f"nmr{ln}_{i}", [TL, 1], F32) for i in range(2)]
               for ln in range(4)]

        pS = [st.enter_context(nc.psum_tensor("pA", [TL, D], F32)),
              st.enter_context(nc.psum_tensor("pB", [TL, D], F32)),
              st.enter_context(nc.psum_tensor("pC", [TL, D], F32))]
        pCum = st.enter_context(nc.psum_tensor("pCum", [TL, D], F32))
        pS_b = [p.ap().bitcast(BF16) for p in pS]

        A = sch.add
        pre_loads = []
        for dst, src in [(usmat, us_d), (idm, id_d),
                         (selm, sel_d), (frow, freqs_d), (brow, bias_d)]:
            pre_loads.append(A("D", lambda e, d=dst, s=src: e.dma_start(d[:, :], s[:, :]),
                               deps=(), sem="DW"))
        e_tcall = A("D", lambda e: e.dma_start(tcall_s[:, :], tcols_d[:, :]),
                    deps=(), sem="DW")
        e_sin_l = A("D", lambda e: e.dma_start(srow[0:1, 0:1], sin_d[:, :]), sem="DW")
        e_sk_l = A("D", lambda e: e.dma_start(srow[0:1, 1:2], sk_d[:, :]), sem="DW")
        pre_loads += [e_tcall, e_sin_l, e_sk_l]
        e_dw_all = [e_sk_l]

        e_ms = [
            A("V", lambda v: v.memset(onesf[:, :], 1.0)),
            A("V", lambda v: v.memset(onescol_b[:, :], 1.0)),
            A("V", lambda v: v.memset(piby2[:, :], TWO_PI / 4.0)),
            A("P", lambda g: g.memset(mhalf[:, :], -0.5)),
            A("V", lambda v: v.memset(onesb_row[:, :], 1.0)),
        ]

        E = {}
        E[("xl", 0)] = A("D", lambda e: e.dma_start(
            xtok_s[0][:, :], xtok_d[0:TL, :]), deps=(), sem="DX0")
        E[("fl", 0)] = A("D", lambda e: e.dma_start(
            xftf_s[0][:, :], xft_d[0:TL, :]), deps=(), sem="DF0")
        E[("xl", 1)] = A("D", lambda e: e.dma_start(
            xtok_s[1][:, :], xtok_d[TL:2 * TL, :]), deps=(), sem="DX1")
        E[("fl", 1)] = A("D", lambda e: e.dma_start(
            xftf_s[1][:, :], xft_d[TL:2 * TL, :]), deps=(), sem="DF1")
        E[("fcast", 0)] = A("P", lambda g: g.tensor_scalar(
            xftb_s[0][:, :], xftf_s[0][:, :], 1.0, 1.0, ALU.mult,
            ALU.mult), deps=[E[("fl", 0)]])
        E[("fcast", 1)] = A("P", lambda g: g.tensor_scalar(
            xftb_s[1][:, :], xftf_s[1][:, :], 1.0, 1.0, ALU.mult,
            ALU.mult), deps=[E[("fl", 1)]])

        e_warm = A("S", lambda s_: s_.activation(
            ybf_s[0][0:1, 0:1], piby2[0:1, 0:1], AF.Sin, scale=0.0),
            deps=[e_ms[2]])

        e_sbcast = A("T", lambda t: t.matmul(
            pS[0][:, 0:2], onesf[:, :], srow[0:1, 0:2], start=True, stop=True),
            deps=e_dw_all + [e_ms[0]])
        e_scols = A("V", lambda v: v.tensor_copy(sincol[:, :], pS[0][:, 0:1]),
                    deps=[e_sbcast])
        e_scols2 = A("V", lambda v: v.tensor_copy(skcol[:, :], pS[0][:, 1:2]),
                     deps=[e_sbcast])
        e_rsin = A("V", lambda v: v.reciprocal(rsin[:, :], sincol[:, :]),
                   deps=[e_scols])
        e_rsk = A("V", lambda v: v.reciprocal(rsk[:, :], skcol[:, :]),
                  deps=[e_scols2])
        e_sisk = A("V", lambda v: v.tensor_tensor(sisk[:, :], sincol[:, :],
                                                  skcol[:, :], ALU.mult),
                   deps=[e_scols, e_scols2])

        e_fb = [A("T", lambda t, j=j: t.matmul(
            pS[1][:, j * 512:(j + 1) * 512], onesf[:, :],
            frow[0:1, j * 512:(j + 1) * 512], start=True, stop=True),
            deps=e_dw_all + [e_ms[0], e_scols, e_scols2]) for j in range(2)]
        e_f2pi = A("V", lambda v: v.tensor_scalar(
            f2pi[:, :], pS[1][:, :], 1.0 / TWO_PI, None, ALU.mult), deps=e_fb)

        e_rr = A("V", lambda v: v.tensor_tensor(
            rr[0:1, :], rsin[0:1, 0:1], rsk[0:1, 0:1], ALU.mult),
            deps=[e_rsin, e_rsk])
        e_bdiv = A("V", lambda v: v.tensor_scalar(
            bdiv[0:1, :], brow[0:1, :], rr[0:1, 0:1], None, ALU.mult),
            deps=[e_rr] + e_dw_all)

        wstage_bf = wstage.ap().bitcast(BF16)
        wstage2_bf = wstage2.ap().bitcast(BF16)
        stg = [wstage, wstage2]
        stg_b = [wstage_bf, wstage2_bf]
        e_wv_cast = []
        e_wv0 = None
        for c in range(8):
            dep = [e_wv_cast[c - 2]] if c >= 2 else []
            qsems = (["DWS", "DWT", "DWU", "DWV"] if c % 2 == 0
                     else ["DW4", "DW5", "DW6", "DW7"])
            lds = []
            for h in range(4):
                lds.append(A("D", lambda e, c=c, h=h: e.dma_start(
                    stg[c % 2][:, h * 256:(h + 1) * 256],
                    wvt_d[c * TL:(c + 1) * TL, h * 256:(h + 1) * 256]),
                    deps=dep, sem=qsems[h]))
            e_wv_cast.append(A("V", lambda v, c=c: v.tensor_scalar(
                wvt_s[:, c * D:(c + 1) * D], stg[c % 2][:, :],
                1.0, None, ALU.mult), deps=lds))

            def w1c_mms(t, c=c):
                r = None
                for j in range(2):
                    r = t.matmul(pCum[0:1, j * 512:(j + 1) * 512],
                                 onescol_b[:, :],
                                 wvt_s[:, c * D + j * 512:c * D + (j + 1) * 512],
                                 start=(c == 0), stop=(c == NCHUNK - 1))
                return r
            e_w1mm = A("T", w1c_mms,
                       deps=[e_wv_cast[c]] + ([e_ms[1]] if c == 0 else []))

            def wv0c_mms(t, c=c):
                r = None
                for j in range(2):
                    r = t.matmul(pS[0][:, j * 512:(j + 1) * 512],
                                 xftb_s[0][:, c * TL:(c + 1) * TL],
                                 wvt_s[:, c * D + j * 512:c * D + (j + 1) * 512],
                                 start=(c == 0), stop=(c == NCHUNK - 1))
                return r
            e_wv0 = A("T", wv0c_mms,
                      deps=[e_wv_cast[c]]
                      + ([E[("fcast", 0)], e_scols, e_scols2] if c == 0 else []))
        e_wvt = e_wv_cast[7]
        E[("wv", 0)] = e_wv0

        e_w1row = A("V", lambda v: v.tensor_scalar(
            w1row[0:1, :], pCum[0:1, 0:D], -1.0, None, ALU.mult), deps=[e_w1mm])
        e_w1b = [A("T", lambda t, j=j: t.matmul(
            pCum[:, j * 512:(j + 1) * 512], onesb_row[0:1, :],
            w1row[0:1, j * 512:(j + 1) * 512], start=True, stop=True),
            deps=[e_w1row, e_ms[4]]) for j in range(2)]
        e_w1rep = A("V", lambda v: v.tensor_scalar(
            w1n_rep[:, :], pCum[:, :], 1.0, None, ALU.mult), deps=e_w1b)

        for i in range(min(4, nt)):
            E[("tc", i)] = A("V", lambda v, i=i: v.tensor_copy(
                tcol_s[i][:, :], tcall_s[:, i:i + 1]), deps=[e_tcall])
        e_wot = None
        e_wffq = None


        def sv(q):
            return q % 3

        def sa(q):
            return (q + 1) % 3

        def sy(q):
            return (q + 2) % 3

        for p in range(nt + 5):
            q0, q1, q2, q3, q4 = p, p - 1, p - 2, p - 3, p - 4

            if 2 <= p + 1 < nt:
                qq = p + 1
                E[("xl", qq)] = A("D", lambda e, qq=qq: e.dma_start(
                    xtok_s[qq % 4][:, :], xtok_d[qq * TL:(qq + 1) * TL, :]),
                    deps=[E.get(("xp", qq - 4))],
                    sem=f"DX{qq % 4}")
                E[("fl", qq)] = A("D", lambda e, qq=qq: e.dma_start(
                    xftf_s[qq % 2][:, :], xft_d[qq * TL:(qq + 1) * TL, :]),
                    deps=[E.get(("fcast", qq - 2))],
                    sem=f"DF{qq % 2}")
                E[("fcast", qq)] = A("P", lambda g, qq=qq: g.tensor_scalar(
                    xftb_s[qq % 2][:, :], xftf_s[qq % 2][:, :], 1.0, 1.0,
                    ALU.mult, ALU.mult),
                    deps=[E[("fl", qq)], E.get(("wv", qq - 2))])

            if 0 <= q4 < nt:
                b = q4 % 2
                E[("deq", q4)] = A("S", lambda s, b=b, q4=q4: s.activation(
                    ybf_s[b][:, :], pS[sy(q4)][:, :], AF.Identity,
                    scale=sisk[:, 0:1]),
                    deps=[E[("ffn", q4)], E.get(("ln4", q4 - 2)),
                          E.get(("mv4", q4 - 2)), e_sisk]
                    + ([e_wf_cast[4 + q4]] if q4 <= 1 else []))

            for qR in ([0, 1] if p == 0 else []):
                if qR >= nt:
                    continue
                b = qR % 2
                E[("bn1", qR)] = A("V", lambda v, b=b, qR=qR: v.bn_stats(
                    bnst[0][b][:, 0:6], xtok_s[qR % 4][:, 0:512]),
                    deps=[E[("xl", qR)]])
                A("V", lambda v, b=b, qR=qR: v.bn_stats(
                    bnst[0][b][:, 6:12], xtok_s[qR % 4][:, 512:1024]),
                    deps=[E[("xl", qR)]])
                E[("mv1", qR)] = A("V", lambda v, b=b: v.bn_aggr(
                    mv[0][b][:, :], bnst[0][b][:, :]),
                    deps=[E.get(("v", qR - 2)), E.get(("vep1", qR - 2))],
                    tag="G")
                E[("vep1", qR)] = A("P", lambda g, b=b: g.tensor_scalar(
                    vep[0][b][:, :], mv[0][b][:, 1:2], 1.0, EPS, ALU.mult,
                    ALU.add), deps=[E[("mv1", qR)], E.get(("pow1", qR - 2))])
                E[("pow1", qR)] = A("P", lambda g, b=b: g.tensor_tensor(
                    rcol[0][b][:, :], vep[0][b][:, :], mhalf[:, :], ALU.pow),
                    deps=[E[("vep1", qR)], E.get(("usr", qR - 2))])
            for qV in ([0, 1] if p == 0 else []):
                if qV >= nt:
                    continue
                E[("u", qV)] = A("V", lambda v, qV=qV: v.tensor_scalar(
                    u_s[:, :], f2pi[:, :], tcol_s[qV][:, 0:1], None, ALU.mult),
                    deps=([e_f2pi] if qV == 0 else []) + [E.get(("k1", qV - 1)),
                          E[("tc", qV)]])
                E[("k1", qV)] = A("P", lambda g: g.tensor_scalar(
                    k_s[:, :], u_s[:, :], 1.0, MAGIC, ALU.mult, ALU.add),
                    deps=[E[("u", qV)], E.get(("w", qV - 1))])
                E[("k", qV)] = A("P", lambda g: g.tensor_scalar(
                    k_s[:, :], k_s[:, :], 1.0, -MAGIC, ALU.mult, ALU.add),
                    deps=[E[("k1", qV)]])
                b = qV % 2
                E[("w", qV)] = A("V", lambda v, b=b: v.scalar_tensor_tensor(
                    w_s[b][:, :], k_s[:, :], -1.0, u_s[:, :], ALU.mult, ALU.add),
                    deps=[E[("k", qV)], E.get(("sin", qV - 2))])
                E[("wc", qV)] = A("V", lambda v, b=b: v.scalar_tensor_tensor(
                    wc_s[b][:, :], w_s[b][:, :], 0.25, w_s[b][:, :],
                    ALU.is_ge, ALU.subtract),
                    deps=[E.get(("cos", qV - 2))])

            for qS in ([0, 1] if p == 0 else []):
                if qS >= nt:
                    continue
                b = qS % 2
                E[("sin", qS)] = A("S", lambda s, b=b: s.activation(
                    sina[b][:, :], w_s[b][:, :], AF.Sin, scale=TWO_PI),
                    deps=[E[("w", qS)]] + [e_ms[2]])
                E[("cos", qS)] = A("S", lambda s, b=b: s.activation(
                    cosa[b][:, :], wc_s[b][:, :], AF.Sin, scale=-TWO_PI,
                    bias=piby2[:, :]),
                    deps=[E[("wc", qS)]] + [e_ms[2]])

            for qV in ([2] if p == 0 else []):
                if qV >= nt:
                    continue
                E[("u", qV)] = A("V", lambda v, qV=qV: v.tensor_scalar(
                    u_s[:, :], f2pi[:, :], tcol_s[qV][:, 0:1], None, ALU.mult),
                    deps=[E.get(("k1", qV - 1)), E[("tc", qV)]])
                E[("k1", qV)] = A("P", lambda g: g.tensor_scalar(
                    k_s[:, :], u_s[:, :], 1.0, MAGIC, ALU.mult, ALU.add),
                    deps=[E[("u", qV)], E.get(("w", qV - 1))])
                E[("k", qV)] = A("P", lambda g: g.tensor_scalar(
                    k_s[:, :], k_s[:, :], 1.0, -MAGIC, ALU.mult, ALU.add),
                    deps=[E[("k1", qV)]])
                b = qV % 2
                E[("w", qV)] = A("V", lambda v, b=b: v.scalar_tensor_tensor(
                    w_s[b][:, :], k_s[:, :], -1.0, u_s[:, :], ALU.mult, ALU.add),
                    deps=[E[("k", qV)], E.get(("sin", qV - 2))])
                E[("wc", qV)] = A("V", lambda v, b=b: v.scalar_tensor_tensor(
                    wc_s[b][:, :], w_s[b][:, :], 0.25, w_s[b][:, :],
                    ALU.is_ge, ALU.subtract),
                    deps=[E.get(("cos", qV - 2))])

            for qR in ([0, 1] if p == 0 else []):
                if qR >= nt:
                    continue
                b = qR % 2
                E[("usr", qR)] = A("P", lambda g, b=b: g.tensor_scalar(
                    usr_s[b][:, :], usmat[:, :], rcol[0][b][:, 0:1], 1.0,
                    ALU.mult, ALU.mult),
                    deps=[E[("pow1", qR)]])

            if q0 < nt:
                b = q0 % 2
                E[("v", q0)] = A("V", lambda v, b=b, q0=q0: v.scalar_tensor_tensor(
                    v_s[b][:, :], w1n_rep[:, :], mv[0][b][:, 0:1],
                    pS[sv(q0)][:, :], ALU.mult, ALU.add),
                    deps=[E[("wv", q0)], E[("mv1", q0)]]
                    + ([e_w1rep] if q0 == 0 else []))
                E[("vc", q0)] = A("V", lambda v, b=b: v.tensor_tensor(
                    vcvs[b][:, 0:D], v_s[b][:, :], cosa[b][:, :], ALU.mult),
                    deps=[E[("v", q0)], E[("cos", q0)],
                          E.get(("cumC", q0 - 2))])
                E[("vs", q0)] = A("V", lambda v, b=b: v.tensor_tensor(
                    vcvs[b][:, D:2 * D], v_s[b][:, :], sina[b][:, :], ALU.mult),
                    deps=[E[("v", q0)], E[("sin", q0)],
                          E.get(("cumS", q0 - 2))])

            if p == 0:
                stgo = [wstage, wstage2, out_s[0], out_s[1]]
                wo_sems = ["DWA", "DWB", "DWC", "DWD"]
                e_wo_cast = []
                for c in range(8):
                    if c >= 4:
                        dep = [e_wo_cast[c - 4]]
                    elif c <= 1:
                        dep = [e_wv_cast[6 + c]]
                    else:
                        dep = [e_wvt]
                    ld = A("D", lambda e, c=c: e.dma_start(
                        stgo[c % 4][:, :], wot_d[c * TL:(c + 1) * TL, :]),
                        deps=dep, sem=wo_sems[c % 4])
                    e_wo_cast.append(A("P", lambda g, c=c: g.tensor_scalar(
                        wot_s[:, c * D:(c + 1) * D], stgo[c % 4][:, :],
                        1.0, 1.0, ALU.mult, ALU.mult), deps=[ld]))
                e_wot = e_wo_cast[7]

            if p == 0:
                stgf = [ybf_s[0], ybf_s[1], ret_s[0], ret_s[1]]
                wf_sems = ["DWE", "DWF", "DWG", "DWH"]
                e_wf_cast = []
                for c in range(8):
                    if c >= 4:
                        dep = [e_wf_cast[c - 4]]
                    elif c == 0:
                        dep = [e_warm, e_wvt]
                    else:
                        dep = [e_wvt]
                    ld = A("D", lambda e, c=c: e.dma_start(
                        stgf[c % 4][:, :], wff_d[c * TL:(c + 1) * TL, :]),
                        deps=dep, sem=wf_sems[c % 4])
                    e_wf_cast.append(A("P", lambda g, c=c: g.tensor_scalar(
                        wffq_s[:, c * D:(c + 1) * D], stgf[c % 4][:, :],
                        rsk[:, 0:1], 1.0, ALU.mult, ALU.mult),
                        deps=[ld, e_rsk]))
                e_wffq = e_wf_cast[7]


            if 0 <= q1 < nt:
                b = q1 % 2
                svr = sa(q1)

                def retT_mms(t, b=b, svr=svr):
                    r = None
                    for j in range(NCHUNK):
                        r = t.transpose(pS_b[svr][:, j * TL:(j + 1) * TL],
                                        ret_s[b][:, j * TL:(j + 1) * TL],
                                        idm[:, :])
                    return r
                E[("retT", q1)] = A("T", retT_mms,
                                    deps=[E[("retadd", q1)],
                                          E.get(("v", q1 + 1))])
                E[("retTe", q1)] = A("S", lambda s, b=b, svr=svr: s.copy(
                    retT_s[b][:, :], pS_b[svr][:, 0:D]),
                    deps=[E[("retT", q1)], E.get(("wo", q1 - 2))])

            if 0 <= q3 < nt:
                b = q3 % 2
                svh = sv(q3)

                def h2T_mms(t, b=b, svh=svh):
                    r = None
                    for j in range(NCHUNK):
                        r = t.transpose(
                            pS_b[svh][:, j * TL:(j + 1) * TL],
                            h2_s[b][:, j * TL:(j + 1) * TL], idm[:, :])
                    return r
                E[("h2T", q3)] = A("T", h2T_mms,
                                   deps=[E[("h2", q3)], E.get(("v", q3 + 3)),
                                         E.get(("retTe", q3 + 2)),
                                         E.get(("quant", q3 - 2))])
                E[("quant", q3)] = A("S", lambda s, b=b, svh=svh: s.activation(
                    h2q_s[b][:, :], pS_b[svh][:, 0:D], AF.Identity,
                    scale=rsin[:, 0:1]),
                    deps=[E[("h2T", q3)], E.get(("ffn", q3 - 2)), e_rsin])

            if q0 < nt:
                b = q0 % 2

                def cumc_mms(t, b=b, q0=q0):
                    r = None
                    for j in range(2):
                        r = t.matmul(pCum[:, j * 512:(j + 1) * 512],
                                     usr_s[b][:, :],
                                     vcvs[b][:, j * 512:(j + 1) * 512],
                                     start=True, stop=(q0 == 0))
                    if q0 > 0:
                        for j in range(2):
                            r = t.matmul(pCum[:, j * 512:(j + 1) * 512],
                                         selm[0:64, :],
                                         cums[(q0 - 1) % 2][:, j * 512:(j + 1) * 512],
                                         start=False, stop=True)
                    return r
                E[("cumC", q0)] = A("T", cumc_mms,
                                    deps=[E[("vc", q0)], E[("usr", q0)],
                                          E.get(("t2", q0 - 1)),
                                          E.get(("cveS", q0 - 1)),
                                          E.get(("cveC", q0 - 1))]
                                    + ([e_w1rep] if q0 == 0 else []))

            if 0 <= q1 < nt:

                def wo_mms(t, b=q1 % 2, q1=q1):
                    r = None
                    for c in range(NCHUNK):
                        for j in range(2):
                            r = t.matmul(
                                pS[sa(q1)][:, j * 512:(j + 1) * 512],
                                retT_s[b][:, c * TL:(c + 1) * TL],
                                wot_s[:, c * D + j * 512:c * D + (j + 1) * 512],
                                start=(c == 0), stop=(c == NCHUNK - 1))
                    return r
                E[("wo", q1)] = A("T", wo_mms,
                                  deps=[E[("retTe", q1)],
                                        E.get(("quant", q1 - 2)),
                                        E.get(("t2act", q1 - 3)),
                                        E.get(("mv2", q1 - 3))]
                                  + ([e_wot] if q1 == 0 else []))

            if 0 <= q2 < nt:
                b = q2 % 2
                pslot = pS[sa(q2)]
                E[("bn2", q2)] = A("V", lambda v, b=b, pslot=pslot: v.bn_stats(
                    bnst[1][b][:, 0:6], pslot[:, 0:512]), deps=[E[("wo", q2)]])
                A("V", lambda v, b=b, pslot=pslot: v.bn_stats(
                    bnst[1][b][:, 6:12], pslot[:, 512:1024]),
                    deps=[E[("wo", q2)]])
                E[("mv2", q2)] = A("V", lambda v, b=b: v.bn_aggr(
                    mv[1][b][:, :], bnst[1][b][:, :]),
                    deps=[E.get(("vep2", q2 - 2))], tag="G")
                E[("vep2", q2)] = A("P", lambda g, b=b: g.tensor_scalar(
                    vep[1][b][:, :], mv[1][b][:, 1:2], 1.0, EPS, ALU.mult,
                    ALU.add), deps=[E[("mv2", q2)], E.get(("pow2", q2 - 2))])
                E[("pow2", q2)] = A("P", lambda g, b=b: g.tensor_tensor(
                    rcol[1][b][:, :], vep[1][b][:, :], mhalf[:, :], ALU.pow),
                    deps=[E[("vep2", q2)], E.get(("t2act", q2 - 2))])
                E[("nmr2", q2)] = A("P", lambda g, b=b: g.tensor_scalar(
                    nmr[1][b][:, :], rcol[1][b][:, :], mv[1][b][:, 0:1], -1.0,
                    ALU.mult, ALU.mult),
                    deps=[E[("pow2", q2)], E.get(("t2act", q2 - 2))])
                E[("t2act", q2)] = A("S", lambda s, b=b, pslot=pslot: s.activation(
                    t2s_s[0][:, :], pslot[:, :], AF.Identity,
                    scale=rcol[1][b][:, 0:1], bias=nmr[1][b][:, 0:1]),
                    deps=[E[("nmr2", q2)], E.get(("xp", q2 - 1))])

            if 0 <= q4 < nt:
                b = q4 % 2
                E[("bn4", q4)] = A("V", lambda v, b=b: v.bn_stats(
                    bnst[3][b][:, 0:6], ybf_s[b][:, 0:512]),
                    deps=[E[("deq", q4)]])
                A("V", lambda v, b=b: v.bn_stats(
                    bnst[3][b][:, 6:12], ybf_s[b][:, 512:1024]),
                    deps=[E[("deq", q4)]])
                E[("mv4", q4)] = A("V", lambda v, b=b: v.bn_aggr(
                    mv[3][b][:, :], bnst[3][b][:, :]),
                    deps=[E.get(("vep4", q4 - 2))], tag="G")

            if q0 < nt:
                b = q0 % 2
                E[("cveC", q0)] = A("S", lambda s, b=b: s.copy(
                    cums[b][:, 0:D], pCum[64:128, :]),
                    deps=[E[("cumC", q0)], E.get(("cumC", q0 - 1))])

            if 0 <= q4 < nt:
                b = q4 % 2
                E[("vep4", q4)] = A("P", lambda g, b=b: g.tensor_scalar(
                    vep[3][b][:, :], mv[3][b][:, 1:2], 1.0, EPS, ALU.mult,
                    ALU.add), deps=[E[("mv4", q4)], E.get(("pow4", q4 - 2))])
                E[("pow4", q4)] = A("P", lambda g, b=b: g.tensor_tensor(
                    rcol[3][b][:, :], vep[3][b][:, :], mhalf[:, :], ALU.pow),
                    deps=[E[("vep4", q4)], E.get(("ln4", q4 - 2))])
                E[("nmr4", q4)] = A("P", lambda g, b=b: g.tensor_scalar(
                    nmr[3][b][:, :], rcol[3][b][:, :], mv[3][b][:, 0:1], -1.0,
                    ALU.mult, ALU.mult),
                    deps=[E[("pow4", q4)], E.get(("ln4", q4 - 2))])
                E[("ln4", q4)] = A("S", lambda s, b=b: s.activation(
                    ln4_s[0][:, :], ybf_s[b][:, :], AF.Identity,
                    scale=rcol[3][b][:, 0:1], bias=nmr[3][b][:, 0:1]),
                    deps=[E[("nmr4", q4)], E.get(("outadd", q4 - 1))]
                    + ([e_wot] if q4 == 0 else []))

            for qR in ([] if p == 0 else [q0 + 1]):
                if qR >= nt:
                    continue
                b = qR % 2
                E[("bn1", qR)] = A("V", lambda v, b=b, qR=qR: v.bn_stats(
                    bnst[0][b][:, 0:6], xtok_s[qR % 4][:, 0:512]),
                    deps=[E[("xl", qR)]])
                A("V", lambda v, b=b, qR=qR: v.bn_stats(
                    bnst[0][b][:, 6:12], xtok_s[qR % 4][:, 512:1024]),
                    deps=[E[("xl", qR)]])
                E[("mv1", qR)] = A("V", lambda v, b=b: v.bn_aggr(
                    mv[0][b][:, :], bnst[0][b][:, :]),
                    deps=[E.get(("v", qR - 2)), E.get(("vep1", qR - 2))],
                    tag="G")
                E[("vep1", qR)] = A("P", lambda g, b=b: g.tensor_scalar(
                    vep[0][b][:, :], mv[0][b][:, 1:2], 1.0, EPS, ALU.mult,
                    ALU.add), deps=[E[("mv1", qR)], E.get(("pow1", qR - 2))])
                E[("pow1", qR)] = A("P", lambda g, b=b: g.tensor_tensor(
                    rcol[0][b][:, :], vep[0][b][:, :], mhalf[:, :], ALU.pow),
                    deps=[E[("vep1", qR)], E.get(("usr", qR - 2))])

            for qV in ([] if p == 0 else [q0 + 2]):
                if qV >= nt:
                    continue
                E[("u", qV)] = A("V", lambda v, qV=qV: v.tensor_scalar(
                    u_s[:, :], f2pi[:, :], tcol_s[qV][:, 0:1], None, ALU.mult),
                    deps=[E.get(("k1", qV - 1)), E[("tc", qV)]])
                E[("k1", qV)] = A("P", lambda g: g.tensor_scalar(
                    k_s[:, :], u_s[:, :], 1.0, MAGIC, ALU.mult, ALU.add),
                    deps=[E[("u", qV)], E.get(("w", qV - 1))])
                E[("k", qV)] = A("P", lambda g: g.tensor_scalar(
                    k_s[:, :], k_s[:, :], 1.0, -MAGIC, ALU.mult, ALU.add),
                    deps=[E[("k1", qV)]])

            for qT in [q0 + 4]:
                if qT >= nt:
                    continue
                E[("tc", qT)] = A("V", lambda v, qT=qT: v.tensor_copy(
                    tcol_s[qT][:, :], tcall_s[:, qT:qT + 1]), deps=[e_tcall])

            if 0 <= q3 < nt:
                b = q3 % 2

                def ffn_mms(t, b=b, q3=q3):
                    r = None
                    for m in range(4):
                        for j in range(2):
                            lhsT = bass.AP(h2q_s[b], m * 256,
                                           [[NCHUNK * D // 8, TL], [TL, 2], [1, TL]])
                            rhs = bass.AP(wffq_s, m * 2 * D + j * 512,
                                          [[NCHUNK * D, TL], [D, 2], [1, 512]])
                            r = t.matmul(
                                pS[sy(q3)][:, j * 512:(j + 1) * 512],
                                lhsT, rhs, start=(m == 0), stop=False,
                                perf_mode=mybir.MatmulPerfMode.DoubleRow)
                    for j in range(2):
                        r = t.matmul(
                            pS[sy(q3)][:, j * 512:(j + 1) * 512],
                            onesb_row[0:1, :],
                            bdiv[0:1, j * 512:(j + 1) * 512],
                            start=False, stop=True)
                    return r
                E[("ffn", q3)] = A("T", ffn_mms,
                                   deps=[E[("quant", q3)], e_wffq,
                                         E.get(("t2act", q3 + 1)),
                                         E.get(("mv2", q3 + 1))]
                                   + ([e_bdiv, e_ms[4]] if q3 == 0 else []))

            if 0 <= q2 < nt:
                b = q2 % 2
                E[("xp", q2)] = A("V", lambda v, b=b, q2=q2: v.tensor_tensor(
                    xp_s[q2 % 3][:, :], xtok_s[q2 % 4][:, :], t2s_s[0][:, :],
                    ALU.add), deps=[E[("t2act", q2)],
                                    E.get(("outadd", q2 - 3))])
                E[("bn3", q2)] = A("V", lambda v, b=b, q2=q2: v.bn_stats(
                    bnst[2][b][:, 0:6], xp_s[q2 % 3][:, 0:512]),
                    deps=[E[("xp", q2)]])
                A("V", lambda v, b=b, q2=q2: v.bn_stats(
                    bnst[2][b][:, 6:12], xp_s[q2 % 3][:, 512:1024]),
                    deps=[E[("xp", q2)]])
                E[("mv3", q2)] = A("V", lambda v, b=b: v.bn_aggr(
                    mv[2][b][:, :], bnst[2][b][:, :]),
                    deps=[E.get(("vep3", q2 - 2))], tag="G")
                E[("vep3", q2)] = A("P", lambda g, b=b: g.tensor_scalar(
                    vep[2][b][:, :], mv[2][b][:, 1:2], 1.0, EPS, ALU.mult,
                    ALU.add), deps=[E[("mv3", q2)], E.get(("pow3", q2 - 2))])
                E[("pow3", q2)] = A("P", lambda g, b=b: g.tensor_tensor(
                    rcol[2][b][:, :], vep[2][b][:, :], mhalf[:, :], ALU.pow),
                    deps=[E[("vep3", q2)], E.get(("h2", q2 - 2))])
                E[("nmr3", q2)] = A("P", lambda g, b=b: g.tensor_scalar(
                    nmr[2][b][:, :], rcol[2][b][:, :], mv[2][b][:, 0:1], -1.0,
                    ALU.mult, ALU.mult),
                    deps=[E[("pow3", q2)], E.get(("h2", q2 - 2))])
                E[("h2", q2)] = A("S", lambda s, b=b, q2=q2: s.activation(
                    h2_s[b][:, :], xp_s[q2 % 3][:, :], AF.Identity,
                    scale=rcol[2][b][:, 0:1], bias=nmr[2][b][:, 0:1]),
                    deps=[E[("nmr3", q2)], E.get(("h2T", q2 - 2))])

            if q0 < nt:
                b = q0 % 2
                E[("t1", q0)] = A("V", lambda v, b=b: v.tensor_tensor(
                    ta_s[b][:, :], pCum[:, :], cosa[b][:, :], ALU.mult),
                    deps=[E[("cumC", q0)], E[("cveC", q0)], E[("cos", q0)],
                          E.get(("retadd", q0 - 2))])

            for qR in ([] if p == 0 else [q0 + 1]):
                if qR >= nt:
                    continue
                b = qR % 2
                E[("usr", qR)] = A("P", lambda g, b=b: g.tensor_scalar(
                    usr_s[b][:, :], usmat[:, :], rcol[0][b][:, 0:1], 1.0,
                    ALU.mult, ALU.mult),
                    deps=[E[("pow1", qR)], E.get(("cumS", qR - 2))])

            for qV in ([] if p == 0 else [q0 + 2]):
                if qV >= nt:
                    continue
                b = qV % 2
                E[("w", qV)] = A("V", lambda v, b=b: v.scalar_tensor_tensor(
                    w_s[b][:, :], k_s[:, :], -1.0, u_s[:, :], ALU.mult, ALU.add),
                    deps=[E[("k", qV)], E.get(("sin", qV - 2))])
                E[("wc", qV)] = A("V", lambda v, b=b: v.scalar_tensor_tensor(
                    wc_s[b][:, :], w_s[b][:, :], 0.25, w_s[b][:, :],
                    ALU.is_ge, ALU.subtract),
                    deps=[E.get(("cos", qV - 2))])

            for qS in ([] if p == 0 else [q0 + 1]):
                if qS >= nt:
                    continue
                b = qS % 2
                E[("sin", qS)] = A("S", lambda s, b=b: s.activation(
                    sina[b][:, :], w_s[b][:, :], AF.Sin, scale=TWO_PI),
                    deps=[E[("w", qS)], E.get(("t2", qS - 2)),
                          E.get(("vs", qS - 2))]
                    + ([e_ms[2]] if qS == 0 else []))
                E[("cos", qS)] = A("S", lambda s, b=b: s.activation(
                    cosa[b][:, :], wc_s[b][:, :], AF.Sin, scale=-TWO_PI,
                    bias=piby2[:, :]),
                    deps=[E[("wc", qS)], E.get(("t1", qS - 2)),
                          E.get(("vc", qS - 2))]
                    + ([e_ms[2]] if qS == 0 else []))

            if q0 < nt:
                b = q0 % 2

                def cums_mms(t, b=b, q0=q0):
                    r = None
                    for j in range(2):
                        r = t.matmul(pCum[:, j * 512:(j + 1) * 512],
                                     usr_s[b][:, :],
                                     vcvs[b][:, D + j * 512:D + (j + 1) * 512],
                                     start=True, stop=(q0 == 0))
                    if q0 > 0:
                        for j in range(2):
                            r = t.matmul(pCum[:, j * 512:(j + 1) * 512],
                                         selm[0:64, :],
                                         cums[(q0 - 1) % 2][:, D + j * 512:D + (j + 1) * 512],
                                         start=False, stop=True)
                    return r
                E[("cumS", q0)] = A("T", cums_mms,
                                    deps=[E[("vs", q0)], E[("t1", q0)],
                                          E[("cveC", q0)]])
                E[("t2", q0)] = A("V", lambda v, b=b: v.tensor_tensor(
                    tb_s[b][:, :], pCum[:, :], sina[b][:, :], ALU.mult),
                    deps=[E[("cumS", q0)], E[("sin", q0)],
                          E.get(("retadd", q0 - 2))])
                E[("retadd", q0)] = A("V", lambda v, b=b: v.tensor_tensor(
                    ret_s[b][:, :], ta_s[b][:, :], tb_s[b][:, :], ALU.add),
                    deps=[E[("t1", q0)], E[("t2", q0)],
                          E.get(("retT", q0 - 2))]
                    + ([e_wf_cast[6 + q0]] if q0 <= 1 else []))
                E[("cveS", q0)] = A("S", lambda s, b=b: s.copy(
                    cums[b][:, D:2 * D], pCum[64:128, :]),
                    deps=[E[("cumS", q0)], E[("t2", q0)],
                          E.get(("cumS", q0 - 1))])

            for qW in ([1] if p == 0 else [q0 + 1]):
                if qW >= nt:
                    continue

                def wv_mms(t, qW=qW):
                    r = None
                    for c in range(NCHUNK):
                        for j in range(2):
                            r = t.matmul(
                                pS[sv(qW)][:, j * 512:(j + 1) * 512],
                                xftb_s[qW % 2][:, c * TL:(c + 1) * TL],
                                wvt_s[:, c * D + j * 512:c * D + (j + 1) * 512],
                                start=(c == 0), stop=(c == NCHUNK - 1))
                    return r
                E[("wv", qW)] = A("T", wv_mms,
                                  deps=[E[("fcast", qW)], e_wvt,
                                        E.get(("deq", qW - 5)),
                                        E.get(("t2act", qW - 4)),
                                        E.get(("mv2", qW - 4))]
                                  + ([e_f2pi] if qW <= 1 else []))
            if 0 <= q4 < nt:
                b = q4 % 2
                E[("outadd", q4)] = A("P", lambda g, b=b, q4=q4: g.tensor_tensor(
                    out_s[b][:, :], xp_s[q4 % 3][:, :], ln4_s[0][:, :], ALU.add),
                    deps=[E[("ln4", q4)], E[("xp", q4)],
                          E.get(("ost", q4 - 2))]
                    + ([e_wo_cast[6 + q4]] if q4 <= 1 else []))
                E[("ost", q4)] = A("P", lambda g, b=b, q4=q4: g.dma_start(
                    out_d[q4 * TL:(q4 + 1) * TL, :], out_s[b][:, :]),
                    deps=[E[("outadd", q4)]], sem=f"DO{q4 % 2}", inc=16)

        with nc.Block() as block:

            @block.sync
            def _(e):
                sch.emit("D", e, sems, cut)
                for sname, cval in sorted(sch.dma_counts_upto(cut).items()):
                    e.wait_ge(sems[sname], cval)

            @block.vector
            def _(v):
                sch.emit("V", v, sems, cut, selfsync=("VSP" if racecheck else selfsync))

            @block.scalar
            def _(s):
                sch.emit("S", s, sems, cut, selfsync=("VSP" if racecheck else selfsync))

            @block.gpsimd
            def _(g):
                sch.emit("P", g, sems, cut, selfsync=("VSP" if racecheck else selfsync))

            @block.tensor
            def _(t):
                sch.emit("T", t, sems, cut, selfsync=("VSP" if racecheck else selfsync))

    return nc


def _host_layout(x_b, Wv, Wo, Wff, freqs, s_in, s_k, bias, nt):
    T = nt * TL
    xtok = np.ascontiguousarray(x_b)
    xft = np.ascontiguousarray(
        x_b.T.reshape(NCHUNK, TL, nt, TL).transpose(2, 1, 0, 3)
        .reshape(nt * TL, D))
    wvt = np.ascontiguousarray(Wv.T)
    wot = np.ascontiguousarray(Wo.T)
    wff = np.ascontiguousarray(Wff)
    tcols = (np.arange(TL, dtype=np.float32)[:, None]
             + TL * np.arange(nt, dtype=np.float32)[None, :])
    us = np.triu(np.ones((TL, TL), dtype=np.float32)).astype(ml_dtypes.bfloat16)
    idm = np.eye(TL, dtype=np.float32).astype(ml_dtypes.bfloat16)
    sel = np.zeros((TL, TL), dtype=np.float32)
    sel[63, :] = 1.0
    sel = sel.astype(ml_dtypes.bfloat16)
    return {
        "xtok": xtok, "xft": xft, "wvt": wvt, "wot": wot, "wff": wff,
        "freqs": np.ascontiguousarray(freqs.reshape(1, D)),
        "tcols": np.ascontiguousarray(tcols),
        "usmat": us, "idmat": idm, "selmat": sel,
        "s_in": s_in.reshape(1, 1), "s_k": s_k.reshape(1, 1),
        "biasv": np.ascontiguousarray(bias.reshape(1, D)),
    }


def kernel(L0_attn_x, L0_attn_Wk, L0_attn_Wv, L0_attn_Wo, L0_attn_freqs,
           L0_ffn_kernel, L0_ffn_input_scale, L0_ffn_kernel_scale, L0_ffn_bias):
    x = np.asarray(L0_attn_x)
    assert np.asarray(L0_attn_Wk).shape == np.asarray(L0_attn_Wv).shape
    B, T, Dd = x.shape
    nt = T // TL
    nc = build(nt)
    in_maps = []
    for b in range(B):
        in_maps.append(_host_layout(
            x[b], np.asarray(L0_attn_Wv), np.asarray(L0_attn_Wo),
            np.asarray(L0_ffn_kernel), np.asarray(L0_attn_freqs),
            np.asarray(L0_ffn_input_scale), np.asarray(L0_ffn_kernel_scale),
            np.asarray(L0_ffn_bias), nt))
    res = run_bass_kernel_spmd(nc, in_maps, core_ids=list(range(B)))
    out = np.stack([np.asarray(r["out"]) for r in res.results], axis=0)
    return out.astype(np.float32)


# version 63
# speedup vs baseline: 1.2121x; 1.0027x over previous
import numpy as np
import ml_dtypes
import concourse.bass as bass
import concourse.mybir as mybir
from concourse.bass_utils import run_bass_kernel_spmd
from contextlib import ExitStack

F32 = mybir.dt.float32
BF16 = mybir.dt.bfloat16
F8 = mybir.dt.float8e4
I32 = mybir.dt.int32
AF = mybir.ActivationFunctionType
ALU = mybir.AluOpType

MAGIC = 12582912.0
TWO_PI = 6.283185307179586
EPS = 1e-5
D = 1024
TL = 128
NCHUNK = 8

ENGSEM = {"T": "semT", "V": "semV", "S": "semS", "P": "semP"}


class Sched:

    def __init__(self):
        self.q = {k: [] for k in ["T", "V", "S", "P", "D"]}
        self.counts = {}
        self.sem_names = set()
        self.seq = 0

    def add(self, eng, fn, deps=(), sem=None, inc=None, tag=""):
        if sem is None:
            sem = ENGSEM[eng]
        if inc is None:
            inc = 16 if eng == "D" else 1
        self.sem_names.add(sem)
        self.counts[sem] = self.counts.get(sem, 0) + inc
        ev = (sem, self.counts[sem])
        deps = [d for d in deps if d is not None]
        self.seq += 1
        self.q[eng].append((deps, fn, sem, inc, self.seq, tag))
        return ev

    def dma_counts_upto(self, cut):
        out = {}
        engsems = set(ENGSEM.values())
        for qk in self.q:
            for (deps, fn, sem, inc, seq, tag) in self.q[qk]:
                if sem in engsems:
                    continue
                if cut is None or seq <= cut:
                    out[sem] = out.get(sem, 0) + inc
        return out

    def emit(self, eng_key, engine, sems, cut=None, selfsync=""):
        floors = {}
        own = ENGSEM.get(eng_key)
        own_count = 0
        for deps, fn, sem, inc, seq, tag in self.q[eng_key]:
            if cut is not None and seq > cut:
                continue
            need = {}
            for (s, v) in deps:
                if v > need.get(s, 0):
                    need[s] = v
            ssync = (eng_key in selfsync) or (tag and tag in selfsync)
            if ssync and own is not None \
                    and own_count > 0 and own_count > need.get(own, 0):
                need[own] = own_count
            for s, v in sorted(need.items()):
                if v > floors.get(s, 0):
                    engine.wait_ge(sems[s], v)
                    floors[s] = v
            instr = fn(engine)
            instr.then_inc(sems[sem], inc)
            if sem == own:
                own_count += inc


def build(nt=32, cut=None, racecheck=False, selfsync="G"):
    nc = bass.Bass("TRN2", target_bir_lowering=False, debug=False,
                   detect_race_conditions=racecheck)
    T = nt * TL

    xtok_d = nc.dram_tensor("xtok", [T, D], F32, kind="ExternalInput")
    xft_d = nc.dram_tensor("xft", [nt * TL, D], F32, kind="ExternalInput")
    wvt_d = nc.dram_tensor("wvt", [NCHUNK * TL, D], F32, kind="ExternalInput")
    wot_d = nc.dram_tensor("wot", [NCHUNK * TL, D], F32, kind="ExternalInput")
    wff_d = nc.dram_tensor("wff", [NCHUNK * TL, D], BF16, kind="ExternalInput")
    freqs_d = nc.dram_tensor("freqs", [1, D], F32, kind="ExternalInput")
    tcols_d = nc.dram_tensor("tcols", [TL, nt], F32, kind="ExternalInput")
    us_d = nc.dram_tensor("usmat", [TL, TL], BF16, kind="ExternalInput")
    id_d = nc.dram_tensor("idmat", [TL, TL], BF16, kind="ExternalInput")
    sel_d = nc.dram_tensor("selmat", [TL, TL], BF16, kind="ExternalInput")
    sin_d = nc.dram_tensor("s_in", [1, 1], F32, kind="ExternalInput")
    sk_d = nc.dram_tensor("s_k", [1, 1], F32, kind="ExternalInput")
    bias_d = nc.dram_tensor("biasv", [1, D], BF16, kind="ExternalInput")
    out_d = nc.dram_tensor("out", [T, D], F32, kind="ExternalOutput")

    sch = Sched()
    with ExitStack() as st:
        def sb(name, shape, dt):
            return st.enter_context(nc.sbuf_tensor(name, shape, dt))

        sems = {}
        for s in ["semT", "semV", "semS", "semP", "DW", "DWS", "DWT",
                  "DWU", "DWV", "DW4", "DW5", "DW6", "DW7",
                  "DWA", "DWB", "DWC", "DWD", "DWE", "DWF", "DWG", "DWH",
                  "DX0", "DX1", "DX2", "DX3", "DF0", "DF1", "DO0", "DO1"]:
            sems[s] = st.enter_context(nc.semaphore(s))

        wstage = sb("wstage", [TL, 1024], F32)
        tcall_s = sb("tcall", [TL, nt], F32)
        wvt_s = sb("wvt_s", [TL, NCHUNK * D], BF16)
        wot_s = sb("wot_s", [TL, NCHUNK * D], BF16)
        wffq_s = sb("wffq_s", [TL, NCHUNK * D], F8)
        f2pi = sb("f2pi", [TL, D], F32)
        frow = sb("frow", [1, D], F32)
        tcol_s = [sb(f"tc{i}", [TL, 1], F32) for i in range(nt)]
        usmat = sb("usmat_s", [TL, TL], BF16)
        idm = sb("idm", [TL, TL], BF16)
        selm = sb("selm", [TL, TL], BF16)
        onesf = sb("onesf", [1, TL], F32)
        onesb_row = sb("onesb_row", [1, TL], BF16)
        onescol_b = sb("onescol_b", [TL, 1], BF16)
        w1n_rep = sb("w1n_rep", [TL, D], BF16)
        w1row = sb("w1row", [1, D], BF16)
        bdiv = sb("bdiv", [1, D], BF16)
        rr = sb("rr", [1, 1], F32)
        brow = sb("brow", [1, D], BF16)
        sincol = sb("sincol", [TL, 1], F32)
        skcol = sb("skcol", [TL, 1], F32)
        rsin = sb("rsin", [TL, 1], F32)
        rsk = sb("rsk", [TL, 1], F32)
        sisk = sb("sisk", [TL, 1], F32)
        piby2 = sb("piby2", [TL, 1], F32)
        mhalf = sb("mhalf", [TL, 1], F32)
        srow = sb("srow", [1, 2], F32)

        xtok_s = [sb(f"xtok{i}", [TL, D], F32) for i in range(4)]
        xftf_s = [sb(f"xftf{i}", [TL, D], F32) for i in range(2)]
        xftb_s = [sb(f"xftb{i}", [TL, D], BF16) for i in range(2)]
        u_s = sb("u_s", [TL, D], F32)
        k_s = sb("k_s", [TL, D], F32)
        w_s = [sb(f"w{i}", [TL, D], F32) for i in range(2)]
        wc_s = [sb(f"wc{i}", [TL, D], F32) for i in range(2)]
        sina = [sb(f"sina{i}", [TL, D], BF16) for i in range(2)]
        cosa = [sb(f"cosa{i}", [TL, D], BF16) for i in range(2)]
        usr_s = [sb(f"usr{i}", [TL, TL], BF16) for i in range(2)]
        v_s = [sb(f"v{i}", [TL, D], BF16) for i in range(2)]
        vcvs = [sb(f"vcvs{i}", [TL, 2 * D], BF16) for i in range(2)]
        ta_s = [sb(f"ta{i}", [TL, D], BF16) for i in range(2)]
        tb_s = [sb(f"tb{i}", [TL, D], BF16) for i in range(2)]
        ret_s = [sb(f"ret{i}", [TL, D], BF16) for i in range(2)]
        retT_s = [sb(f"retT{i}", [TL, D], BF16) for i in range(2)]
        cums = [sb(f"cums{i}", [64, 2 * D], BF16) for i in range(2)]
        t2s_s = [sb("t2s0", [TL, D], F32)]
        xp_s = [sb(f"xp{i}", [TL, D], F32) for i in range(3)]
        h2_s = [sb(f"h2_{i}", [TL, D], BF16) for i in range(2)]
        h2q_s = [sb(f"h2q{i}", [TL, D], F8) for i in range(2)]
        ybf_s = [sb(f"ybf{i}", [TL, D], BF16) for i in range(2)]
        ln4_s = [sb("ln4_0", [TL, D], F32)]
        wstage2 = ln4_s[0]
        out_s = [sb(f"outs{i}", [TL, D], F32) for i in range(2)]
        bnst = [[sb(f"bnst{ln}_{i}", [TL, 12], F32) for i in range(2)]
                for ln in range(4)]
        mv = [[sb(f"mv{ln}_{i}", [TL, 2], F32) for i in range(2)]
              for ln in range(4)]
        vep = [[sb(f"vep{ln}_{i}", [TL, 1], F32) for i in range(2)]
               for ln in range(4)]
        rcol = [[sb(f"r{ln}_{i}", [TL, 1], F32) for i in range(2)]
                for ln in range(4)]
        nmr = [[sb(f"nmr{ln}_{i}", [TL, 1], F32) for i in range(2)]
               for ln in range(4)]

        pS = [st.enter_context(nc.psum_tensor("pA", [TL, D], F32)),
              st.enter_context(nc.psum_tensor("pB", [TL, D], F32)),
              st.enter_context(nc.psum_tensor("pC", [TL, D], F32))]
        pCum = st.enter_context(nc.psum_tensor("pCum", [TL, D], F32))
        pS_b = [p.ap().bitcast(BF16) for p in pS]

        A = sch.add
        pre_loads = []
        for dst, src in [(usmat, us_d), (idm, id_d),
                         (selm, sel_d), (frow, freqs_d), (brow, bias_d)]:
            pre_loads.append(A("D", lambda e, d=dst, s=src: e.dma_start(d[:, :], s[:, :]),
                               deps=(), sem="DW"))
        e_tcall = A("D", lambda e: e.dma_start(tcall_s[:, :], tcols_d[:, :]),
                    deps=(), sem="DW")
        e_sin_l = A("D", lambda e: e.dma_start(srow[0:1, 0:1], sin_d[:, :]), sem="DW")
        e_sk_l = A("D", lambda e: e.dma_start(srow[0:1, 1:2], sk_d[:, :]), sem="DW")
        pre_loads += [e_tcall, e_sin_l, e_sk_l]
        e_dw_all = [e_sk_l]

        e_ms = [
            A("V", lambda v: v.memset(onesf[:, :], 1.0)),
            A("V", lambda v: v.memset(onescol_b[:, :], 1.0)),
            A("V", lambda v: v.memset(piby2[:, :], TWO_PI / 4.0)),
            A("P", lambda g: g.memset(mhalf[:, :], -0.5)),
            A("V", lambda v: v.memset(onesb_row[:, :], 1.0)),
        ]

        E = {}
        E[("xl", 0)] = A("D", lambda e: e.dma_start(
            xtok_s[0][:, :], xtok_d[0:TL, :]), deps=(), sem="DX0")
        E[("fl", 0)] = A("D", lambda e: e.dma_start(
            xftf_s[0][:, :], xft_d[0:TL, :]), deps=(), sem="DF0")
        E[("xl", 1)] = A("D", lambda e: e.dma_start(
            xtok_s[1][:, :], xtok_d[TL:2 * TL, :]), deps=(), sem="DX1")
        E[("fl", 1)] = A("D", lambda e: e.dma_start(
            xftf_s[1][:, :], xft_d[TL:2 * TL, :]), deps=(), sem="DF1")
        E[("fcast", 0)] = A("P", lambda g: g.tensor_scalar(
            xftb_s[0][:, :], xftf_s[0][:, :], 1.0, 1.0, ALU.mult,
            ALU.mult), deps=[E[("fl", 0)]])
        E[("fcast", 1)] = A("P", lambda g: g.tensor_scalar(
            xftb_s[1][:, :], xftf_s[1][:, :], 1.0, 1.0, ALU.mult,
            ALU.mult), deps=[E[("fl", 1)]])

        e_warm = A("S", lambda s_: s_.activation(
            ybf_s[0][0:1, 0:1], piby2[0:1, 0:1], AF.Sin, scale=0.0),
            deps=[e_ms[2]])

        e_sbcast = A("T", lambda t: t.matmul(
            pS[0][:, 0:2], onesf[:, :], srow[0:1, 0:2], start=True, stop=True),
            deps=e_dw_all + [e_ms[0]])
        e_scols = A("V", lambda v: v.tensor_copy(sincol[:, :], pS[0][:, 0:1]),
                    deps=[e_sbcast])
        e_scols2 = A("V", lambda v: v.tensor_copy(skcol[:, :], pS[0][:, 1:2]),
                     deps=[e_sbcast])
        e_rsin = A("V", lambda v: v.reciprocal(rsin[:, :], sincol[:, :]),
                   deps=[e_scols])
        e_rsk = A("V", lambda v: v.reciprocal(rsk[:, :], skcol[:, :]),
                  deps=[e_scols2])
        e_sisk = A("V", lambda v: v.tensor_tensor(sisk[:, :], sincol[:, :],
                                                  skcol[:, :], ALU.mult),
                   deps=[e_scols, e_scols2])

        e_fb = [A("T", lambda t, j=j: t.matmul(
            pS[1][:, j * 512:(j + 1) * 512], onesf[:, :],
            frow[0:1, j * 512:(j + 1) * 512], start=True, stop=True),
            deps=e_dw_all + [e_ms[0], e_scols, e_scols2]) for j in range(2)]
        e_f2pi = A("V", lambda v: v.tensor_scalar(
            f2pi[:, :], pS[1][:, :], 1.0 / TWO_PI, None, ALU.mult), deps=e_fb)

        e_rr = A("V", lambda v: v.tensor_tensor(
            rr[0:1, :], rsin[0:1, 0:1], rsk[0:1, 0:1], ALU.mult),
            deps=[e_rsin, e_rsk])
        e_bdiv = A("V", lambda v: v.tensor_scalar(
            bdiv[0:1, :], brow[0:1, :], rr[0:1, 0:1], None, ALU.mult),
            deps=[e_rr] + e_dw_all)

        wstage_bf = wstage.ap().bitcast(BF16)
        wstage2_bf = wstage2.ap().bitcast(BF16)
        stg = [wstage, wstage2]
        stg_b = [wstage_bf, wstage2_bf]
        e_wv_cast = []
        e_wv0 = None
        stgv = [wstage, wstage2, out_s[0], out_s[1]]
        wv_sems = [["DWS", "DWT", "DWU", "DWV"],
                   ["DW4", "DW5", "DW6", "DW7"],
                   ["DWA", "DWB", "DWC", "DWD"],
                   ["DWE", "DWF", "DWG", "DWH"]]
        for c in range(8):
            dep = [e_wv_cast[c - 4]] if c >= 4 else []
            lds = []
            for h in range(4):
                lds.append(A("D", lambda e, c=c, h=h: e.dma_start(
                    stgv[c % 4][:, h * 256:(h + 1) * 256],
                    wvt_d[c * TL:(c + 1) * TL, h * 256:(h + 1) * 256]),
                    deps=dep, sem=wv_sems[c % 4][h]))
            e_wv_cast.append(A("V", lambda v, c=c: v.tensor_scalar(
                wvt_s[:, c * D:(c + 1) * D], stgv[c % 4][:, :],
                1.0, None, ALU.mult), deps=lds))

            def w1c_mms(t, c=c):
                r = None
                for j in range(2):
                    r = t.matmul(pCum[0:1, j * 512:(j + 1) * 512],
                                 onescol_b[:, :],
                                 wvt_s[:, c * D + j * 512:c * D + (j + 1) * 512],
                                 start=(c == 0), stop=(c == NCHUNK - 1))
                return r
            e_w1mm = A("T", w1c_mms,
                       deps=[e_wv_cast[c]] + ([e_ms[1]] if c == 0 else []))

            def wv0c_mms(t, c=c):
                r = None
                for j in range(2):
                    r = t.matmul(pS[0][:, j * 512:(j + 1) * 512],
                                 xftb_s[0][:, c * TL:(c + 1) * TL],
                                 wvt_s[:, c * D + j * 512:c * D + (j + 1) * 512],
                                 start=(c == 0), stop=(c == NCHUNK - 1))
                return r
            e_wv0 = A("T", wv0c_mms,
                      deps=[e_wv_cast[c]]
                      + ([E[("fcast", 0)], e_scols, e_scols2] if c == 0 else []))
        e_wvt = e_wv_cast[7]
        E[("wv", 0)] = e_wv0

        e_w1row = A("V", lambda v: v.tensor_scalar(
            w1row[0:1, :], pCum[0:1, 0:D], -1.0, None, ALU.mult), deps=[e_w1mm])
        e_w1b = [A("T", lambda t, j=j: t.matmul(
            pCum[:, j * 512:(j + 1) * 512], onesb_row[0:1, :],
            w1row[0:1, j * 512:(j + 1) * 512], start=True, stop=True),
            deps=[e_w1row, e_ms[4]]) for j in range(2)]
        e_w1rep = A("V", lambda v: v.tensor_scalar(
            w1n_rep[:, :], pCum[:, :], 1.0, None, ALU.mult), deps=e_w1b)

        for i in range(min(4, nt)):
            E[("tc", i)] = A("V", lambda v, i=i: v.tensor_copy(
                tcol_s[i][:, :], tcall_s[:, i:i + 1]), deps=[e_tcall])
        e_wot = None
        e_wffq = None


        def sv(q):
            return q % 3

        def sa(q):
            return (q + 1) % 3

        def sy(q):
            return (q + 2) % 3

        for p in range(nt + 5):
            q0, q1, q2, q3, q4 = p, p - 1, p - 2, p - 3, p - 4

            if 2 <= p + 1 < nt:
                qq = p + 1
                E[("xl", qq)] = A("D", lambda e, qq=qq: e.dma_start(
                    xtok_s[qq % 4][:, :], xtok_d[qq * TL:(qq + 1) * TL, :]),
                    deps=[E.get(("xp", qq - 4))],
                    sem=f"DX{qq % 4}")
                E[("fl", qq)] = A("D", lambda e, qq=qq: e.dma_start(
                    xftf_s[qq % 2][:, :], xft_d[qq * TL:(qq + 1) * TL, :]),
                    deps=[E.get(("fcast", qq - 2))],
                    sem=f"DF{qq % 2}")
                E[("fcast", qq)] = A("P", lambda g, qq=qq: g.tensor_scalar(
                    xftb_s[qq % 2][:, :], xftf_s[qq % 2][:, :], 1.0, 1.0,
                    ALU.mult, ALU.mult),
                    deps=[E[("fl", qq)], E.get(("wv", qq - 2))])

            if 0 <= q4 < nt:
                b = q4 % 2
                E[("deq", q4)] = A("S", lambda s, b=b, q4=q4: s.activation(
                    ybf_s[b][:, :], pS[sy(q4)][:, :], AF.Identity,
                    scale=sisk[:, 0:1]),
                    deps=[E[("ffn", q4)], E.get(("ln4", q4 - 2)),
                          E.get(("mv4", q4 - 2)), e_sisk]
                    + ([e_wf_cast[4 + q4]] if q4 <= 1 else []))

            for qR in ([0, 1] if p == 0 else []):
                if qR >= nt:
                    continue
                b = qR % 2
                E[("bn1", qR)] = A("V", lambda v, b=b, qR=qR: v.bn_stats(
                    bnst[0][b][:, 0:6], xtok_s[qR % 4][:, 0:512]),
                    deps=[E[("xl", qR)]])
                A("V", lambda v, b=b, qR=qR: v.bn_stats(
                    bnst[0][b][:, 6:12], xtok_s[qR % 4][:, 512:1024]),
                    deps=[E[("xl", qR)]])
                E[("mv1", qR)] = A("V", lambda v, b=b: v.bn_aggr(
                    mv[0][b][:, :], bnst[0][b][:, :]),
                    deps=[E.get(("v", qR - 2)), E.get(("vep1", qR - 2))],
                    tag="G")
                E[("vep1", qR)] = A("P", lambda g, b=b: g.tensor_scalar(
                    vep[0][b][:, :], mv[0][b][:, 1:2], 1.0, EPS, ALU.mult,
                    ALU.add), deps=[E[("mv1", qR)], E.get(("pow1", qR - 2))])
                E[("pow1", qR)] = A("P", lambda g, b=b: g.tensor_tensor(
                    rcol[0][b][:, :], vep[0][b][:, :], mhalf[:, :], ALU.pow),
                    deps=[E[("vep1", qR)], E.get(("usr", qR - 2))])
            for qV in ([0, 1] if p == 0 else []):
                if qV >= nt:
                    continue
                E[("u", qV)] = A("V", lambda v, qV=qV: v.tensor_scalar(
                    u_s[:, :], f2pi[:, :], tcol_s[qV][:, 0:1], None, ALU.mult),
                    deps=([e_f2pi] if qV == 0 else []) + [E.get(("k1", qV - 1)),
                          E[("tc", qV)]])
                E[("k1", qV)] = A("P", lambda g: g.tensor_scalar(
                    k_s[:, :], u_s[:, :], 1.0, MAGIC, ALU.mult, ALU.add),
                    deps=[E[("u", qV)], E.get(("w", qV - 1))])
                E[("k", qV)] = A("P", lambda g: g.tensor_scalar(
                    k_s[:, :], k_s[:, :], 1.0, -MAGIC, ALU.mult, ALU.add),
                    deps=[E[("k1", qV)]])
                b = qV % 2
                E[("w", qV)] = A("V", lambda v, b=b: v.scalar_tensor_tensor(
                    w_s[b][:, :], k_s[:, :], -1.0, u_s[:, :], ALU.mult, ALU.add),
                    deps=[E[("k", qV)], E.get(("sin", qV - 2))])
                E[("wc", qV)] = A("V", lambda v, b=b: v.scalar_tensor_tensor(
                    wc_s[b][:, :], w_s[b][:, :], 0.25, w_s[b][:, :],
                    ALU.is_ge, ALU.subtract),
                    deps=[E.get(("cos", qV - 2))])

            for qS in ([0, 1] if p == 0 else []):
                if qS >= nt:
                    continue
                b = qS % 2
                E[("sin", qS)] = A("S", lambda s, b=b: s.activation(
                    sina[b][:, :], w_s[b][:, :], AF.Sin, scale=TWO_PI),
                    deps=[E[("w", qS)]] + [e_ms[2]])
                E[("cos", qS)] = A("S", lambda s, b=b: s.activation(
                    cosa[b][:, :], wc_s[b][:, :], AF.Sin, scale=-TWO_PI,
                    bias=piby2[:, :]),
                    deps=[E[("wc", qS)]] + [e_ms[2]])

            for qV in ([2] if p == 0 else []):
                if qV >= nt:
                    continue
                E[("u", qV)] = A("V", lambda v, qV=qV: v.tensor_scalar(
                    u_s[:, :], f2pi[:, :], tcol_s[qV][:, 0:1], None, ALU.mult),
                    deps=[E.get(("k1", qV - 1)), E[("tc", qV)]])
                E[("k1", qV)] = A("P", lambda g: g.tensor_scalar(
                    k_s[:, :], u_s[:, :], 1.0, MAGIC, ALU.mult, ALU.add),
                    deps=[E[("u", qV)], E.get(("w", qV - 1))])
                E[("k", qV)] = A("P", lambda g: g.tensor_scalar(
                    k_s[:, :], k_s[:, :], 1.0, -MAGIC, ALU.mult, ALU.add),
                    deps=[E[("k1", qV)]])
                b = qV % 2
                E[("w", qV)] = A("V", lambda v, b=b: v.scalar_tensor_tensor(
                    w_s[b][:, :], k_s[:, :], -1.0, u_s[:, :], ALU.mult, ALU.add),
                    deps=[E[("k", qV)], E.get(("sin", qV - 2))])
                E[("wc", qV)] = A("V", lambda v, b=b: v.scalar_tensor_tensor(
                    wc_s[b][:, :], w_s[b][:, :], 0.25, w_s[b][:, :],
                    ALU.is_ge, ALU.subtract),
                    deps=[E.get(("cos", qV - 2))])

            for qR in ([0, 1] if p == 0 else []):
                if qR >= nt:
                    continue
                b = qR % 2
                E[("usr", qR)] = A("P", lambda g, b=b: g.tensor_scalar(
                    usr_s[b][:, :], usmat[:, :], rcol[0][b][:, 0:1], 1.0,
                    ALU.mult, ALU.mult),
                    deps=[E[("pow1", qR)]])

            if q0 < nt:
                b = q0 % 2
                E[("v", q0)] = A("V", lambda v, b=b, q0=q0: v.scalar_tensor_tensor(
                    v_s[b][:, :], w1n_rep[:, :], mv[0][b][:, 0:1],
                    pS[sv(q0)][:, :], ALU.mult, ALU.add),
                    deps=[E[("wv", q0)], E[("mv1", q0)]]
                    + ([e_w1rep] if q0 == 0 else []))
                E[("vc", q0)] = A("V", lambda v, b=b: v.tensor_tensor(
                    vcvs[b][:, 0:D], v_s[b][:, :], cosa[b][:, :], ALU.mult),
                    deps=[E[("v", q0)], E[("cos", q0)],
                          E.get(("cumC", q0 - 2))])
                E[("vs", q0)] = A("V", lambda v, b=b: v.tensor_tensor(
                    vcvs[b][:, D:2 * D], v_s[b][:, :], sina[b][:, :], ALU.mult),
                    deps=[E[("v", q0)], E[("sin", q0)],
                          E.get(("cumS", q0 - 2))])

            if p == 0:
                stgo = [wstage, wstage2, out_s[0], out_s[1]]
                wo_sems = ["DWA", "DWB", "DWC", "DWD"]
                e_wo_cast = []
                for c in range(8):
                    if c >= 4:
                        dep = [e_wo_cast[c - 4]]
                    else:
                        dep = [e_wvt]
                    ld = A("D", lambda e, c=c: e.dma_start(
                        stgo[c % 4][:, :], wot_d[c * TL:(c + 1) * TL, :]),
                        deps=dep, sem=wo_sems[c % 4])
                    e_wo_cast.append(A("P", lambda g, c=c: g.tensor_scalar(
                        wot_s[:, c * D:(c + 1) * D], stgo[c % 4][:, :],
                        1.0, 1.0, ALU.mult, ALU.mult), deps=[ld]))
                e_wot = e_wo_cast[7]

            if p == 0:
                stgf = [ybf_s[0], ybf_s[1], ret_s[0], ret_s[1]]
                wf_sems = ["DWE", "DWF", "DWG", "DWH"]
                e_wf_cast = []
                for c in range(8):
                    if c >= 4:
                        dep = [e_wf_cast[c - 4]]
                    elif c == 0:
                        dep = [e_warm, e_wvt]
                    else:
                        dep = [e_wvt]
                    ld = A("D", lambda e, c=c: e.dma_start(
                        stgf[c % 4][:, :], wff_d[c * TL:(c + 1) * TL, :]),
                        deps=dep, sem=wf_sems[c % 4])
                    e_wf_cast.append(A("P", lambda g, c=c: g.tensor_scalar(
                        wffq_s[:, c * D:(c + 1) * D], stgf[c % 4][:, :],
                        rsk[:, 0:1], 1.0, ALU.mult, ALU.mult),
                        deps=[ld, e_rsk]))
                e_wffq = e_wf_cast[7]


            if 0 <= q1 < nt:
                b = q1 % 2
                svr = sa(q1)

                def retT_mms(t, b=b, svr=svr):
                    r = None
                    for j in range(NCHUNK):
                        r = t.transpose(pS_b[svr][:, j * TL:(j + 1) * TL],
                                        ret_s[b][:, j * TL:(j + 1) * TL],
                                        idm[:, :])
                    return r
                E[("retT", q1)] = A("T", retT_mms,
                                    deps=[E[("retadd", q1)],
                                          E.get(("v", q1 + 1))])
                E[("retTe", q1)] = A("S", lambda s, b=b, svr=svr: s.copy(
                    retT_s[b][:, :], pS_b[svr][:, 0:D]),
                    deps=[E[("retT", q1)], E.get(("wo", q1 - 2))])

            if 0 <= q3 < nt:
                b = q3 % 2
                svh = sv(q3)

                def h2T_mms(t, b=b, svh=svh):
                    r = None
                    for j in range(NCHUNK):
                        r = t.transpose(
                            pS_b[svh][:, j * TL:(j + 1) * TL],
                            h2_s[b][:, j * TL:(j + 1) * TL], idm[:, :])
                    return r
                E[("h2T", q3)] = A("T", h2T_mms,
                                   deps=[E[("h2", q3)], E.get(("v", q3 + 3)),
                                         E.get(("retTe", q3 + 2)),
                                         E.get(("quant", q3 - 2))])
                E[("quant", q3)] = A("S", lambda s, b=b, svh=svh: s.activation(
                    h2q_s[b][:, :], pS_b[svh][:, 0:D], AF.Identity,
                    scale=rsin[:, 0:1]),
                    deps=[E[("h2T", q3)], E.get(("ffn", q3 - 2)), e_rsin])

            if q0 < nt:
                b = q0 % 2

                def cumc_mms(t, b=b, q0=q0):
                    r = None
                    for j in range(2):
                        r = t.matmul(pCum[:, j * 512:(j + 1) * 512],
                                     usr_s[b][:, :],
                                     vcvs[b][:, j * 512:(j + 1) * 512],
                                     start=True, stop=(q0 == 0))
                    if q0 > 0:
                        for j in range(2):
                            r = t.matmul(pCum[:, j * 512:(j + 1) * 512],
                                         selm[0:64, :],
                                         cums[(q0 - 1) % 2][:, j * 512:(j + 1) * 512],
                                         start=False, stop=True)
                    return r
                E[("cumC", q0)] = A("T", cumc_mms,
                                    deps=[E[("vc", q0)], E[("usr", q0)],
                                          E.get(("t2", q0 - 1)),
                                          E.get(("cveS", q0 - 1)),
                                          E.get(("cveC", q0 - 1))]
                                    + ([e_w1rep] if q0 == 0 else []))

            if 0 <= q1 < nt:

                def wo_mms(t, b=q1 % 2, q1=q1):
                    r = None
                    for c in range(NCHUNK):
                        for j in range(2):
                            r = t.matmul(
                                pS[sa(q1)][:, j * 512:(j + 1) * 512],
                                retT_s[b][:, c * TL:(c + 1) * TL],
                                wot_s[:, c * D + j * 512:c * D + (j + 1) * 512],
                                start=(c == 0), stop=(c == NCHUNK - 1))
                    return r
                E[("wo", q1)] = A("T", wo_mms,
                                  deps=[E[("retTe", q1)],
                                        E.get(("quant", q1 - 2)),
                                        E.get(("t2act", q1 - 3)),
                                        E.get(("mv2", q1 - 3))]
                                  + ([e_wot] if q1 == 0 else []))

            if 0 <= q2 < nt:
                b = q2 % 2
                pslot = pS[sa(q2)]
                E[("bn2", q2)] = A("V", lambda v, b=b, pslot=pslot: v.bn_stats(
                    bnst[1][b][:, 0:6], pslot[:, 0:512]), deps=[E[("wo", q2)]])
                A("V", lambda v, b=b, pslot=pslot: v.bn_stats(
                    bnst[1][b][:, 6:12], pslot[:, 512:1024]),
                    deps=[E[("wo", q2)]])
                E[("mv2", q2)] = A("V", lambda v, b=b: v.bn_aggr(
                    mv[1][b][:, :], bnst[1][b][:, :]),
                    deps=[E.get(("vep2", q2 - 2))], tag="G")
                E[("vep2", q2)] = A("P", lambda g, b=b: g.tensor_scalar(
                    vep[1][b][:, :], mv[1][b][:, 1:2], 1.0, EPS, ALU.mult,
                    ALU.add), deps=[E[("mv2", q2)], E.get(("pow2", q2 - 2))])
                E[("pow2", q2)] = A("P", lambda g, b=b: g.tensor_tensor(
                    rcol[1][b][:, :], vep[1][b][:, :], mhalf[:, :], ALU.pow),
                    deps=[E[("vep2", q2)], E.get(("t2act", q2 - 2))])
                E[("nmr2", q2)] = A("P", lambda g, b=b: g.tensor_scalar(
                    nmr[1][b][:, :], rcol[1][b][:, :], mv[1][b][:, 0:1], -1.0,
                    ALU.mult, ALU.mult),
                    deps=[E[("pow2", q2)], E.get(("t2act", q2 - 2))])
                E[("t2act", q2)] = A("S", lambda s, b=b, pslot=pslot: s.activation(
                    t2s_s[0][:, :], pslot[:, :], AF.Identity,
                    scale=rcol[1][b][:, 0:1], bias=nmr[1][b][:, 0:1]),
                    deps=[E[("nmr2", q2)], E.get(("xp", q2 - 1))])

            if 0 <= q4 < nt:
                b = q4 % 2
                E[("bn4", q4)] = A("V", lambda v, b=b: v.bn_stats(
                    bnst[3][b][:, 0:6], ybf_s[b][:, 0:512]),
                    deps=[E[("deq", q4)]])
                A("V", lambda v, b=b: v.bn_stats(
                    bnst[3][b][:, 6:12], ybf_s[b][:, 512:1024]),
                    deps=[E[("deq", q4)]])
                E[("mv4", q4)] = A("V", lambda v, b=b: v.bn_aggr(
                    mv[3][b][:, :], bnst[3][b][:, :]),
                    deps=[E.get(("vep4", q4 - 2))], tag="G")

            if q0 < nt:
                b = q0 % 2
                E[("cveC", q0)] = A("S", lambda s, b=b: s.copy(
                    cums[b][:, 0:D], pCum[64:128, :]),
                    deps=[E[("cumC", q0)], E.get(("cumC", q0 - 1))])

            if 0 <= q4 < nt:
                b = q4 % 2
                E[("vep4", q4)] = A("P", lambda g, b=b: g.tensor_scalar(
                    vep[3][b][:, :], mv[3][b][:, 1:2], 1.0, EPS, ALU.mult,
                    ALU.add), deps=[E[("mv4", q4)], E.get(("pow4", q4 - 2))])
                E[("pow4", q4)] = A("P", lambda g, b=b: g.tensor_tensor(
                    rcol[3][b][:, :], vep[3][b][:, :], mhalf[:, :], ALU.pow),
                    deps=[E[("vep4", q4)], E.get(("ln4", q4 - 2))])
                E[("nmr4", q4)] = A("P", lambda g, b=b: g.tensor_scalar(
                    nmr[3][b][:, :], rcol[3][b][:, :], mv[3][b][:, 0:1], -1.0,
                    ALU.mult, ALU.mult),
                    deps=[E[("pow4", q4)], E.get(("ln4", q4 - 2))])
                E[("ln4", q4)] = A("S", lambda s, b=b: s.activation(
                    ln4_s[0][:, :], ybf_s[b][:, :], AF.Identity,
                    scale=rcol[3][b][:, 0:1], bias=nmr[3][b][:, 0:1]),
                    deps=[E[("nmr4", q4)], E.get(("outadd", q4 - 1))]
                    + ([e_wot] if q4 == 0 else []))

            for qR in ([] if p == 0 else [q0 + 1]):
                if qR >= nt:
                    continue
                b = qR % 2
                E[("bn1", qR)] = A("V", lambda v, b=b, qR=qR: v.bn_stats(
                    bnst[0][b][:, 0:6], xtok_s[qR % 4][:, 0:512]),
                    deps=[E[("xl", qR)]])
                A("V", lambda v, b=b, qR=qR: v.bn_stats(
                    bnst[0][b][:, 6:12], xtok_s[qR % 4][:, 512:1024]),
                    deps=[E[("xl", qR)]])
                E[("mv1", qR)] = A("V", lambda v, b=b: v.bn_aggr(
                    mv[0][b][:, :], bnst[0][b][:, :]),
                    deps=[E.get(("v", qR - 2)), E.get(("vep1", qR - 2))],
                    tag="G")
                E[("vep1", qR)] = A("P", lambda g, b=b: g.tensor_scalar(
                    vep[0][b][:, :], mv[0][b][:, 1:2], 1.0, EPS, ALU.mult,
                    ALU.add), deps=[E[("mv1", qR)], E.get(("pow1", qR - 2))])
                E[("pow1", qR)] = A("P", lambda g, b=b: g.tensor_tensor(
                    rcol[0][b][:, :], vep[0][b][:, :], mhalf[:, :], ALU.pow),
                    deps=[E[("vep1", qR)], E.get(("usr", qR - 2))])

            for qV in ([] if p == 0 else [q0 + 2]):
                if qV >= nt:
                    continue
                E[("u", qV)] = A("V", lambda v, qV=qV: v.tensor_scalar(
                    u_s[:, :], f2pi[:, :], tcol_s[qV][:, 0:1], None, ALU.mult),
                    deps=[E.get(("k1", qV - 1)), E[("tc", qV)]])
                E[("k1", qV)] = A("P", lambda g: g.tensor_scalar(
                    k_s[:, :], u_s[:, :], 1.0, MAGIC, ALU.mult, ALU.add),
                    deps=[E[("u", qV)], E.get(("w", qV - 1))])
                E[("k", qV)] = A("P", lambda g: g.tensor_scalar(
                    k_s[:, :], k_s[:, :], 1.0, -MAGIC, ALU.mult, ALU.add),
                    deps=[E[("k1", qV)]])

            for qT in [q0 + 4]:
                if qT >= nt:
                    continue
                E[("tc", qT)] = A("V", lambda v, qT=qT: v.tensor_copy(
                    tcol_s[qT][:, :], tcall_s[:, qT:qT + 1]), deps=[e_tcall])

            if 0 <= q3 < nt:
                b = q3 % 2

                def ffn_mms(t, b=b, q3=q3):
                    r = None
                    for m in range(4):
                        for j in range(2):
                            lhsT = bass.AP(h2q_s[b], m * 256,
                                           [[NCHUNK * D // 8, TL], [TL, 2], [1, TL]])
                            rhs = bass.AP(wffq_s, m * 2 * D + j * 512,
                                          [[NCHUNK * D, TL], [D, 2], [1, 512]])
                            r = t.matmul(
                                pS[sy(q3)][:, j * 512:(j + 1) * 512],
                                lhsT, rhs, start=(m == 0), stop=False,
                                perf_mode=mybir.MatmulPerfMode.DoubleRow)
                    for j in range(2):
                        r = t.matmul(
                            pS[sy(q3)][:, j * 512:(j + 1) * 512],
                            onesb_row[0:1, :],
                            bdiv[0:1, j * 512:(j + 1) * 512],
                            start=False, stop=True)
                    return r
                E[("ffn", q3)] = A("T", ffn_mms,
                                   deps=[E[("quant", q3)], e_wffq,
                                         E.get(("t2act", q3 + 1)),
                                         E.get(("mv2", q3 + 1))]
                                   + ([e_bdiv, e_ms[4]] if q3 == 0 else []))

            if 0 <= q2 < nt:
                b = q2 % 2
                E[("xp", q2)] = A("V", lambda v, b=b, q2=q2: v.tensor_tensor(
                    xp_s[q2 % 3][:, :], xtok_s[q2 % 4][:, :], t2s_s[0][:, :],
                    ALU.add), deps=[E[("t2act", q2)],
                                    E.get(("outadd", q2 - 3))])
                E[("bn3", q2)] = A("V", lambda v, b=b, q2=q2: v.bn_stats(
                    bnst[2][b][:, 0:6], xp_s[q2 % 3][:, 0:512]),
                    deps=[E[("xp", q2)]])
                A("V", lambda v, b=b, q2=q2: v.bn_stats(
                    bnst[2][b][:, 6:12], xp_s[q2 % 3][:, 512:1024]),
                    deps=[E[("xp", q2)]])
                E[("mv3", q2)] = A("V", lambda v, b=b: v.bn_aggr(
                    mv[2][b][:, :], bnst[2][b][:, :]),
                    deps=[E.get(("vep3", q2 - 2))], tag="G")
                E[("vep3", q2)] = A("P", lambda g, b=b: g.tensor_scalar(
                    vep[2][b][:, :], mv[2][b][:, 1:2], 1.0, EPS, ALU.mult,
                    ALU.add), deps=[E[("mv3", q2)], E.get(("pow3", q2 - 2))])
                E[("pow3", q2)] = A("P", lambda g, b=b: g.tensor_tensor(
                    rcol[2][b][:, :], vep[2][b][:, :], mhalf[:, :], ALU.pow),
                    deps=[E[("vep3", q2)], E.get(("h2", q2 - 2))])
                E[("nmr3", q2)] = A("P", lambda g, b=b: g.tensor_scalar(
                    nmr[2][b][:, :], rcol[2][b][:, :], mv[2][b][:, 0:1], -1.0,
                    ALU.mult, ALU.mult),
                    deps=[E[("pow3", q2)], E.get(("h2", q2 - 2))])
                E[("h2", q2)] = A("S", lambda s, b=b, q2=q2: s.activation(
                    h2_s[b][:, :], xp_s[q2 % 3][:, :], AF.Identity,
                    scale=rcol[2][b][:, 0:1], bias=nmr[2][b][:, 0:1]),
                    deps=[E[("nmr3", q2)], E.get(("h2T", q2 - 2))])

            if q0 < nt:
                b = q0 % 2
                E[("t1", q0)] = A("V", lambda v, b=b: v.tensor_tensor(
                    ta_s[b][:, :], pCum[:, :], cosa[b][:, :], ALU.mult),
                    deps=[E[("cumC", q0)], E[("cveC", q0)], E[("cos", q0)],
                          E.get(("retadd", q0 - 2))])

            for qR in ([] if p == 0 else [q0 + 1]):
                if qR >= nt:
                    continue
                b = qR % 2
                E[("usr", qR)] = A("P", lambda g, b=b: g.tensor_scalar(
                    usr_s[b][:, :], usmat[:, :], rcol[0][b][:, 0:1], 1.0,
                    ALU.mult, ALU.mult),
                    deps=[E[("pow1", qR)], E.get(("cumS", qR - 2))])

            for qV in ([] if p == 0 else [q0 + 2]):
                if qV >= nt:
                    continue
                b = qV % 2
                E[("w", qV)] = A("V", lambda v, b=b: v.scalar_tensor_tensor(
                    w_s[b][:, :], k_s[:, :], -1.0, u_s[:, :], ALU.mult, ALU.add),
                    deps=[E[("k", qV)], E.get(("sin", qV - 2))])
                E[("wc", qV)] = A("V", lambda v, b=b: v.scalar_tensor_tensor(
                    wc_s[b][:, :], w_s[b][:, :], 0.25, w_s[b][:, :],
                    ALU.is_ge, ALU.subtract),
                    deps=[E.get(("cos", qV - 2))])

            for qS in ([] if p == 0 else [q0 + 1]):
                if qS >= nt:
                    continue
                b = qS % 2
                E[("sin", qS)] = A("S", lambda s, b=b: s.activation(
                    sina[b][:, :], w_s[b][:, :], AF.Sin, scale=TWO_PI),
                    deps=[E[("w", qS)], E.get(("t2", qS - 2)),
                          E.get(("vs", qS - 2))]
                    + ([e_ms[2]] if qS == 0 else []))
                E[("cos", qS)] = A("S", lambda s, b=b: s.activation(
                    cosa[b][:, :], wc_s[b][:, :], AF.Sin, scale=-TWO_PI,
                    bias=piby2[:, :]),
                    deps=[E[("wc", qS)], E.get(("t1", qS - 2)),
                          E.get(("vc", qS - 2))]
                    + ([e_ms[2]] if qS == 0 else []))

            if q0 < nt:
                b = q0 % 2

                def cums_mms(t, b=b, q0=q0):
                    r = None
                    for j in range(2):
                        r = t.matmul(pCum[:, j * 512:(j + 1) * 512],
                                     usr_s[b][:, :],
                                     vcvs[b][:, D + j * 512:D + (j + 1) * 512],
                                     start=True, stop=(q0 == 0))
                    if q0 > 0:
                        for j in range(2):
                            r = t.matmul(pCum[:, j * 512:(j + 1) * 512],
                                         selm[0:64, :],
                                         cums[(q0 - 1) % 2][:, D + j * 512:D + (j + 1) * 512],
                                         start=False, stop=True)
                    return r
                E[("cumS", q0)] = A("T", cums_mms,
                                    deps=[E[("vs", q0)], E[("t1", q0)],
                                          E[("cveC", q0)]])
                E[("t2", q0)] = A("V", lambda v, b=b: v.tensor_tensor(
                    tb_s[b][:, :], pCum[:, :], sina[b][:, :], ALU.mult),
                    deps=[E[("cumS", q0)], E[("sin", q0)],
                          E.get(("retadd", q0 - 2))])
                E[("retadd", q0)] = A("V", lambda v, b=b: v.tensor_tensor(
                    ret_s[b][:, :], ta_s[b][:, :], tb_s[b][:, :], ALU.add),
                    deps=[E[("t1", q0)], E[("t2", q0)],
                          E.get(("retT", q0 - 2))]
                    + ([e_wf_cast[6 + q0]] if q0 <= 1 else []))
                E[("cveS", q0)] = A("S", lambda s, b=b: s.copy(
                    cums[b][:, D:2 * D], pCum[64:128, :]),
                    deps=[E[("cumS", q0)], E[("t2", q0)],
                          E.get(("cumS", q0 - 1))])

            for qW in ([1] if p == 0 else [q0 + 1]):
                if qW >= nt:
                    continue

                def wv_mms(t, qW=qW):
                    r = None
                    for c in range(NCHUNK):
                        for j in range(2):
                            r = t.matmul(
                                pS[sv(qW)][:, j * 512:(j + 1) * 512],
                                xftb_s[qW % 2][:, c * TL:(c + 1) * TL],
                                wvt_s[:, c * D + j * 512:c * D + (j + 1) * 512],
                                start=(c == 0), stop=(c == NCHUNK - 1))
                    return r
                E[("wv", qW)] = A("T", wv_mms,
                                  deps=[E[("fcast", qW)], e_wvt,
                                        E.get(("deq", qW - 5)),
                                        E.get(("t2act", qW - 4)),
                                        E.get(("mv2", qW - 4))]
                                  + ([e_f2pi] if qW <= 1 else []))
            if 0 <= q4 < nt:
                b = q4 % 2
                E[("outadd", q4)] = A("P", lambda g, b=b, q4=q4: g.tensor_tensor(
                    out_s[b][:, :], xp_s[q4 % 3][:, :], ln4_s[0][:, :], ALU.add),
                    deps=[E[("ln4", q4)], E[("xp", q4)],
                          E.get(("ost", q4 - 2))]
                    + ([e_wo_cast[6 + q4]] if q4 <= 1 else []))
                E[("ost", q4)] = A("P", lambda g, b=b, q4=q4: g.dma_start(
                    out_d[q4 * TL:(q4 + 1) * TL, :], out_s[b][:, :]),
                    deps=[E[("outadd", q4)]], sem=f"DO{q4 % 2}", inc=16)

        with nc.Block() as block:

            @block.sync
            def _(e):
                sch.emit("D", e, sems, cut)
                for sname, cval in sorted(sch.dma_counts_upto(cut).items()):
                    e.wait_ge(sems[sname], cval)

            @block.vector
            def _(v):
                sch.emit("V", v, sems, cut, selfsync=("VSP" if racecheck else selfsync))

            @block.scalar
            def _(s):
                sch.emit("S", s, sems, cut, selfsync=("VSP" if racecheck else selfsync))

            @block.gpsimd
            def _(g):
                sch.emit("P", g, sems, cut, selfsync=("VSP" if racecheck else selfsync))

            @block.tensor
            def _(t):
                sch.emit("T", t, sems, cut, selfsync=("VSP" if racecheck else selfsync))

    return nc


def _host_layout(x_b, Wv, Wo, Wff, freqs, s_in, s_k, bias, nt):
    T = nt * TL
    xtok = np.ascontiguousarray(x_b)
    xft = np.ascontiguousarray(
        x_b.T.reshape(NCHUNK, TL, nt, TL).transpose(2, 1, 0, 3)
        .reshape(nt * TL, D))
    wvt = np.ascontiguousarray(Wv.T)
    wot = np.ascontiguousarray(Wo.T)
    wff = np.ascontiguousarray(Wff)
    tcols = (np.arange(TL, dtype=np.float32)[:, None]
             + TL * np.arange(nt, dtype=np.float32)[None, :])
    us = np.triu(np.ones((TL, TL), dtype=np.float32)).astype(ml_dtypes.bfloat16)
    idm = np.eye(TL, dtype=np.float32).astype(ml_dtypes.bfloat16)
    sel = np.zeros((TL, TL), dtype=np.float32)
    sel[63, :] = 1.0
    sel = sel.astype(ml_dtypes.bfloat16)
    return {
        "xtok": xtok, "xft": xft, "wvt": wvt, "wot": wot, "wff": wff,
        "freqs": np.ascontiguousarray(freqs.reshape(1, D)),
        "tcols": np.ascontiguousarray(tcols),
        "usmat": us, "idmat": idm, "selmat": sel,
        "s_in": s_in.reshape(1, 1), "s_k": s_k.reshape(1, 1),
        "biasv": np.ascontiguousarray(bias.reshape(1, D)),
    }


def kernel(L0_attn_x, L0_attn_Wk, L0_attn_Wv, L0_attn_Wo, L0_attn_freqs,
           L0_ffn_kernel, L0_ffn_input_scale, L0_ffn_kernel_scale, L0_ffn_bias):
    x = np.asarray(L0_attn_x)
    assert np.asarray(L0_attn_Wk).shape == np.asarray(L0_attn_Wv).shape
    B, T, Dd = x.shape
    nt = T // TL
    nc = build(nt)
    in_maps = []
    for b in range(B):
        in_maps.append(_host_layout(
            x[b], np.asarray(L0_attn_Wv), np.asarray(L0_attn_Wo),
            np.asarray(L0_ffn_kernel), np.asarray(L0_attn_freqs),
            np.asarray(L0_ffn_input_scale), np.asarray(L0_ffn_kernel_scale),
            np.asarray(L0_ffn_bias), nt))
    res = run_bass_kernel_spmd(nc, in_maps, core_ids=list(range(B)))
    out = np.stack([np.asarray(r["out"]) for r in res.results], axis=0)
    return out.astype(np.float32)


# version 64
# speedup vs baseline: 1.2193x; 1.0060x over previous
import numpy as np
import ml_dtypes
import concourse.bass as bass
import concourse.mybir as mybir
from concourse.bass_utils import run_bass_kernel_spmd
from contextlib import ExitStack

F32 = mybir.dt.float32
BF16 = mybir.dt.bfloat16
F8 = mybir.dt.float8e4
I32 = mybir.dt.int32
AF = mybir.ActivationFunctionType
ALU = mybir.AluOpType

MAGIC = 12582912.0
TWO_PI = 6.283185307179586
EPS = 1e-5
D = 1024
TL = 128
NCHUNK = 8

ENGSEM = {"T": "semT", "V": "semV", "S": "semS", "P": "semP"}


class Sched:

    def __init__(self):
        self.q = {k: [] for k in ["T", "V", "S", "P", "D"]}
        self.counts = {}
        self.sem_names = set()
        self.seq = 0

    def add(self, eng, fn, deps=(), sem=None, inc=None, tag=""):
        if sem is None:
            sem = ENGSEM[eng]
        if inc is None:
            inc = 16 if eng == "D" else 1
        self.sem_names.add(sem)
        self.counts[sem] = self.counts.get(sem, 0) + inc
        ev = (sem, self.counts[sem])
        deps = [d for d in deps if d is not None]
        self.seq += 1
        self.q[eng].append((deps, fn, sem, inc, self.seq, tag))
        return ev

    def dma_counts_upto(self, cut):
        out = {}
        engsems = set(ENGSEM.values())
        for qk in self.q:
            for (deps, fn, sem, inc, seq, tag) in self.q[qk]:
                if sem in engsems:
                    continue
                if cut is None or seq <= cut:
                    out[sem] = out.get(sem, 0) + inc
        return out

    def emit(self, eng_key, engine, sems, cut=None, selfsync=""):
        floors = {}
        own = ENGSEM.get(eng_key)
        own_count = 0
        for deps, fn, sem, inc, seq, tag in self.q[eng_key]:
            if cut is not None and seq > cut:
                continue
            need = {}
            for (s, v) in deps:
                if v > need.get(s, 0):
                    need[s] = v
            ssync = (eng_key in selfsync) or (tag and tag in selfsync)
            if ssync and own is not None \
                    and own_count > 0 and own_count > need.get(own, 0):
                need[own] = own_count
            for s, v in sorted(need.items()):
                if v > floors.get(s, 0):
                    engine.wait_ge(sems[s], v)
                    floors[s] = v
            instr = fn(engine)
            instr.then_inc(sems[sem], inc)
            if sem == own:
                own_count += inc


def build(nt=32, cut=None, racecheck=False, selfsync="G"):
    nc = bass.Bass("TRN2", target_bir_lowering=False, debug=False,
                   detect_race_conditions=racecheck)
    T = nt * TL

    xtok_d = nc.dram_tensor("xtok", [T, D], F32, kind="ExternalInput")
    xft_d = nc.dram_tensor("xft", [nt * TL, D], F32, kind="ExternalInput")
    wvt_d = nc.dram_tensor("wvt", [NCHUNK * TL, D], F32, kind="ExternalInput")
    wot_d = nc.dram_tensor("wot", [NCHUNK * TL, D], F32, kind="ExternalInput")
    wff_d = nc.dram_tensor("wff", [NCHUNK * TL, D], BF16, kind="ExternalInput")
    freqs_d = nc.dram_tensor("freqs", [1, D], F32, kind="ExternalInput")
    tcols_d = nc.dram_tensor("tcols", [TL, nt], F32, kind="ExternalInput")
    us_d = nc.dram_tensor("usmat", [TL, TL], BF16, kind="ExternalInput")
    id_d = nc.dram_tensor("idmat", [TL, TL], BF16, kind="ExternalInput")
    sel_d = nc.dram_tensor("selmat", [TL, TL], BF16, kind="ExternalInput")
    sin_d = nc.dram_tensor("s_in", [1, 1], F32, kind="ExternalInput")
    sk_d = nc.dram_tensor("s_k", [1, 1], F32, kind="ExternalInput")
    bias_d = nc.dram_tensor("biasv", [1, D], BF16, kind="ExternalInput")
    out_d = nc.dram_tensor("out", [T, D], F32, kind="ExternalOutput")

    sch = Sched()
    with ExitStack() as st:
        def sb(name, shape, dt):
            return st.enter_context(nc.sbuf_tensor(name, shape, dt))

        sems = {}
        for s in ["semT", "semV", "semS", "semP", "DW", "DWS", "DWT",
                  "DWU", "DWV", "DW4", "DW5", "DW6", "DW7",
                  "DWA", "DWB", "DWC", "DWD", "DWE", "DWF", "DWG", "DWH",
                  "DX0", "DX1", "DX2", "DX3", "DF0", "DF1", "DO0", "DO1"]:
            sems[s] = st.enter_context(nc.semaphore(s))

        wstage = sb("wstage", [TL, 1024], F32)
        tcall_s = sb("tcall", [TL, nt], F32)
        wvt_s = sb("wvt_s", [TL, NCHUNK * D], BF16)
        wot_s = sb("wot_s", [TL, NCHUNK * D], BF16)
        wffq_s = sb("wffq_s", [TL, NCHUNK * D], F8)
        f2pi = sb("f2pi", [TL, D], F32)
        frow = sb("frow", [1, D], F32)
        tcol_s = [sb(f"tc{i}", [TL, 1], F32) for i in range(nt)]
        usmat = sb("usmat_s", [TL, TL], BF16)
        idm = sb("idm", [TL, TL], BF16)
        selm = sb("selm", [TL, TL], BF16)
        onesf = sb("onesf", [1, TL], F32)
        onesb_row = sb("onesb_row", [1, TL], BF16)
        onescol_b = sb("onescol_b", [TL, 1], BF16)
        w1n_rep = sb("w1n_rep", [TL, D], BF16)
        w1row = sb("w1row", [1, D], BF16)
        bdiv = sb("bdiv", [1, D], BF16)
        rr = sb("rr", [1, 1], F32)
        brow = sb("brow", [1, D], BF16)
        sincol = sb("sincol", [TL, 1], F32)
        skcol = sb("skcol", [TL, 1], F32)
        rsin = sb("rsin", [TL, 1], F32)
        rsk = sb("rsk", [TL, 1], F32)
        sisk = sb("sisk", [TL, 1], F32)
        piby2 = sb("piby2", [TL, 1], F32)
        mhalf = sb("mhalf", [TL, 1], F32)
        srow = sb("srow", [1, 2], F32)

        xtok_s = [sb(f"xtok{i}", [TL, D], F32) for i in range(4)]
        xftf_s = [sb(f"xftf{i}", [TL, D], F32) for i in range(2)]
        xftb_s = [sb(f"xftb{i}", [TL, D], BF16) for i in range(2)]
        u_s = sb("u_s", [TL, D], F32)
        k_s = sb("k_s", [TL, D], F32)
        w_s = [sb(f"w{i}", [TL, D], F32) for i in range(2)]
        wc_s = [sb(f"wc{i}", [TL, D], F32) for i in range(2)]
        sina = [sb(f"sina{i}", [TL, D], BF16) for i in range(2)]
        cosa = [sb(f"cosa{i}", [TL, D], BF16) for i in range(2)]
        usr_s = [sb(f"usr{i}", [TL, TL], BF16) for i in range(2)]
        v_s = [sb(f"v{i}", [TL, D], BF16) for i in range(2)]
        vcvs = [sb(f"vcvs{i}", [TL, 2 * D], BF16) for i in range(2)]
        ta_s = [sb(f"ta{i}", [TL, D], BF16) for i in range(2)]
        tb_s = [sb(f"tb{i}", [TL, D], BF16) for i in range(2)]
        ret_s = [sb(f"ret{i}", [TL, D], BF16) for i in range(2)]
        retT_s = [sb(f"retT{i}", [TL, D], BF16) for i in range(2)]
        cums = [sb(f"cums{i}", [64, 2 * D], BF16) for i in range(2)]
        t2s_s = [sb(f"t2s{i}", [TL, D], F32) for i in range(2)]
        xp_s = [sb(f"xp{i}", [TL, D], F32) for i in range(3)]
        h2_s = [sb(f"h2_{i}", [TL, D], BF16) for i in range(2)]
        h2q_s = [sb(f"h2q{i}", [TL, D], F8) for i in range(2)]
        ybf_s = [sb(f"ybf{i}", [TL, D], BF16) for i in range(2)]
        ln4_s = [sb("ln4_0", [TL, D], F32)]
        wstage2 = ln4_s[0]
        out_s = [sb(f"outs{i}", [TL, D], F32) for i in range(2)]
        bnst = [[sb(f"bnst{ln}_{i}", [TL, 12], F32) for i in range(2)]
                for ln in range(4)]
        mv = [[sb(f"mv{ln}_{i}", [TL, 2], F32) for i in range(2)]
              for ln in range(4)]
        vep = [[sb(f"vep{ln}_{i}", [TL, 1], F32) for i in range(2)]
               for ln in range(4)]
        rcol = [[sb(f"r{ln}_{i}", [TL, 1], F32) for i in range(2)]
                for ln in range(4)]
        nmr = [[sb(f"nmr{ln}_{i}", [TL, 1], F32) for i in range(2)]
               for ln in range(4)]

        pS = [st.enter_context(nc.psum_tensor("pA", [TL, D], F32)),
              st.enter_context(nc.psum_tensor("pB", [TL, D], F32)),
              st.enter_context(nc.psum_tensor("pC", [TL, D], F32))]
        pCum = st.enter_context(nc.psum_tensor("pCum", [TL, D], F32))
        pS_b = [p.ap().bitcast(BF16) for p in pS]

        A = sch.add
        pre_loads = []
        for dst, src in [(usmat, us_d), (idm, id_d),
                         (selm, sel_d), (frow, freqs_d), (brow, bias_d)]:
            pre_loads.append(A("D", lambda e, d=dst, s=src: e.dma_start(d[:, :], s[:, :]),
                               deps=(), sem="DW"))
        e_tcall = A("D", lambda e: e.dma_start(tcall_s[:, :], tcols_d[:, :]),
                    deps=(), sem="DW")
        e_sin_l = A("D", lambda e: e.dma_start(srow[0:1, 0:1], sin_d[:, :]), sem="DW")
        e_sk_l = A("D", lambda e: e.dma_start(srow[0:1, 1:2], sk_d[:, :]), sem="DW")
        pre_loads += [e_tcall, e_sin_l, e_sk_l]
        e_dw_all = [e_sk_l]

        e_ms = [
            A("V", lambda v: v.memset(onesf[:, :], 1.0)),
            A("V", lambda v: v.memset(onescol_b[:, :], 1.0)),
            A("V", lambda v: v.memset(piby2[:, :], TWO_PI / 4.0)),
            A("P", lambda g: g.memset(mhalf[:, :], -0.5)),
            A("V", lambda v: v.memset(onesb_row[:, :], 1.0)),
        ]

        E = {}
        E[("xl", 0)] = A("D", lambda e: e.dma_start(
            xtok_s[0][:, :], xtok_d[0:TL, :]), deps=(), sem="DX0")
        E[("fl", 0)] = A("D", lambda e: e.dma_start(
            xftf_s[0][:, :], xft_d[0:TL, :]), deps=(), sem="DF0")
        E[("xl", 1)] = A("D", lambda e: e.dma_start(
            xtok_s[1][:, :], xtok_d[TL:2 * TL, :]), deps=(), sem="DX1")
        E[("fl", 1)] = A("D", lambda e: e.dma_start(
            xftf_s[1][:, :], xft_d[TL:2 * TL, :]), deps=(), sem="DF1")
        E[("fcast", 0)] = A("P", lambda g: g.tensor_scalar(
            xftb_s[0][:, :], xftf_s[0][:, :], 1.0, 1.0, ALU.mult,
            ALU.mult), deps=[E[("fl", 0)]])
        E[("fcast", 1)] = A("P", lambda g: g.tensor_scalar(
            xftb_s[1][:, :], xftf_s[1][:, :], 1.0, 1.0, ALU.mult,
            ALU.mult), deps=[E[("fl", 1)]])

        e_warm = A("S", lambda s_: s_.activation(
            ybf_s[0][0:1, 0:1], piby2[0:1, 0:1], AF.Sin, scale=0.0),
            deps=[e_ms[2]])

        e_sbcast = A("T", lambda t: t.matmul(
            pS[0][:, 0:2], onesf[:, :], srow[0:1, 0:2], start=True, stop=True),
            deps=e_dw_all + [e_ms[0]])
        e_scols = A("V", lambda v: v.tensor_copy(sincol[:, :], pS[0][:, 0:1]),
                    deps=[e_sbcast])
        e_scols2 = A("V", lambda v: v.tensor_copy(skcol[:, :], pS[0][:, 1:2]),
                     deps=[e_sbcast])
        e_rsin = A("V", lambda v: v.reciprocal(rsin[:, :], sincol[:, :]),
                   deps=[e_scols])
        e_rsk = A("V", lambda v: v.reciprocal(rsk[:, :], skcol[:, :]),
                  deps=[e_scols2])
        e_sisk = A("V", lambda v: v.tensor_tensor(sisk[:, :], sincol[:, :],
                                                  skcol[:, :], ALU.mult),
                   deps=[e_scols, e_scols2])

        e_fb = [A("T", lambda t, j=j: t.matmul(
            pS[1][:, j * 512:(j + 1) * 512], onesf[:, :],
            frow[0:1, j * 512:(j + 1) * 512], start=True, stop=True),
            deps=e_dw_all + [e_ms[0], e_scols, e_scols2]) for j in range(2)]
        e_f2pi = A("V", lambda v: v.tensor_scalar(
            f2pi[:, :], pS[1][:, :], 1.0 / TWO_PI, None, ALU.mult), deps=e_fb)

        e_rr = A("V", lambda v: v.tensor_tensor(
            rr[0:1, :], rsin[0:1, 0:1], rsk[0:1, 0:1], ALU.mult),
            deps=[e_rsin, e_rsk])
        e_bdiv = A("V", lambda v: v.tensor_scalar(
            bdiv[0:1, :], brow[0:1, :], rr[0:1, 0:1], None, ALU.mult),
            deps=[e_rr] + e_dw_all)

        wstage_bf = wstage.ap().bitcast(BF16)
        wstage2_bf = wstage2.ap().bitcast(BF16)
        stg = [wstage, wstage2]
        stg_b = [wstage_bf, wstage2_bf]
        e_wv_cast = []
        e_wv0 = None
        stgv = [wstage, wstage2, out_s[0], out_s[1]]
        wv_sems = [["DWS", "DWT", "DWU", "DWV"],
                   ["DW4", "DW5", "DW6", "DW7"],
                   ["DWA", "DWB", "DWC", "DWD"],
                   ["DWE", "DWF", "DWG", "DWH"]]
        for c in range(8):
            dep = [e_wv_cast[c - 4]] if c >= 4 else []
            lds = []
            for h in range(4):
                lds.append(A("D", lambda e, c=c, h=h: e.dma_start(
                    stgv[c % 4][:, h * 256:(h + 1) * 256],
                    wvt_d[c * TL:(c + 1) * TL, h * 256:(h + 1) * 256]),
                    deps=dep, sem=wv_sems[c % 4][h]))
            e_wv_cast.append(A("V", lambda v, c=c: v.tensor_scalar(
                wvt_s[:, c * D:(c + 1) * D], stgv[c % 4][:, :],
                1.0, None, ALU.mult), deps=lds))

            def w1c_mms(t, c=c):
                r = None
                for j in range(2):
                    r = t.matmul(pCum[0:1, j * 512:(j + 1) * 512],
                                 onescol_b[:, :],
                                 wvt_s[:, c * D + j * 512:c * D + (j + 1) * 512],
                                 start=(c == 0), stop=(c == NCHUNK - 1))
                return r
            e_w1mm = A("T", w1c_mms,
                       deps=[e_wv_cast[c]] + ([e_ms[1]] if c == 0 else []))

            def wv0c_mms(t, c=c):
                r = None
                for j in range(2):
                    r = t.matmul(pS[0][:, j * 512:(j + 1) * 512],
                                 xftb_s[0][:, c * TL:(c + 1) * TL],
                                 wvt_s[:, c * D + j * 512:c * D + (j + 1) * 512],
                                 start=(c == 0), stop=(c == NCHUNK - 1))
                return r
            e_wv0 = A("T", wv0c_mms,
                      deps=[e_wv_cast[c]]
                      + ([E[("fcast", 0)], e_scols, e_scols2] if c == 0 else []))
        e_wvt = e_wv_cast[7]
        E[("wv", 0)] = e_wv0

        e_w1row = A("V", lambda v: v.tensor_scalar(
            w1row[0:1, :], pCum[0:1, 0:D], -1.0, None, ALU.mult), deps=[e_w1mm])
        e_w1b = [A("T", lambda t, j=j: t.matmul(
            pCum[:, j * 512:(j + 1) * 512], onesb_row[0:1, :],
            w1row[0:1, j * 512:(j + 1) * 512], start=True, stop=True),
            deps=[e_w1row, e_ms[4]]) for j in range(2)]
        e_w1rep = A("V", lambda v: v.tensor_scalar(
            w1n_rep[:, :], pCum[:, :], 1.0, None, ALU.mult), deps=e_w1b)

        for i in range(min(4, nt)):
            E[("tc", i)] = A("V", lambda v, i=i: v.tensor_copy(
                tcol_s[i][:, :], tcall_s[:, i:i + 1]), deps=[e_tcall])
        e_wot = None
        e_wffq = None


        def sv(q):
            return q % 3

        def sa(q):
            return (q + 1) % 3

        def sy(q):
            return (q + 2) % 3

        for p in range(nt + 5):
            q0, q1, q2, q3, q4 = p, p - 1, p - 2, p - 3, p - 4

            if 2 <= p + 1 < nt:
                qq = p + 1
                E[("xl", qq)] = A("D", lambda e, qq=qq: e.dma_start(
                    xtok_s[qq % 4][:, :], xtok_d[qq * TL:(qq + 1) * TL, :]),
                    deps=[E.get(("xp", qq - 4))],
                    sem=f"DX{qq % 4}")
                E[("fl", qq)] = A("D", lambda e, qq=qq: e.dma_start(
                    xftf_s[qq % 2][:, :], xft_d[qq * TL:(qq + 1) * TL, :]),
                    deps=[E.get(("fcast", qq - 2))],
                    sem=f"DF{qq % 2}")
                E[("fcast", qq)] = A("P", lambda g, qq=qq: g.tensor_scalar(
                    xftb_s[qq % 2][:, :], xftf_s[qq % 2][:, :], 1.0, 1.0,
                    ALU.mult, ALU.mult),
                    deps=[E[("fl", qq)], E.get(("wv", qq - 2))])

            if 0 <= q4 < nt:
                b = q4 % 2
                E[("deq", q4)] = A("S", lambda s, b=b, q4=q4: s.activation(
                    ybf_s[b][:, :], pS[sy(q4)][:, :], AF.Identity,
                    scale=sisk[:, 0:1]),
                    deps=[E[("ffn", q4)], E.get(("ln4", q4 - 2)),
                          E.get(("mv4", q4 - 2)), e_sisk]
                    + ([e_wf_cast[4 + q4]] if q4 <= 1 else []))

            for qR in ([0, 1] if p == 0 else []):
                if qR >= nt:
                    continue
                b = qR % 2
                E[("bn1", qR)] = A("V", lambda v, b=b, qR=qR: v.bn_stats(
                    bnst[0][b][:, 0:6], xtok_s[qR % 4][:, 0:512]),
                    deps=[E[("xl", qR)]])
                A("V", lambda v, b=b, qR=qR: v.bn_stats(
                    bnst[0][b][:, 6:12], xtok_s[qR % 4][:, 512:1024]),
                    deps=[E[("xl", qR)]])
                E[("mv1", qR)] = A("V", lambda v, b=b: v.bn_aggr(
                    mv[0][b][:, :], bnst[0][b][:, :]),
                    deps=[E.get(("v", qR - 2)), E.get(("vep1", qR - 2))],
                    tag="G")
                E[("vep1", qR)] = A("P", lambda g, b=b: g.tensor_scalar(
                    vep[0][b][:, :], mv[0][b][:, 1:2], 1.0, EPS, ALU.mult,
                    ALU.add), deps=[E[("mv1", qR)], E.get(("pow1", qR - 2))])
                E[("pow1", qR)] = A("P", lambda g, b=b: g.tensor_tensor(
                    rcol[0][b][:, :], vep[0][b][:, :], mhalf[:, :], ALU.pow),
                    deps=[E[("vep1", qR)], E.get(("usr", qR - 2))])
            for qV in ([0, 1] if p == 0 else []):
                if qV >= nt:
                    continue
                E[("u", qV)] = A("V", lambda v, qV=qV: v.tensor_scalar(
                    u_s[:, :], f2pi[:, :], tcol_s[qV][:, 0:1], None, ALU.mult),
                    deps=([e_f2pi] if qV == 0 else []) + [E.get(("k1", qV - 1)),
                          E[("tc", qV)]])
                E[("k1", qV)] = A("P", lambda g: g.tensor_scalar(
                    k_s[:, :], u_s[:, :], 1.0, MAGIC, ALU.mult, ALU.add),
                    deps=[E[("u", qV)], E.get(("w", qV - 1))])
                E[("k", qV)] = A("P", lambda g: g.tensor_scalar(
                    k_s[:, :], k_s[:, :], 1.0, -MAGIC, ALU.mult, ALU.add),
                    deps=[E[("k1", qV)]])
                b = qV % 2
                E[("w", qV)] = A("V", lambda v, b=b: v.scalar_tensor_tensor(
                    w_s[b][:, :], k_s[:, :], -1.0, u_s[:, :], ALU.mult, ALU.add),
                    deps=[E[("k", qV)], E.get(("sin", qV - 2))])
                E[("wc", qV)] = A("V", lambda v, b=b: v.scalar_tensor_tensor(
                    wc_s[b][:, :], w_s[b][:, :], 0.25, w_s[b][:, :],
                    ALU.is_ge, ALU.subtract),
                    deps=[E.get(("cos", qV - 2))])

            for qS in ([0, 1] if p == 0 else []):
                if qS >= nt:
                    continue
                b = qS % 2
                E[("sin", qS)] = A("S", lambda s, b=b: s.activation(
                    sina[b][:, :], w_s[b][:, :], AF.Sin, scale=TWO_PI),
                    deps=[E[("w", qS)]] + [e_ms[2]])
                E[("cos", qS)] = A("S", lambda s, b=b: s.activation(
                    cosa[b][:, :], wc_s[b][:, :], AF.Sin, scale=-TWO_PI,
                    bias=piby2[:, :]),
                    deps=[E[("wc", qS)]] + [e_ms[2]])

            for qV in ([2] if p == 0 else []):
                if qV >= nt:
                    continue
                E[("u", qV)] = A("V", lambda v, qV=qV: v.tensor_scalar(
                    u_s[:, :], f2pi[:, :], tcol_s[qV][:, 0:1], None, ALU.mult),
                    deps=[E.get(("k1", qV - 1)), E[("tc", qV)]])
                E[("k1", qV)] = A("P", lambda g: g.tensor_scalar(
                    k_s[:, :], u_s[:, :], 1.0, MAGIC, ALU.mult, ALU.add),
                    deps=[E[("u", qV)], E.get(("w", qV - 1))])
                E[("k", qV)] = A("P", lambda g: g.tensor_scalar(
                    k_s[:, :], k_s[:, :], 1.0, -MAGIC, ALU.mult, ALU.add),
                    deps=[E[("k1", qV)]])
                b = qV % 2
                E[("w", qV)] = A("V", lambda v, b=b: v.scalar_tensor_tensor(
                    w_s[b][:, :], k_s[:, :], -1.0, u_s[:, :], ALU.mult, ALU.add),
                    deps=[E[("k", qV)], E.get(("sin", qV - 2))])
                E[("wc", qV)] = A("V", lambda v, b=b: v.scalar_tensor_tensor(
                    wc_s[b][:, :], w_s[b][:, :], 0.25, w_s[b][:, :],
                    ALU.is_ge, ALU.subtract),
                    deps=[E.get(("cos", qV - 2))])

            for qR in ([0, 1] if p == 0 else []):
                if qR >= nt:
                    continue
                b = qR % 2
                E[("usr", qR)] = A("P", lambda g, b=b: g.tensor_scalar(
                    usr_s[b][:, :], usmat[:, :], rcol[0][b][:, 0:1], 1.0,
                    ALU.mult, ALU.mult),
                    deps=[E[("pow1", qR)]])

            if q0 < nt:
                b = q0 % 2
                E[("v", q0)] = A("V", lambda v, b=b, q0=q0: v.scalar_tensor_tensor(
                    v_s[b][:, :], w1n_rep[:, :], mv[0][b][:, 0:1],
                    pS[sv(q0)][:, :], ALU.mult, ALU.add),
                    deps=[E[("wv", q0)], E[("mv1", q0)]]
                    + ([e_w1rep] if q0 == 0 else []))
                E[("vc", q0)] = A("V", lambda v, b=b: v.tensor_tensor(
                    vcvs[b][:, 0:D], v_s[b][:, :], cosa[b][:, :], ALU.mult),
                    deps=[E[("v", q0)], E[("cos", q0)],
                          E.get(("cumC", q0 - 2))])
                E[("vs", q0)] = A("V", lambda v, b=b: v.tensor_tensor(
                    vcvs[b][:, D:2 * D], v_s[b][:, :], sina[b][:, :], ALU.mult),
                    deps=[E[("v", q0)], E[("sin", q0)],
                          E.get(("cumS", q0 - 2))])

            if p == 0:
                stgo = [wstage, wstage2, out_s[0], out_s[1]]
                wo_sems = ["DWA", "DWB", "DWC", "DWD"]
                e_wo_cast = []
                for c in range(8):
                    if c >= 4:
                        dep = [e_wo_cast[c - 4]]
                    else:
                        dep = [e_wvt]
                    ld = A("D", lambda e, c=c: e.dma_start(
                        stgo[c % 4][:, :], wot_d[c * TL:(c + 1) * TL, :]),
                        deps=dep, sem=wo_sems[c % 4])
                    e_wo_cast.append(A("P", lambda g, c=c: g.tensor_scalar(
                        wot_s[:, c * D:(c + 1) * D], stgo[c % 4][:, :],
                        1.0, 1.0, ALU.mult, ALU.mult), deps=[ld]))
                e_wot = e_wo_cast[7]

            if p == 0:
                stgf = [ybf_s[0], ybf_s[1], ret_s[0], ret_s[1]]
                wf_sems = ["DWE", "DWF", "DWG", "DWH"]
                e_wf_cast = []
                for c in range(8):
                    if c >= 4:
                        dep = [e_wf_cast[c - 4]]
                    elif c == 0:
                        dep = [e_warm, e_wvt]
                    else:
                        dep = [e_wvt]
                    ld = A("D", lambda e, c=c: e.dma_start(
                        stgf[c % 4][:, :], wff_d[c * TL:(c + 1) * TL, :]),
                        deps=dep, sem=wf_sems[c % 4])
                    e_wf_cast.append(A("P", lambda g, c=c: g.tensor_scalar(
                        wffq_s[:, c * D:(c + 1) * D], stgf[c % 4][:, :],
                        rsk[:, 0:1], 1.0, ALU.mult, ALU.mult),
                        deps=[ld, e_rsk]))
                e_wffq = e_wf_cast[7]


            if 0 <= q1 < nt:
                b = q1 % 2
                svr = sa(q1)

                def retT_mms(t, b=b, svr=svr):
                    r = None
                    for j in range(NCHUNK):
                        r = t.transpose(pS_b[svr][:, j * TL:(j + 1) * TL],
                                        ret_s[b][:, j * TL:(j + 1) * TL],
                                        idm[:, :])
                    return r
                E[("retT", q1)] = A("T", retT_mms,
                                    deps=[E[("retadd", q1)],
                                          E.get(("v", q1 + 1))])
                E[("retTe", q1)] = A("S", lambda s, b=b, svr=svr: s.copy(
                    retT_s[b][:, :], pS_b[svr][:, 0:D]),
                    deps=[E[("retT", q1)], E.get(("wo", q1 - 2))])

            if 0 <= q3 < nt:
                b = q3 % 2
                svh = sv(q3)

                def h2T_mms(t, b=b, svh=svh):
                    r = None
                    for j in range(NCHUNK):
                        r = t.transpose(
                            pS_b[svh][:, j * TL:(j + 1) * TL],
                            h2_s[b][:, j * TL:(j + 1) * TL], idm[:, :])
                    return r
                E[("h2T", q3)] = A("T", h2T_mms,
                                   deps=[E[("h2", q3)], E.get(("v", q3 + 3)),
                                         E.get(("retTe", q3 + 2)),
                                         E.get(("quant", q3 - 2))])
                E[("quant", q3)] = A("S", lambda s, b=b, svh=svh: s.activation(
                    h2q_s[b][:, :], pS_b[svh][:, 0:D], AF.Identity,
                    scale=rsin[:, 0:1]),
                    deps=[E[("h2T", q3)], E.get(("ffn", q3 - 2)), e_rsin])

            if q0 < nt:
                b = q0 % 2

                def cumc_mms(t, b=b, q0=q0):
                    r = None
                    for j in range(2):
                        r = t.matmul(pCum[:, j * 512:(j + 1) * 512],
                                     usr_s[b][:, :],
                                     vcvs[b][:, j * 512:(j + 1) * 512],
                                     start=True, stop=(q0 == 0))
                    if q0 > 0:
                        for j in range(2):
                            r = t.matmul(pCum[:, j * 512:(j + 1) * 512],
                                         selm[0:64, :],
                                         cums[(q0 - 1) % 2][:, j * 512:(j + 1) * 512],
                                         start=False, stop=True)
                    return r
                E[("cumC", q0)] = A("T", cumc_mms,
                                    deps=[E[("vc", q0)], E[("usr", q0)],
                                          E.get(("t2", q0 - 1)),
                                          E.get(("cveS", q0 - 1)),
                                          E.get(("cveC", q0 - 1))]
                                    + ([e_w1rep] if q0 == 0 else []))

            if 0 <= q1 < nt:

                def wo_mms(t, b=q1 % 2, q1=q1):
                    r = None
                    for c in range(NCHUNK):
                        for j in range(2):
                            r = t.matmul(
                                pS[sa(q1)][:, j * 512:(j + 1) * 512],
                                retT_s[b][:, c * TL:(c + 1) * TL],
                                wot_s[:, c * D + j * 512:c * D + (j + 1) * 512],
                                start=(c == 0), stop=(c == NCHUNK - 1))
                    return r
                E[("wo", q1)] = A("T", wo_mms,
                                  deps=[E[("retTe", q1)],
                                        E.get(("quant", q1 - 2)),
                                        E.get(("t2act", q1 - 3)),
                                        E.get(("mv2", q1 - 3))]
                                  + ([e_wot] if q1 == 0 else []))

            if 0 <= q2 < nt:
                b = q2 % 2
                pslot = pS[sa(q2)]
                E[("bn2", q2)] = A("V", lambda v, b=b, pslot=pslot: v.bn_stats(
                    bnst[1][b][:, 0:6], pslot[:, 0:512]), deps=[E[("wo", q2)]])
                A("V", lambda v, b=b, pslot=pslot: v.bn_stats(
                    bnst[1][b][:, 6:12], pslot[:, 512:1024]),
                    deps=[E[("wo", q2)]])
                E[("mv2", q2)] = A("V", lambda v, b=b: v.bn_aggr(
                    mv[1][b][:, :], bnst[1][b][:, :]),
                    deps=[E.get(("vep2", q2 - 2))], tag="G")
                E[("vep2", q2)] = A("P", lambda g, b=b: g.tensor_scalar(
                    vep[1][b][:, :], mv[1][b][:, 1:2], 1.0, EPS, ALU.mult,
                    ALU.add), deps=[E[("mv2", q2)], E.get(("pow2", q2 - 2))])
                E[("pow2", q2)] = A("P", lambda g, b=b: g.tensor_tensor(
                    rcol[1][b][:, :], vep[1][b][:, :], mhalf[:, :], ALU.pow),
                    deps=[E[("vep2", q2)], E.get(("t2act", q2 - 2))])
                E[("nmr2", q2)] = A("P", lambda g, b=b: g.tensor_scalar(
                    nmr[1][b][:, :], rcol[1][b][:, :], mv[1][b][:, 0:1], -1.0,
                    ALU.mult, ALU.mult),
                    deps=[E[("pow2", q2)], E.get(("t2act", q2 - 2))])
                E[("t2act", q2)] = A("S", lambda s, b=b, pslot=pslot: s.activation(
                    t2s_s[b][:, :], pslot[:, :], AF.Identity,
                    scale=rcol[1][b][:, 0:1], bias=nmr[1][b][:, 0:1]),
                    deps=[E[("nmr2", q2)], E.get(("xp", q2 - 2))])

            if 0 <= q4 < nt:
                b = q4 % 2
                E[("bn4", q4)] = A("V", lambda v, b=b: v.bn_stats(
                    bnst[3][b][:, 0:6], ybf_s[b][:, 0:512]),
                    deps=[E[("deq", q4)]])
                A("V", lambda v, b=b: v.bn_stats(
                    bnst[3][b][:, 6:12], ybf_s[b][:, 512:1024]),
                    deps=[E[("deq", q4)]])
                E[("mv4", q4)] = A("V", lambda v, b=b: v.bn_aggr(
                    mv[3][b][:, :], bnst[3][b][:, :]),
                    deps=[E.get(("vep4", q4 - 2))], tag="G")

            if q0 < nt:
                b = q0 % 2
                E[("cveC", q0)] = A("S", lambda s, b=b: s.copy(
                    cums[b][:, 0:D], pCum[64:128, :]),
                    deps=[E[("cumC", q0)], E.get(("cumC", q0 - 1))])

            if 0 <= q4 < nt:
                b = q4 % 2
                E[("vep4", q4)] = A("P", lambda g, b=b: g.tensor_scalar(
                    vep[3][b][:, :], mv[3][b][:, 1:2], 1.0, EPS, ALU.mult,
                    ALU.add), deps=[E[("mv4", q4)], E.get(("pow4", q4 - 2))])
                E[("pow4", q4)] = A("P", lambda g, b=b: g.tensor_tensor(
                    rcol[3][b][:, :], vep[3][b][:, :], mhalf[:, :], ALU.pow),
                    deps=[E[("vep4", q4)], E.get(("ln4", q4 - 2))])
                E[("nmr4", q4)] = A("P", lambda g, b=b: g.tensor_scalar(
                    nmr[3][b][:, :], rcol[3][b][:, :], mv[3][b][:, 0:1], -1.0,
                    ALU.mult, ALU.mult),
                    deps=[E[("pow4", q4)], E.get(("ln4", q4 - 2))])
                E[("ln4", q4)] = A("S", lambda s, b=b: s.activation(
                    ln4_s[0][:, :], ybf_s[b][:, :], AF.Identity,
                    scale=rcol[3][b][:, 0:1], bias=nmr[3][b][:, 0:1]),
                    deps=[E[("nmr4", q4)], E.get(("outadd", q4 - 1))]
                    + ([e_wot] if q4 == 0 else []))

            for qR in ([] if p == 0 else [q0 + 1]):
                if qR >= nt:
                    continue
                b = qR % 2
                E[("bn1", qR)] = A("V", lambda v, b=b, qR=qR: v.bn_stats(
                    bnst[0][b][:, 0:6], xtok_s[qR % 4][:, 0:512]),
                    deps=[E[("xl", qR)]])
                A("V", lambda v, b=b, qR=qR: v.bn_stats(
                    bnst[0][b][:, 6:12], xtok_s[qR % 4][:, 512:1024]),
                    deps=[E[("xl", qR)]])
                E[("mv1", qR)] = A("V", lambda v, b=b: v.bn_aggr(
                    mv[0][b][:, :], bnst[0][b][:, :]),
                    deps=[E.get(("v", qR - 2)), E.get(("vep1", qR - 2))],
                    tag="G")
                E[("vep1", qR)] = A("P", lambda g, b=b: g.tensor_scalar(
                    vep[0][b][:, :], mv[0][b][:, 1:2], 1.0, EPS, ALU.mult,
                    ALU.add), deps=[E[("mv1", qR)], E.get(("pow1", qR - 2))])
                E[("pow1", qR)] = A("P", lambda g, b=b: g.tensor_tensor(
                    rcol[0][b][:, :], vep[0][b][:, :], mhalf[:, :], ALU.pow),
                    deps=[E[("vep1", qR)], E.get(("usr", qR - 2))])

            for qV in ([] if p == 0 else [q0 + 2]):
                if qV >= nt:
                    continue
                E[("u", qV)] = A("V", lambda v, qV=qV: v.tensor_scalar(
                    u_s[:, :], f2pi[:, :], tcol_s[qV][:, 0:1], None, ALU.mult),
                    deps=[E.get(("k1", qV - 1)), E[("tc", qV)]])
                E[("k1", qV)] = A("P", lambda g: g.tensor_scalar(
                    k_s[:, :], u_s[:, :], 1.0, MAGIC, ALU.mult, ALU.add),
                    deps=[E[("u", qV)], E.get(("w", qV - 1))])
                E[("k", qV)] = A("P", lambda g: g.tensor_scalar(
                    k_s[:, :], k_s[:, :], 1.0, -MAGIC, ALU.mult, ALU.add),
                    deps=[E[("k1", qV)]])

            for qT in [q0 + 4]:
                if qT >= nt:
                    continue
                E[("tc", qT)] = A("V", lambda v, qT=qT: v.tensor_copy(
                    tcol_s[qT][:, :], tcall_s[:, qT:qT + 1]), deps=[e_tcall])

            if 0 <= q3 < nt:
                b = q3 % 2

                def ffn_mms(t, b=b, q3=q3):
                    r = None
                    for m in range(4):
                        for j in range(2):
                            lhsT = bass.AP(h2q_s[b], m * 256,
                                           [[NCHUNK * D // 8, TL], [TL, 2], [1, TL]])
                            rhs = bass.AP(wffq_s, m * 2 * D + j * 512,
                                          [[NCHUNK * D, TL], [D, 2], [1, 512]])
                            r = t.matmul(
                                pS[sy(q3)][:, j * 512:(j + 1) * 512],
                                lhsT, rhs, start=(m == 0), stop=False,
                                perf_mode=mybir.MatmulPerfMode.DoubleRow)
                    for j in range(2):
                        r = t.matmul(
                            pS[sy(q3)][:, j * 512:(j + 1) * 512],
                            onesb_row[0:1, :],
                            bdiv[0:1, j * 512:(j + 1) * 512],
                            start=False, stop=True)
                    return r
                E[("ffn", q3)] = A("T", ffn_mms,
                                   deps=[E[("quant", q3)], e_wffq,
                                         E.get(("t2act", q3 + 1)),
                                         E.get(("mv2", q3 + 1))]
                                   + ([e_bdiv, e_ms[4]] if q3 == 0 else []))

            if 0 <= q2 < nt:
                b = q2 % 2
                E[("xp", q2)] = A("V", lambda v, b=b, q2=q2: v.tensor_tensor(
                    xp_s[q2 % 3][:, :], xtok_s[q2 % 4][:, :], t2s_s[b][:, :],
                    ALU.add), deps=[E[("t2act", q2)],
                                    E.get(("outadd", q2 - 3))])
                E[("bn3", q2)] = A("V", lambda v, b=b, q2=q2: v.bn_stats(
                    bnst[2][b][:, 0:6], xp_s[q2 % 3][:, 0:512]),
                    deps=[E[("xp", q2)]])
                A("V", lambda v, b=b, q2=q2: v.bn_stats(
                    bnst[2][b][:, 6:12], xp_s[q2 % 3][:, 512:1024]),
                    deps=[E[("xp", q2)]])
                E[("mv3", q2)] = A("V", lambda v, b=b: v.bn_aggr(
                    mv[2][b][:, :], bnst[2][b][:, :]),
                    deps=[E.get(("vep3", q2 - 2))], tag="G")
                E[("vep3", q2)] = A("P", lambda g, b=b: g.tensor_scalar(
                    vep[2][b][:, :], mv[2][b][:, 1:2], 1.0, EPS, ALU.mult,
                    ALU.add), deps=[E[("mv3", q2)], E.get(("pow3", q2 - 2))])
                E[("pow3", q2)] = A("P", lambda g, b=b: g.tensor_tensor(
                    rcol[2][b][:, :], vep[2][b][:, :], mhalf[:, :], ALU.pow),
                    deps=[E[("vep3", q2)], E.get(("h2", q2 - 2))])
                E[("nmr3", q2)] = A("P", lambda g, b=b: g.tensor_scalar(
                    nmr[2][b][:, :], rcol[2][b][:, :], mv[2][b][:, 0:1], -1.0,
                    ALU.mult, ALU.mult),
                    deps=[E[("pow3", q2)], E.get(("h2", q2 - 2))])
                E[("h2", q2)] = A("S", lambda s, b=b, q2=q2: s.activation(
                    h2_s[b][:, :], xp_s[q2 % 3][:, :], AF.Identity,
                    scale=rcol[2][b][:, 0:1], bias=nmr[2][b][:, 0:1]),
                    deps=[E[("nmr3", q2)], E.get(("h2T", q2 - 2))])

            if q0 < nt:
                b = q0 % 2
                E[("t1", q0)] = A("V", lambda v, b=b: v.tensor_tensor(
                    ta_s[b][:, :], pCum[:, :], cosa[b][:, :], ALU.mult),
                    deps=[E[("cumC", q0)], E[("cveC", q0)], E[("cos", q0)],
                          E.get(("retadd", q0 - 2))])

            for qR in ([] if p == 0 else [q0 + 1]):
                if qR >= nt:
                    continue
                b = qR % 2
                E[("usr", qR)] = A("P", lambda g, b=b: g.tensor_scalar(
                    usr_s[b][:, :], usmat[:, :], rcol[0][b][:, 0:1], 1.0,
                    ALU.mult, ALU.mult),
                    deps=[E[("pow1", qR)], E.get(("cumS", qR - 2))])

            for qV in ([] if p == 0 else [q0 + 2]):
                if qV >= nt:
                    continue
                b = qV % 2
                E[("w", qV)] = A("V", lambda v, b=b: v.scalar_tensor_tensor(
                    w_s[b][:, :], k_s[:, :], -1.0, u_s[:, :], ALU.mult, ALU.add),
                    deps=[E[("k", qV)], E.get(("sin", qV - 2))])
                E[("wc", qV)] = A("V", lambda v, b=b: v.scalar_tensor_tensor(
                    wc_s[b][:, :], w_s[b][:, :], 0.25, w_s[b][:, :],
                    ALU.is_ge, ALU.subtract),
                    deps=[E.get(("cos", qV - 2))])

            for qS in ([] if p == 0 else [q0 + 1]):
                if qS >= nt:
                    continue
                b = qS % 2
                E[("sin", qS)] = A("S", lambda s, b=b: s.activation(
                    sina[b][:, :], w_s[b][:, :], AF.Sin, scale=TWO_PI),
                    deps=[E[("w", qS)], E.get(("t2", qS - 2)),
                          E.get(("vs", qS - 2))]
                    + ([e_ms[2]] if qS == 0 else []))
                E[("cos", qS)] = A("S", lambda s, b=b: s.activation(
                    cosa[b][:, :], wc_s[b][:, :], AF.Sin, scale=-TWO_PI,
                    bias=piby2[:, :]),
                    deps=[E[("wc", qS)], E.get(("t1", qS - 2)),
                          E.get(("vc", qS - 2))]
                    + ([e_ms[2]] if qS == 0 else []))

            if q0 < nt:
                b = q0 % 2

                def cums_mms(t, b=b, q0=q0):
                    r = None
                    for j in range(2):
                        r = t.matmul(pCum[:, j * 512:(j + 1) * 512],
                                     usr_s[b][:, :],
                                     vcvs[b][:, D + j * 512:D + (j + 1) * 512],
                                     start=True, stop=(q0 == 0))
                    if q0 > 0:
                        for j in range(2):
                            r = t.matmul(pCum[:, j * 512:(j + 1) * 512],
                                         selm[0:64, :],
                                         cums[(q0 - 1) % 2][:, D + j * 512:D + (j + 1) * 512],
                                         start=False, stop=True)
                    return r
                E[("cumS", q0)] = A("T", cums_mms,
                                    deps=[E[("vs", q0)], E[("t1", q0)],
                                          E[("cveC", q0)]])
                E[("t2", q0)] = A("V", lambda v, b=b: v.tensor_tensor(
                    tb_s[b][:, :], pCum[:, :], sina[b][:, :], ALU.mult),
                    deps=[E[("cumS", q0)], E[("sin", q0)],
                          E.get(("retadd", q0 - 2))])
                E[("retadd", q0)] = A("V", lambda v, b=b: v.tensor_tensor(
                    ret_s[b][:, :], ta_s[b][:, :], tb_s[b][:, :], ALU.add),
                    deps=[E[("t1", q0)], E[("t2", q0)],
                          E.get(("retT", q0 - 2))]
                    + ([e_wf_cast[6 + q0]] if q0 <= 1 else []))
                E[("cveS", q0)] = A("S", lambda s, b=b: s.copy(
                    cums[b][:, D:2 * D], pCum[64:128, :]),
                    deps=[E[("cumS", q0)], E[("t2", q0)],
                          E.get(("cumS", q0 - 1))])

            for qW in ([1] if p == 0 else [q0 + 1]):
                if qW >= nt:
                    continue

                def wv_mms(t, qW=qW):
                    r = None
                    for c in range(NCHUNK):
                        for j in range(2):
                            r = t.matmul(
                                pS[sv(qW)][:, j * 512:(j + 1) * 512],
                                xftb_s[qW % 2][:, c * TL:(c + 1) * TL],
                                wvt_s[:, c * D + j * 512:c * D + (j + 1) * 512],
                                start=(c == 0), stop=(c == NCHUNK - 1))
                    return r
                E[("wv", qW)] = A("T", wv_mms,
                                  deps=[E[("fcast", qW)], e_wvt,
                                        E.get(("deq", qW - 5)),
                                        E.get(("t2act", qW - 4)),
                                        E.get(("mv2", qW - 4))]
                                  + ([e_f2pi] if qW <= 1 else []))
            if 0 <= q4 < nt:
                b = q4 % 2
                E[("outadd", q4)] = A("P", lambda g, b=b, q4=q4: g.tensor_tensor(
                    out_s[b][:, :], xp_s[q4 % 3][:, :], ln4_s[0][:, :], ALU.add),
                    deps=[E[("ln4", q4)], E[("xp", q4)],
                          E.get(("ost", q4 - 2))]
                    + ([e_wo_cast[6 + q4]] if q4 <= 1 else []))
                E[("ost", q4)] = A("P", lambda g, b=b, q4=q4: g.dma_start(
                    out_d[q4 * TL:(q4 + 1) * TL, :], out_s[b][:, :]),
                    deps=[E[("outadd", q4)]], sem=f"DO{q4 % 2}", inc=16)

        with nc.Block() as block:

            @block.sync
            def _(e):
                sch.emit("D", e, sems, cut)
                for sname, cval in sorted(sch.dma_counts_upto(cut).items()):
                    e.wait_ge(sems[sname], cval)

            @block.vector
            def _(v):
                sch.emit("V", v, sems, cut, selfsync=("VSP" if racecheck else selfsync))

            @block.scalar
            def _(s):
                sch.emit("S", s, sems, cut, selfsync=("VSP" if racecheck else selfsync))

            @block.gpsimd
            def _(g):
                sch.emit("P", g, sems, cut, selfsync=("VSP" if racecheck else selfsync))

            @block.tensor
            def _(t):
                sch.emit("T", t, sems, cut, selfsync=("VSP" if racecheck else selfsync))

    return nc


def _host_layout(x_b, Wv, Wo, Wff, freqs, s_in, s_k, bias, nt):
    T = nt * TL
    xtok = np.ascontiguousarray(x_b)
    xft = np.ascontiguousarray(
        x_b.T.reshape(NCHUNK, TL, nt, TL).transpose(2, 1, 0, 3)
        .reshape(nt * TL, D))
    wvt = np.ascontiguousarray(Wv.T)
    wot = np.ascontiguousarray(Wo.T)
    wff = np.ascontiguousarray(Wff)
    tcols = (np.arange(TL, dtype=np.float32)[:, None]
             + TL * np.arange(nt, dtype=np.float32)[None, :])
    us = np.triu(np.ones((TL, TL), dtype=np.float32)).astype(ml_dtypes.bfloat16)
    idm = np.eye(TL, dtype=np.float32).astype(ml_dtypes.bfloat16)
    sel = np.zeros((TL, TL), dtype=np.float32)
    sel[63, :] = 1.0
    sel = sel.astype(ml_dtypes.bfloat16)
    return {
        "xtok": xtok, "xft": xft, "wvt": wvt, "wot": wot, "wff": wff,
        "freqs": np.ascontiguousarray(freqs.reshape(1, D)),
        "tcols": np.ascontiguousarray(tcols),
        "usmat": us, "idmat": idm, "selmat": sel,
        "s_in": s_in.reshape(1, 1), "s_k": s_k.reshape(1, 1),
        "biasv": np.ascontiguousarray(bias.reshape(1, D)),
    }


def kernel(L0_attn_x, L0_attn_Wk, L0_attn_Wv, L0_attn_Wo, L0_attn_freqs,
           L0_ffn_kernel, L0_ffn_input_scale, L0_ffn_kernel_scale, L0_ffn_bias):
    x = np.asarray(L0_attn_x)
    assert np.asarray(L0_attn_Wk).shape == np.asarray(L0_attn_Wv).shape
    B, T, Dd = x.shape
    nt = T // TL
    nc = build(nt)
    in_maps = []
    for b in range(B):
        in_maps.append(_host_layout(
            x[b], np.asarray(L0_attn_Wv), np.asarray(L0_attn_Wo),
            np.asarray(L0_ffn_kernel), np.asarray(L0_attn_freqs),
            np.asarray(L0_ffn_input_scale), np.asarray(L0_ffn_kernel_scale),
            np.asarray(L0_ffn_bias), nt))
    res = run_bass_kernel_spmd(nc, in_maps, core_ids=list(range(B)))
    out = np.stack([np.asarray(r["out"]) for r in res.results], axis=0)
    return out.astype(np.float32)
